# version 17
# baseline (speedup 1.0000x reference)
import contextlib
import numpy as np
import concourse.bass as bass
import concourse.mybir as mybir
from concourse.bass_utils import run_bass_kernel_spmd

F32 = mybir.dt.float32
BF16 = mybir.dt.bfloat16
AF = mybir.ActivationFunctionType
ALU = mybir.AluOpType
AX = mybir.AxisListType

D = 4096
DC = 32
T = 1024
S = 2048
DH = 11008
HCN = 86
HH = 43
EPS = 1e-5
SM_SCALE = 0.03125
INV_D = 1.0 / 4096.0

NT = (slice(0, 512), slice(512, 1024))
ENGINES = ("tensor", "vector", "scalar", "gpsimd", "sync")

DEBUG_TAPS = False
TAP_KEYS = None
TRUNC = "full"
CSKIP = ()


class _Stop(Exception):
    pass

PERS = 17024
BG = PERS + 1280
R56 = BG + 147456
R7 = R56 + 32768
ARENA_END = R7 + 30464
assert ARENA_END <= 229344 - 128, ARENA_END


class Sched:
    def __init__(self):
        self.ops = {e: [] for e in ENGINES}
        self.cnt = {}
        self.last_wait = {e: {} for e in ENGINES}

    def emit(self, eng, fn, waits=(), inc=None, n=1):
        w = {}
        def add(ev):
            if ev is None:
                return
            if isinstance(ev, (list, tuple)) and len(ev) == 2 \
                    and isinstance(ev[0], str) and isinstance(ev[1], int):
                s, v = ev
                if v > self.last_wait[eng].get(s, 0):
                    w[s] = max(w.get(s, 0), v)
                return
            for sub in ev:
                add(sub)
        add(list(waits))
        for s, v in w.items():
            self.last_wait[eng][s] = v
        ev_out = None
        if inc is not None:
            self.cnt[inc] = self.cnt.get(inc, 0) + n
            ev_out = (inc, self.cnt[inc])
        self.ops[eng].append((tuple(w.items()), fn, inc, n))
        return ev_out

    def at(self, sem):
        return (sem, self.cnt.get(sem, 0))


def build_program():
    nc = bass.Bass("TRN2", target_bir_lowering=False)

    def inp(name, shape):
        return nc.dram_tensor(name, list(shape), F32, kind="ExternalInput").ap()

    xo_t = inp("xo_t", (4, 128, 8192))
    xp_t = inp("xp_t", (4, 128, 8192))
    wqkv_t = inp("wqkv_t", (24, 128, 4096))
    wo_t = inp("wo_t", (32, 128, 1024))
    gup_g = inp("gup_g", (HCN, 128, 4224))
    gup_u = inp("gup_u", (HCN, 128, 4224))
    dwn_t = inp("dwn_t", (32, 128, 11136))
    gaup_h = inp("gaup_h", (128, 1536))
    da_h = inp("da_h", (128, 1376))
    bqkv_h = inp("bqkv_h", (128, 24))
    bo_h = inp("bo_h", (128, 32))
    ident_h = inp("ident_h", (128, 128))

    o = nc.dram_tensor("o", [D, T], F32, kind="ExternalOutput").ap()
    ypark = nc.dram_tensor("ypark", [4, 128, 8192], BF16, kind="Internal").ap()
    xbf = nc.dram_tensor("xbf", [4, 128, 8192], BF16, kind="Internal").ap()
    accd = nc.dram_tensor("accd", [32, 128, 1024], F32, kind="Internal").ap()
    doutd = nc.dram_tensor("doutd", [32, 128, 1024], F32, kind="Internal").ap()

    taps = {}
    def tapon(nm):
        return DEBUG_TAPS and (TAP_KEYS is None or nm in TAP_KEYS)
    if DEBUG_TAPS:
        for nm, shape in (
            ("t_ln1", (128, 32768)), ("t_q", (128, 8192)), ("t_k", (128, 16384)),
            ("t_vtm", (128, 16384)), ("t_probsT", (128, 16384)),
            ("t_negmaxT", (1, 1024)), ("t_ao", (128, 8192)),
            ("t_y", (128, 32768)), ("t_ln3", (128, 32768)), ("t_hid", (128, 44032)),
        ):
            taps[nm] = nc.dram_tensor(nm, list(shape), F32, kind="ExternalOutput").ap()

    def sb(name, off, shape, dtype):
        return nc.alloc_sbuf_tensor_at(name, list(shape), dtype, offset=off).ap()

    ident_bf = sb("ident_bf", PERS + 0, (128, 128), BF16)
    ones_row = sb("ones_row", PERS + 256, (1, 128), BF16)
    ones_col = sb("ones_col", PERS + 512, (128, 1), BF16)
    eps_sb = sb("eps_sb", PERS + 576, (1, 1), F32)
    bqkv_sb = sb("bqkv_sb", PERS + 640, (128, 24), F32)
    bo_sb = sb("bo_sb", PERS + 768, (128, 32), F32)
    nm1 = [sb("nm1a", PERS + 896, (128, 1), F32),
           sb("nm1b", PERS + 928, (128, 1), F32)]
    nm2 = [sb("nm2a", PERS + 960, (128, 1), F32),
           sb("nm2b", PERS + 992, (128, 1), F32)]
    nmbf = [sb("nmbfa", PERS + 1024, (128, 1), BF16),
            sb("nmbfb", PERS + 1056, (128, 1), BF16)]

    q_fm = sb("q_fm", BG + 0, (128, 8192), BF16)
    k_fm = sb("k_fm", BG + 16384, (128, 16384), BF16)
    v_fm = sb("v_fm", BG + 49152, (128, 16384), BF16)
    xh = sb("xh", BG + 81920, (128, 32768), BF16)
    v_tm = sb("v_tm", BG + 81920, (128, 16384), BF16)
    probsT = sb("probsT", BG + 114688, (128, 16384), BF16)
    z = sb("z", BG + 0, (128, 32768), BF16)
    ln3 = sb("ln3", BG + 65536, (128, 32768), BF16)
    hidden = sb("hidden", BG + 0, (128, 29696), BF16)
    uwb = [sb("uwb0", R56 + 16896, (128, 4224), BF16),
           sb("uwb1", BG + 131072, (128, 4224), BF16)]
    dwb = [sb("dwb0", R56 + 16896, (128, 3712), BF16),
           sb("dwb1", BG + 131072, (128, 3712), BF16)]
    ytile = [sb("ytile%d" % i, BG + 2048 * i, (128, 1024), BF16)
             for i in range(4)]
    ypre = [sb("ypre%d" % i, R56 + 256 + 2048 * i, (128, 1024), BF16)
            for i in range(8)] + \
           [sb("ypre%d" % (8 + i), BG + 59392 + 2048 * i, (128, 1024), BF16)
            for i in range(3)]
    otile = [sb("otile%d" % i, BG + 131072 + 4096 * i, (128, 1024), F32)
             for i in range(4)]

    wqb = [sb("wqb0", R56 + 0, (128, 4096), BF16),
           sb("wqb1", R56 + 8192, (128, 4096), BF16)]
    attnout = sb("attnout", R56 + 0, (128, 8192), BF16)
    wob = [sb("wob0", R56 + 16384, (128, 1024), BF16),
           sb("wob1", R56 + 18432, (128, 1024), BF16)]
    xtile = [sb("xtile%d" % i, R7 + 4096 + 2048 * i, (128, 1024), BF16)
             for i in range(4)]
    zbf_buf = sb("zbf_buf", R56 + 25344, (128, 1024), BF16)
    sqc_buf = sb("sqc_buf", R56 + 27392, (128, 1024), BF16)
    lo1_bf = sb("lo1_bf", R56 + 25344, (1, 1024), BF16)
    lo2_bf = sb("lo2_bf", R56 + 27392, (1, 1024), BF16)
    gwb = [sb("gwb0", R56 + 0, (128, 4224), BF16),
           sb("gwb1", R56 + 8448, (128, 4224), BF16)]
    dout_sb = sb("dout_sb", BG + 65536, (128, 32768), BF16)

    sqb = [sb("sqb0", R7 + 0, (128, 1024), BF16),
           sb("sqb1", R7 + 2048, (128, 1024), BF16)]
    st_a = sb("st_a", R7 + 4096, (1, 1024), F32)
    st_b = sb("st_b", R7 + 8192, (1, 1024), F32)
    bf_a = sb("bf_a", R7 + 12288, (1, 1024), BF16)
    bf_b = sb("bf_b", R7 + 14336, (1, 1024), BF16)
    rb_bf = sb("rb_bf", R7 + 16384, (128, 1024), BF16)
    nb_bf = sb("nb_bf", R7 + 18432, (128, 1024), BF16)
    negmaxT = sb("negmaxT", R7 + 0, (1, 1024), BF16)
    zbf2 = sb("zbf2", R7 + 0, (128, 1024), BF16)
    sqc2 = sb("sqc2", R7 + 2048, (128, 1024), BF16)
    st_c = sb("st_c", R7 + 0, (1, 1024), F32)
    rb_f32 = sb("rb_f32", R7 + 16384, (128, 1024), F32)
    nb_f32 = sb("nb_f32", R7 + 20480, (128, 1024), F32)
    tmpg_pad = sb("tmpg_pad", R7 + 0, (128, 1024), BF16)
    tmpu_pad = sb("tmpu_pad", R7 + 2048, (128, 1024), BF16)
    tmpd_pad = sb("tmpd_pad", R7 + 4096, (128, 1024), BF16)
    silu_buf = sb("silu_buf", R7 + 6144, (128, 1024), BF16)
    acct = [sb("acct0", R7 + 8192, (128, 1024), F32),
            sb("acct1", R7 + 12288, (128, 1024), F32)]
    tmpd_h1 = sb("tmpd_h1", R7 + 16384, (16, 1024), F32)
    est_a = sb("est_a", R7 + 0, (1, 1024), F32)
    est_b = sb("est_b", R7 + 4096, (1, 1024), F32)
    est_c = sb("est_c", R7 + 8192, (1, 1024), F32)
    erb_f32 = sb("erb_f32", R7 + 16384, (128, 1024), F32)
    enb_f32 = sb("enb_f32", R7 + 20480, (128, 1024), F32)
    gaup_sb = sb("gaup_sb", R7 + 24576, (128, 1536), BF16)
    da_sb = sb("da_sb", R7 + 27648, (128, 1376), BF16)

    psA = nc.alloc_psum_tensor("psA", [128, 1024], F32).ap()
    psB = nc.alloc_psum_tensor("psB", [128, 1024], F32).ap()
    psStat = nc.alloc_psum_tensor("psStat", [33, 1024], F32).ap()
    psT = [nc.alloc_psum_tensor("psT1", [128, 1024], BF16).ap(),
           nc.alloc_psum_tensor("psT2", [128, 1024], BF16).ap()]
    psAB = (psA, psB)

    sch = Sched()

    def E(eng, fn, waits=(), inc=None, n=1):
        if stopped[0]:
            return None
        return sch.emit(eng, fn, waits, inc, n)

    stopped = [False]

    def ckpt(name):
        if TRUNC == name:
            stopped[0] = True

    GD_SEMS = ("su", "su2", "xa0", "xa1", "xa2", "xa3", "xa4", "xa5", "xa6", "xa7",
               "wq0", "wq1", "wo0", "wo1",
               "gw0", "gw1", "uw0", "uw1", "dw0", "dw1", "og0", "og1", "tap")
    SC_SEMS = ("yt0", "yt1", "yt2", "yt3", "xk0", "xk1", "xk2", "xk3",
               "pp")
    SD_SEMS = ("sb", "xt0", "xt1", "xt2", "xt3", "yp0", "yp1", "yp2", "yp3", "ap",
               "ac0", "ac1", "ow0", "ow1")

    def gdma(dst, src, sem, waits=()):
        assert sem in GD_SEMS, sem
        return E("gpsimd", lambda e, d=dst, s=src: e.dma_start(d, s), waits, sem, 16)

    def sdma(dst, src, sem, waits=()):
        assert sem in SD_SEMS, sem
        return E("sync", lambda e, d=dst, s=src: e.dma_start(d, s), waits, sem, 16)

    def cdma(dst, src, sem, waits=()):
        assert sem in SC_SEMS, sem
        return E("scalar", lambda e, d=dst, s=src: e.dma_start(d, s), waits, sem, 16)

    def mm(out_ap, lhs_ap, rhs_ap, start, stop, waits=()):
        return E("tensor",
                 lambda e, o=out_ap, l=lhs_ap, r=rhs_ap, st=start, sp=stop:
                 e.matmul(o, l, r, start=st, stop=sp), waits, "ts")

    def vop(fn, waits=()):
        return E("vector", fn, waits, "vs")

    def sop(fn, waits=()):
        return E("scalar", fn, waits, "ss")

    E("gpsimd", lambda e: e.memset(ones_row[:], 1.0), inc="gs")
    E("gpsimd", lambda e: e.memset(ones_col[:], 1.0), inc="gs")
    ev_memset = E("gpsimd", lambda e: e.memset(eps_sb[:], EPS), inc="gs")
    ev_ident = gdma(ident_bf[:], ident_h[:], "su")
    sdma(bqkv_sb[:], bqkv_h[:], "sb")
    sdma(bo_sb[:], bo_h[:], "sb")
    ev_bq = ev_bo = sch.at("sb")

    psAB_free = [None, None]
    psStat_free = None

    def ln_finalize(wait_ev, a_st, b_st, a_bf, b_bf, scr, extra=()):
        E("vector", lambda e: e.tensor_scalar(a_st[:], psStat[0:1, :], INV_D, None, ALU.mult),
          (wait_ev, ev_memset) + tuple(extra), inc="vs")
        ev_sr = E("vector", lambda e: e.tensor_scalar(b_st[:], psStat[32:33, :], INV_D, None, ALU.mult),
                  inc="vs")
        E("vector", lambda e: e.scalar_tensor_tensor(scr[:], a_st[:], -1.0, a_st[:],
                                                     ALU.mult, ALU.mult), inc="vs")
        ev_var = E("vector", lambda e: e.tensor_add(b_st[:], b_st[:], scr[:]), inc="vs")
        E("scalar", lambda e: e.activation(b_st[:], b_st[:], AF.Ln, bias=eps_sb[0:1, 0:1]),
          (ev_var,), inc="ss")
        ev_rstd = E("scalar", lambda e: e.activation(b_st[:], b_st[:], AF.Exp, scale=-0.5),
                    inc="ss")
        E("vector", lambda e: e.scalar_tensor_tensor(a_st[:], a_st[:], -1.0, b_st[:],
                                                     ALU.mult, ALU.mult), (ev_rstd,), inc="vs")
        E("vector", lambda e: e.tensor_copy(a_bf[:], a_st[:]), inc="vs")
        ev_done = E("vector", lambda e: e.tensor_copy(b_bf[:], b_st[:]), inc="vs")
        return ev_sr, ev_done

    def bcast_single(bf_ap, ps, waits):
        mm(ps[:, 0:512], ones_row[0:1, :], bf_ap[0:1, 0:512], True, True, waits)
        return mm(ps[:, 512:1024], ones_row[0:1, :], bf_ap[0:1, 512:1024], True, True)

    def bcast_double(hi_ap, lo_ap, ps, waits):
        mm(ps[:, 0:512], ones_row[0:1, :], hi_ap[0:1, 0:512], True, False, waits)
        mm(ps[:, 0:512], ones_row[0:1, :], lo_ap[0:1, 0:512], False, True)
        mm(ps[:, 512:1024], ones_row[0:1, :], hi_ap[0:1, 512:1024], True, False)
        return mm(ps[:, 512:1024], ones_row[0:1, :], lo_ap[0:1, 512:1024], False, True)

    def hi_lo(src_f32, hi_bf, lo_f32_scr, lo_bf, waits):
        ev1 = sop(lambda e: e.activation(lo_f32_scr[:], hi_bf[:], AF.Identity), waits)
        ev2 = vop(lambda e: e.tensor_sub(lo_f32_scr[:], src_f32[:], lo_f32_scr[:]), (ev1,))
        return vop(lambda e: e.tensor_copy(lo_bf[:], lo_f32_scr[:]), (ev2,))

    wq_free = [None, None]
    ev_qkv_evac = {}
    last_qkv_mm = None

    for hi_half, (xsrc, fcs) in enumerate(((xo_t, list(range(24))),
                                           (xp_t, list(range(8, 24))))):
        ev_wq_pre = {}
        ev_wq_pre[0] = gdma(wqb[0][:], wqkv_t[fcs[0]], "wq0", (wq_free[0],))
        ld_s = []
        for sg in range(8):
            g, hlf = sg // 2, sg % 2
            waits = [last_qkv_mm, sch.at("xk%d" % g)] if hi_half == 1 else []
            ld_s.append(gdma(xh[:, sg * 4096:(sg + 1) * 4096],
                             xsrc[g][:, hlf * 4096:(hlf + 1) * 4096],
                             "xa%d" % sg, waits))
        ev_wq_pre[1] = gdma(wqb[1][:], wqkv_t[fcs[1]], "wq1", (wq_free[1],))
        sq_done = {}
        stat_mm = {}
        park_ev = {}
        for c in range(DC):
            if hi_half == 0 and c % 8 == 4:
                g = c // 8
                park_ev[g] = cdma(xbf[g], xh[:, g * 8192:(g + 1) * 8192], "xk%d" % g,
                                  (ld_s[2 * g], ld_s[2 * g + 1]))
            w = [ld_s[c // 4]]
            if c >= 2:
                w.append(stat_mm[c - 2])
            sq_done[c] = sop(lambda e, c=c: e.activation(
                sqb[c % 2][:], xh[:, c * 1024:(c + 1) * 1024], AF.Square), w)
            st, sp = (c == 0), (c == DC - 1)
            w2 = [sq_done[c]]
            if c == 0:
                w2.append(psStat_free)
            first = True
            for nts in NT:
                mm(psStat[0:1, nts], ones_col[:, 0:1],
                   xh[:, c * 1024 + nts.start:c * 1024 + nts.stop], st, sp,
                   w2 if first else ())
                first = False
                stat_mm[c] = mm(psStat[32:33, nts], ones_col[:, 0:1],
                                sqb[c % 2][:, nts], st, sp)
        ev_sr, ev_fin = ln_finalize(stat_mm[DC - 1], st_a, st_b, bf_a, bf_b, st_c)
        psStat_free = ev_sr
        ev_b1 = bcast_single(bf_b, psA, (ev_fin, psAB_free[0]))
        ev_b2 = bcast_single(bf_a, psB, (psAB_free[1],))
        ev_rbc = vop(lambda e: e.tensor_copy(rb_bf[:], psA[:]), (ev_b1,))
        ev_nbc = vop(lambda e: e.tensor_copy(nb_bf[:], psB[:]), (ev_b2,))
        psAB_free = [ev_rbc, ev_nbc]
        ev_norm_v = ev_norm_g = None
        norm_ev = {}
        for c in range(DC):
            cs = slice(c * 1024, (c + 1) * 1024)
            w1 = [ev_rbc, ev_nbc]
            if hi_half == 0:
                w1.append(park_ev[c // 8])
            if c % 5 == 4:
                E("gpsimd", lambda e, cs=cs: e.tensor_tensor(
                    xh[:, cs], xh[:, cs], rb_bf[:], ALU.mult), w1, inc="gs")
                ev_norm_g = E("gpsimd", lambda e, cs=cs: e.tensor_tensor(
                    xh[:, cs], xh[:, cs], nb_bf[:], ALU.add), inc="gs")
                norm_ev[c] = ev_norm_g
            else:
                vop(lambda e, cs=cs: e.tensor_tensor(xh[:, cs], xh[:, cs], rb_bf[:], ALU.mult),
                    w1)
                ev_norm_v = vop(lambda e, cs=cs: e.tensor_tensor(
                    xh[:, cs], xh[:, cs], nb_bf[:], ALU.add))
                norm_ev[c] = ev_norm_v
        ev_norm = [ev_norm_v, ev_norm_g]
        if tapon("t_ln1") and hi_half == 0:
            gdma(taps["t_ln1"][:], xh[:], "tap", (ev_norm,))
        for i, fc in enumerate(fcs):
            par = i % 2
            if i in ev_wq_pre:
                ev_w = ev_wq_pre[i]
            else:
                ev_w = gdma(wqb[par][:], wqkv_t[fc], "wq%d" % par, (wq_free[par],))
            w0 = [ev_w, psAB_free[par]]
            if i >= 2:
                w0.append(ev_norm)
            last = None
            for c in range(DC):
                w_c = list(w0) if last is None else []
                if i < 2:
                    w_c.append(norm_ev[c])
                for nts in NT:
                    last = mm(psAB[par][:, nts], wqb[par][:, c * 128:(c + 1) * 128],
                              xh[:, c * 1024 + nts.start:c * 1024 + nts.stop],
                              c == 0, c == DC - 1, w_c if w_c else ())
                    w_c = []
            wq_free[par] = last
            last_qkv_mm = last
            if fc < 8:
                dst = q_fm[:, fc * 1024:(fc + 1) * 1024]
            elif fc < 16:
                base = (fc - 8) * 2048 + hi_half * 1024
                dst = k_fm[:, base:base + 1024]
            else:
                base = (fc - 16) * 2048 + hi_half * 1024
                dst = v_fm[:, base:base + 1024]
            ev_ev = vop(lambda e, par=par, dst=dst, fc=fc: e.tensor_scalar(
                dst, psAB[par][:], bqkv_sb[:, fc:fc + 1], None, ALU.add), (last, ev_bq))
            psAB_free[par] = ev_ev
            ev_qkv_evac[(hi_half, fc)] = ev_ev

    ev_q_done = ev_qkv_evac[(0, 7)]
    ev_k_done = ev_qkv_evac[(1, 15)]
    ev_v_done = ev_qkv_evac[(1, 23)]
    if tapon("t_q"):
        gdma(taps["t_q"][:], q_fm[:], "tap", (ev_q_done,))
    if tapon("t_k"):
        gdma(taps["t_k"][:], k_fm[:], "tap", (ev_k_done,))

    ckpt("A")
    gdma(gaup_sb[:], gaup_h[:], "su2")
    gdma(da_sb[:], da_h[:], "su2")
    ev_gaup = ev_da = sch.at("su2")
    psT_free = [None, None]
    ev_vtm = None
    for kc in range(16):
        par = kc % 2
        tr = None
        for dc in range(8):
            w = (ev_v_done, psT_free[par], ev_ident) if dc == 0 else ()
            src = v_fm[:, dc * 2048 + kc * 128:dc * 2048 + kc * 128 + 128]
            tr = E("tensor", lambda e, par=par, dc=dc, src=src: e.transpose(
                psT[par][:, dc * 128:(dc + 1) * 128], src, ident_bf[:]), w, "ts")
        ev_vtm = vop(lambda e, par=par, kc=kc: e.tensor_copy(
            v_tm[:, kc * 1024:(kc + 1) * 1024], psT[par][:]), (tr,))
        psT_free[par] = ev_vtm
    if tapon("t_vtm"):
        gdma(taps["t_vtm"][:], v_tm[:], "tap", (ev_vtm,))

    last_q_mm = None
    for c in range(8):
        w = [ev_q_done] if c == 0 else [last_q_mm]
        ev_s = sop(lambda e, c=c: e.activation(
            sqb[1][:], q_fm[:, c * 1024:(c + 1) * 1024], AF.Square), w)
        first = True
        for nts in NT:
            last_q_mm = mm(psStat[0:1, nts], ones_col[:, 0:1], sqb[1][:, nts],
                           c == 0, c == 7,
                           (ev_s, psStat_free) if first else (ev_s,))
            first = False
    ev_mk = []
    last_k_mm = None
    for half in range(2):
        for c in range(8):
            w = [ev_k_done] if (half == 0 and c == 0) else [last_k_mm]
            if half == 1 and c == 0:
                w.append(ev_mk[0])
            ev_s = sop(lambda e, c=c, half=half: e.activation(
                sqb[0][:], k_fm[:, c * 2048 + half * 1024:c * 2048 + half * 1024 + 1024],
                AF.Square), w)
            first = True
            for nts in NT:
                last_k_mm = mm(psStat[32:33, nts], ones_col[:, 0:1], sqb[0][:, nts],
                               c == 0, c == 7, (ev_s,) if first else ())
                first = False
        ev_mk.append(vop(lambda e, half=half: e.reduce_max(
            nm1[half][0:1, 0:1], psStat[32:33, :], axis=AX.X), (last_k_mm,)))
    ev_mkx = vop(lambda e: e.tensor_tensor(nm1[0][0:1, 0:1], nm1[0][0:1, 0:1],
                                           nm1[1][0:1, 0:1], ALU.max), (ev_mk[0], ev_mk[1]))
    ev_p = vop(lambda e: e.tensor_scalar(st_a[:], psStat[0:1, :], nm1[0][0:1, 0:1],
                                         None, ALU.mult), (ev_mkx, ev_mkx))
    ev_l = sop(lambda e: e.activation(st_a[:], st_a[:], AF.Ln), (ev_p,))
    ev_e = sop(lambda e: e.activation(st_a[:], st_a[:], AF.Exp, scale=0.5))
    ev_nmT_evac = vop(lambda e: e.tensor_scalar(negmaxT[:], st_a[:], -1.0, None,
                                                ALU.mult), (ev_e,))
    psStat_free = ev_nmT_evac
    psAB_free = [psAB_free[0], psAB_free[1]]
    if tapon("t_negmaxT"):
        gdma(taps["t_negmaxT"][:], negmaxT[:], "tap", (ev_nmT_evac,))

    pend_b = []
    ev_exp = None
    last_rs = None
    for kc in range(16):
        par = kc % 2
        ps = psAB[par]
        w0 = [ev_nmT_evac, psAB_free[par]]
        last = None
        for dc in range(8):
            lhs = k_fm[:, dc * 2048 + kc * 128:dc * 2048 + kc * 128 + 128]
            for nts in NT:
                last = mm(ps[:, nts], lhs, q_fm[:, dc * 1024 + nts.start:dc * 1024 + nts.stop],
                          dc == 0, False, w0 if last is None else ())
        for nts in NT:
            last = mm(ps[:, nts], ones_row[0:1, :], negmaxT[0:1, nts], False, True)
        ev_exp = sop(lambda e, par=par, kc=kc: e.activation(
            probsT[:, kc * 1024:(kc + 1) * 1024], psAB[par][:], AF.Exp, scale=SM_SCALE),
            (last,))
        psAB_free[par] = ev_exp
        def emit_rs(kc=kc, ev_exp=ev_exp):
            nonlocal last_rs
            w_rs = [ev_exp]
            if kc == 0:
                w_rs.append(psStat_free)
            first = True
            for nts in NT:
                last_rs = mm(psStat[0:1, nts], ones_col[:, 0:1],
                             probsT[:, kc * 1024 + nts.start:kc * 1024 + nts.stop],
                             kc == 0, kc == 15, w_rs if first else ())
                first = False
        pend_b.append(emit_rs)
        if len(pend_b) > 1:
            pend_b.pop(0)()
    if tapon("t_probsT"):
        gdma(taps["t_probsT"][:], probsT[:], "tap", (ev_exp,))

    while pend_b:
        pend_b.pop(0)()
    ev_rs = vop(lambda e: e.reciprocal(st_a[:], psStat[0:1, :]), (last_rs,))
    psStat_free = ev_rs
    ev_rs_bf = vop(lambda e: e.tensor_copy(bf_a[:], st_a[:]), (ev_rs, ev_rs))
    ev_bc = bcast_single(bf_a, psA, (ev_rs_bf, psAB_free[0]))
    ev_rsbc = vop(lambda e: e.tensor_copy(rb_bf[:], psA[:]), (ev_bc,))
    psAB_free[0] = ev_rsbc

    wob_free = [None, None]
    ev_wo_pre = {}
    for i in (0, 1):
        ev_wo_pre[i] = gdma(wob[i][:], wo_t[i], "wo%d" % i, ())

    ev_ao = None
    for dc in range(8):
        par = dc % 2
        ps = psAB[par]
        w0 = [ev_vtm, ev_exp, psAB_free[par]]
        last = None
        for kc in range(16):
            lhs = v_tm[:, kc * 1024 + dc * 128:kc * 1024 + dc * 128 + 128]
            for nts in NT:
                last = mm(ps[:, nts], lhs, probsT[:, kc * 1024 + nts.start:kc * 1024 + nts.stop],
                          kc == 0, kc == 15, w0 if last is None else ())
        ev_ao = vop(lambda e, par=par, dc=dc: e.tensor_tensor(
            attnout[:, dc * 1024:(dc + 1) * 1024], psAB[par][:], rb_bf[:], ALU.mult),
            (last, ev_rsbc))
        psAB_free[par] = ev_ao
    if tapon("t_ao"):
        gdma(taps["t_ao"][:], attnout[:], "tap", (ev_ao,))

    ckpt("B")
    ev_zev = {}
    stat_mm_c = {}
    ev_sbf = {}
    pend_c = []
    for fc in range(32):
        par = fc % 2
        if fc < 2:
            ev_w = ev_wo_pre[fc]
        else:
            ev_w = gdma(wob[par][:], wo_t[fc], "wo%d" % par, (wob_free[par],))
        ps = psAB[par]
        w0 = [ev_w, ev_ao, psAB_free[par]]
        last = None
        for dc in range(8):
            lhs = wob[par][:, dc * 128:(dc + 1) * 128]
            for nts in NT:
                last = mm(ps[:, nts], lhs, attnout[:, dc * 1024 + nts.start:dc * 1024 + nts.stop],
                          dc == 0, dc == 7, w0 if last is None else ())
        wob_free[par] = last
        while len(pend_c) > 1:
            pend_c.pop(0)()
        if "evac" not in CSKIP:
            w_z = [last, ev_bo]
            if DEBUG_TAPS and fc == 0:
                w_z.append(sch.at("tap"))
            ev_zev[fc] = vop(lambda e, par=par, fc=fc: e.tensor_scalar(
                z[:, fc * 1024:(fc + 1) * 1024], psAB[par][:], bo_sb[:, fc:fc + 1], None, ALU.add),
                w_z)
        else:
            ev_zev[fc] = vop(lambda e, par=par, fc=fc: e.tensor_copy(
                z[:, fc * 1024:(fc + 1) * 1024], psAB[par][:]), (last,))
        sq_ = sqc_buf if par == 0 else sqc2
        zs_c = z[:, fc * 1024:(fc + 1) * 1024]
        if "act" not in CSKIP:
            w_s = [ev_zev[fc]]
            if fc >= 2:
                w_s.append(stat_mm_c[fc - 2])
            ev_sbf[fc] = sop(lambda e, sq_=sq_, zs_c=zs_c: e.activation(
                sq_[:], zs_c, AF.Square), w_s)
        else:
            ev_sbf[fc] = ev_zev[fc]
        psAB_free[par] = ev_zev[fc]
        if "stats" not in CSKIP and "act" not in CSKIP:
            def emit_stats_c(fc=fc, sq_=sq_, zs_c=zs_c):
                w_st = [ev_sbf[fc]]
                if fc == 0:
                    w_st.append(psStat_free)
                first = True
                for nts in NT:
                    mm(psStat[0:1, nts], ones_col[:, 0:1], zs_c[:, nts], fc == 0, fc == 31,
                       w_st if first else ())
                    first = False
                    stat_mm_c[fc] = mm(psStat[32:33, nts], ones_col[:, 0:1], sq_[:, nts],
                                       fc == 0, fc == 31)
            pend_c.append(emit_stats_c)
        else:
            stat_mm_c[fc] = ev_zev[fc]
    if DEBUG_TAPS and TRUNC == "C1":
        gdma(taps["t_y"][:], z[:], "tap", (ev_zev[31],))
    ckpt("C1")
    for p_ in pend_c:
        p_()
    pend_c = []
    ev_sr, ev_fin = ln_finalize(stat_mm_c[31], st_a, st_b, bf_a, bf_b, st_c,
                                extra=(sch.at("tap"),) if DEBUG_TAPS else ())
    psStat_free = ev_sr
    ev_b1 = bcast_single(bf_b, psA, (ev_fin, psAB_free[0]))
    ev_b2 = bcast_single(bf_a, psB, (psAB_free[1],))
    ev_rbf = vop(lambda e: e.tensor_copy(rb_bf[:], psA[:]), (ev_b1,))
    ev_nbf = vop(lambda e: e.tensor_copy(nb_bf[:], psB[:]), (ev_b2,))
    psAB_free = [ev_rbf, ev_nbf]
    ckpt("C2")
    def eop(fc3, fn, waits=(), mod=3):
        if fc3 % mod == mod - 1:
            return E("gpsimd", fn, waits, inc="gs")
        return E("vector", fn, waits, inc="vs")

    xt_free = [None, None, None, None]
    ev_y = {}
    ev_dcast = {}
    stat_mm_d = {}
    for fc in range(32):
        xb = fc % 4
        w_x = [xt_free[xb]] if fc >= 4 else [ev_fin, ev_rbf, ev_nbf]
        ev_x = sdma(xtile[xb][:], xbf[fc // 8][:, (fc % 8) * 1024:(fc % 8) * 1024 + 1024],
                    "xt%d" % xb, w_x)
        zs = z[:, fc * 1024:(fc + 1) * 1024]
        w1 = [ev_rbf, ev_nbf, ev_zev[fc], stat_mm_c[31]]
        if DEBUG_TAPS and fc == 0:
            w1.append(sch.at("tap"))
        eop(fc, lambda e, zs=zs: e.tensor_tensor(zs, zs, rb_bf[:], ALU.mult), w1)
        eop(fc, lambda e, zs=zs: e.tensor_tensor(zs, zs, nb_bf[:], ALU.add))
        ev_y[fc] = eop(fc, lambda e, zs=zs, xb=xb: e.tensor_tensor(
            zs, zs, xtile[xb][:], ALU.add), (ev_x,))
        xt_free[xb] = ev_y[fc]
        w_c = [ev_y[fc]]
        if fc >= 1:
            w_c.append(stat_mm_d[fc - 1])
        ev_dcast[fc] = sop(lambda e, zs=zs: e.activation(sqc_buf[:], zs, AF.Square), w_c)
        w_st = [ev_dcast[fc]]
        if fc == 0:
            w_st.append(psStat_free)
        first = True
        for nts in NT:
            mm(psStat[0:1, nts], ones_col[:, 0:1], zs[:, nts], fc == 0, fc == 31,
               w_st if first else ())
            first = False
            stat_mm_d[fc] = mm(psStat[32:33, nts], ones_col[:, 0:1], sqc_buf[:, nts],
                               fc == 0, fc == 31)
    if tapon("t_y"):
        gdma(taps["t_y"][:], z[:], "tap", (ev_y[31],))

    ckpt("C")
    ev_sr, ev_fin = ln_finalize(stat_mm_d[31], st_a, st_b, bf_a, bf_b, st_c)
    psStat_free = ev_sr
    ev_b1 = bcast_single(bf_b, psA, (ev_fin, psAB_free[0]))
    ev_b2 = bcast_single(bf_a, psB, (psAB_free[1],))
    ev_rbc = vop(lambda e: e.tensor_copy(rb_bf[:], psA[:]), (ev_b1,))
    ev_nbc = vop(lambda e: e.tensor_copy(nb_bf[:], psB[:]), (ev_b2,))
    psAB_free = [ev_rbc, ev_nbc]

    ev_gw_pre = {}
    ev_uw_pre = {}
    for hc in (0, 1):
        par = hc % 2
        w_g = [wob_free[0], wob_free[1]]
        if DEBUG_TAPS:
            w_g.append(sch.at("tap"))
        ev_gw_pre[hc] = gdma(gwb[par][:], gup_g[hc], "gw%d" % par, w_g)
    E("gpsimd", lambda e: e.memset(tmpg_pad[:], 0.0), (ev_fin, ev_rbc, ev_nbc), inc="gs")
    E("gpsimd", lambda e: e.memset(tmpu_pad[:], 0.0), inc="gs")
    ev_pads = E("gpsimd", lambda e: e.memset(tmpd_pad[:], 0.0), inc="gs")

    ev_n3_v = ev_n3_g = None
    ev_e0 = None
    ev_l1 = {}
    w_e0 = [ev_gaup, ev_pads, psAB_free[0]]
    for fc in range(32):
        cs = slice(fc * 1024, (fc + 1) * 1024)
        ev_l1[fc] = eop(fc, lambda e, cs=cs: e.tensor_tensor(
            ln3[:, cs], z[:, cs], rb_bf[:], ALU.mult), (ev_rbc, ev_nbc), mod=5)
        ev_nn = eop(fc, lambda e, cs=cs: e.tensor_tensor(
            ln3[:, cs], ln3[:, cs], nb_bf[:], ALU.add), mod=5)
        if fc % 5 == 4:
            ev_n3_g = ev_nn
        else:
            ev_n3_v = ev_nn
        lhs = gaup_sb[:, fc * 48:(fc + 1) * 48]
        for nts in NT:
            ev_e0 = mm(psA[0:48, nts], lhs, ln3[:, fc * 1024 + nts.start:fc * 1024 + nts.stop],
                       fc == 0, fc == DC - 1, [ev_nn] + (w_e0 if fc == 0 else []))
    ev_n3 = [ev_n3_v, ev_n3_g]
    yp_ev = []
    for g in range(4):
        w = [ev_l1[g * 8 + i] for i in range(8)] + [stat_mm_d[31]]
        if DEBUG_TAPS and g == 0:
            w.append(sch.at("tap"))
        yp_ev.append(sdma(ypark[g], z[:, g * 8192:(g + 1) * 8192], "yp%d" % g, w))
    yp_all = [sch.at("yp%d" % g) for g in range(4)]
    for hc in (0, 1):
        ev_uw_pre[hc] = gdma(uwb[hc % 2][:], gup_u[hc], "uw%d" % (hc % 2),
                             (wob_free[0], wob_free[1]))
    if tapon("t_ln3"):
        gdma(taps["t_ln3"][:], ln3[:], "tap", (ev_n3,))

    ckpt("D")
    ev_tg = vop(lambda e: e.tensor_copy(tmpg_pad[0:16, :], psA[0:16, :]), (ev_e0,))
    ev_tu = vop(lambda e: e.tensor_copy(tmpu_pad[32:48, :], psA[32:48, :]))
    psAB_free[0] = ev_tu

    gw_free = [None, None]
    uw_free = [None, None]
    dw_free = [None, None]
    acct_free = [None, None]
    psA_free = ev_tu
    psB_free = psAB_free[1]
    silu_free = None
    ev_tdp = None
    ev_park = {}
    ev_dpark = {}
    stat_mm_e = {}
    pend_e = []
    last_acc_ld = [None, None]
    ev_hid_tap = None
    last_down_mm = None

    HPH = ((0, 29), (29, 29), (58, 28))
    ev_prev_down = None
    for pi, (h0, cnt) in enumerate(HPH):
        last_ph = (pi == 2)
        ev_mul = None
        for j in range(cnt):
            hc = h0 + j
            par = hc % 2
            if hc in ev_gw_pre:
                ev_gw = ev_gw_pre[hc]
                ev_uw = ev_uw_pre[hc]
            else:
                ev_gw = gdma(gwb[par][:], gup_g[hc], "gw%d" % par, (gw_free[par],))
                ev_uw = gdma(uwb[par][:], gup_u[hc], "uw%d" % par,
                             (uw_free[par], dw_free[par]))
            w0 = [ev_gw, psA_free, ev_tg, ev_n3]
            last_g = None
            for c in range(33):
                lhs = gwb[par][:, c * 128:(c + 1) * 128]
                for nts in NT:
                    rhs = (ln3[:, c * 1024 + nts.start:c * 1024 + nts.stop]
                           if c < DC else tmpg_pad[:, nts])
                    last_g = mm(psA[:, nts], lhs, rhs, c == 0, c == 32,
                                w0 if last_g is None else ())
            gw_free[par] = last_g
            ev_sl = sop(lambda e: e.activation(silu_buf[:], psA[:], AF.Silu),
                        (last_g, silu_free))
            psA_free = ev_sl
            w0 = [ev_uw, psB_free, ev_tu]
            last_u = None
            for c in range(33):
                lhs = uwb[par][:, c * 128:(c + 1) * 128]
                for nts in NT:
                    rhs = (ln3[:, c * 1024 + nts.start:c * 1024 + nts.stop]
                           if c < DC else tmpu_pad[:, nts])
                    last_u = mm(psB[:, nts], lhs, rhs, c == 0, c == 32,
                                w0 if last_u is None else ())
            uw_free[par] = last_u
            w_m = [last_u, ev_sl]
            if j == 0:
                w_m += [ev_prev_down] + yp_all
                if DEBUG_TAPS:
                    w_m.append(sch.at("tap"))
            ev_mul = vop(lambda e, j=j: e.tensor_tensor(
                hidden[:, j * 1024:(j + 1) * 1024], silu_buf[:], psB[:], ALU.mult), w_m)
            psB_free = ev_mul
            silu_free = ev_mul
        if tapon("t_hid") and pi == 0:
            ev_hid_tap = gdma(taps["t_hid"][:, 0:cnt * 1024], hidden[:, 0:cnt * 1024],
                              "tap", (ev_mul,))
        if last_ph:
            for i in range(11):
                cdma(ypre[i][:], ypark[i // 8][:, (i % 8) * 1024:(i % 8) * 1024 + 1024],
                     "pp", [gw_free[0], gw_free[1], uw_free[0], uw_free[1]] + yp_all)
        w0 = [ev_mul, psA_free, ev_da]
        last_td = None
        for j in range(cnt):
            hc = h0 + j
            lhs = da_sb[:, hc * 16:(hc + 1) * 16]
            for nts in NT:
                last_td = mm(psA[0:16, nts], lhs, hidden[:, j * 1024 + nts.start:j * 1024 + nts.stop],
                             j == 0, j == cnt - 1, w0 if last_td is None else ())
        if pi == 0:
            ev_td = vop(lambda e: e.tensor_copy(tmpd_h1[:], psA[0:16, :]), (last_td,))
        else:
            ev_td = vop(lambda e: e.tensor_add(tmpd_h1[:], tmpd_h1[:], psA[0:16, :]),
                        (last_td,))
            if last_ph:
                ev_tdp = vop(lambda e: e.tensor_copy(tmpd_pad[0:16, :], tmpd_h1[:]))
                ev_td = ev_tdp
        psA_free = ev_td

        ev_gu_done = (gw_free[0], gw_free[1], uw_free[0], uw_free[1])
        ap_prev = sch.at("ap")
        nch = cnt + (1 if last_ph else 0)
        col0 = h0 * 128
        psd_free = [psA_free, psB_free]
        for fc in range(32):
            par = fc % 2
            ev_dw = gdma(dwb[par][:, 0:nch * 128], dwn_t[fc][:, col0:col0 + nch * 128],
                         "dw%d" % par, (dw_free[par], uw_free[par]))
            ps = psAB[par]
            w0 = [ev_dw, ev_mul, psd_free[par]]
            if last_ph:
                w0.append(ev_tdp)
            last = None
            for j in range(nch):
                lhs = dwb[par][:, j * 128:(j + 1) * 128]
                for nts in NT:
                    rhs = (hidden[:, j * 1024 + nts.start:j * 1024 + nts.stop]
                           if j < cnt else tmpd_pad[:, nts])
                    last = mm(ps[:, nts], lhs, rhs, j == 0, j == nch - 1,
                              w0 if last is None else ())
            dw_free[par] = last
            last_down_mm = last
            while pend_e:
                pend_e.pop(0)()
            if pi == 0:
                ev_cp = vop(lambda e, par=par: e.tensor_copy(acct[par][:], psAB[par][:]),
                            (last, acct_free[par]))
                ev_park[fc] = sdma(accd[fc], acct[par][:], "ap", (ev_cp,))
                acct_free[par] = ev_park[fc]
                psd_free[par] = ev_cp
            elif pi == 1:
                ev_ld = sdma(acct[par][:], accd[fc], "ac%d" % par,
                             (acct_free[par], ap_prev))
                ev_add = vop(lambda e, par=par: e.tensor_tensor(
                    acct[par][:], psAB[par][:], acct[par][:], ALU.add), (last, ev_ld))
                psd_free[par] = ev_add
                ev_park[fc] = sdma(accd[fc], acct[par][:], "ap", (ev_add,))
                acct_free[par] = ev_park[fc]
            else:
                ev_ld = sdma(acct[par][:], accd[fc], "ac%d" % par,
                             (acct_free[par], ap_prev))
                last_acc_ld[par] = ev_ld
                w_add = [last, ev_ld]
                if fc == 0:
                    w_add.append(ev_gu_done)
                ds = dout_sb[:, fc * 1024:(fc + 1) * 1024]
                ev_add = vop(lambda e, par=par, ds=ds: e.tensor_tensor(
                    ds, psAB[par][:], acct[par][:], ALU.add), w_add)
                psd_free[par] = ev_add
                acct_free[par] = ev_add
                w_s = [ev_add]
                if fc >= 1:
                    w_s.append(stat_mm_e[fc - 1])
                ev_sq = sop(lambda e, ds=ds: e.activation(sqc_buf[:], ds, AF.Square), w_s)
                def emit_stats_e(fc=fc, ev_sq=ev_sq, ds=ds):
                    w_st = [ev_sq]
                    if fc == 0:
                        w_st.append(psStat_free)
                    first = True
                    for nts in NT:
                        mm(psStat[0:1, nts], ones_col[:, 0:1],
                           ds[:, nts], fc == 0, fc == 31, w_st if first else ())
                        first = False
                        stat_mm_e[fc] = mm(psStat[32:33, nts], ones_col[:, 0:1], sqc_buf[:, nts],
                                           fc == 0, fc == 31)
                pend_e.append(emit_stats_e)
        while pend_e:
            pend_e.pop(0)()
        ev_prev_down = last_down_mm
        psA_free, psB_free = psd_free[0], psd_free[1]
        psAB_free = [psA_free, psB_free]

    ckpt("E")
    ev_sr, ev_fin = ln_finalize(stat_mm_e[31], est_a, est_b, bf_a, bf_b, est_c,
                                extra=(last_acc_ld[0], last_acc_ld[1]))
    psStat_free = ev_sr
    ev_lo_r = hi_lo(est_b, bf_b, est_c, lo1_bf, (ev_fin,))
    ev_b1 = bcast_double(bf_b, lo1_bf, psA, (ev_lo_r, psAB_free[0]))
    ev_rbf = vop(lambda e: e.tensor_copy(erb_f32[:], psA[:]), (ev_b1,))
    ev_lo_n = hi_lo(est_a, bf_a, est_c, lo2_bf, (ev_lo_r,))
    ev_b2 = bcast_double(bf_a, lo2_bf, psB, (ev_lo_n, psAB_free[1]))
    ev_nbf = vop(lambda e: e.tensor_copy(enb_f32[:], psB[:]), (ev_b2,))

    pp_all = sch.at("pp")
    yt_free = [None] * 4
    ot_free = [None] * 4
    POOL_FC = frozenset((1, 3, 5, 8, 10, 12, 15, 17, 19, 22, 24, 26, 29, 31))
    ev_out = None
    for fc in range(32):
        b = fc % 4
        if fc < 11:
            ybuf, ev_yl = ypre[fc], pp_all
        else:
            ybuf = ytile[b]
            w_y = [yt_free[b]] if fc >= 15 else [last_down_mm]
            ev_yl = cdma(ybuf[:], ypark[fc // 8][:, (fc % 8) * 1024:(fc % 8) * 1024 + 1024],
                         "yt%d" % b, w_y)
        ds = dout_sb[:, fc * 1024:(fc + 1) * 1024]
        ot = otile[b]
        eng = "gpsimd" if fc in POOL_FC else "vector"
        esem = "gs" if eng == "gpsimd" else "vs"
        E(eng, lambda e, ds=ds, ot=ot: e.tensor_tensor(ot[:], ds, erb_f32[:], ALU.mult),
          (ev_rbf, ev_nbf, ot_free[b]), inc=esem)
        E(eng, lambda e, ot=ot: e.tensor_tensor(ot[:], ot[:], enb_f32[:], ALU.add), inc=esem)
        ev_o = E(eng, lambda e, ot=ot, ybuf=ybuf: e.tensor_tensor(
            ot[:], ot[:], ybuf[:], ALU.add), (ev_yl,), inc=esem)
        if fc % 2 == 0:
            ev_out = sdma(o[fc * 128:(fc + 1) * 128, :], ot[:], "ow%d" % (b // 2), (ev_o,))
        else:
            ev_out = gdma(o[fc * 128:(fc + 1) * 128, :], ot[:], "og%d" % (b // 2), (ev_o,))
        yt_free[b] = ev_o
        ot_free[b] = ev_out
    stopped[0] = False
    E("sync", None, tuple(sch.at(s) for s in SD_SEMS))
    E("gpsimd", None, tuple(sch.at(s) for s in GD_SEMS))
    E("scalar", None, tuple(sch.at(s) for s in SC_SEMS))

    es = contextlib.ExitStack()
    with es:
        sems = {}
        for s in ("ts", "vs", "ss", "gs") + GD_SEMS + SD_SEMS + SC_SEMS:
            sems[s] = es.enter_context(nc.semaphore(s))
        block = es.enter_context(nc.Block())

        def runner(name):
            def run(e):
                for waits, fn, inc, n in sch.ops[name]:
                    for s, v in waits:
                        e.wait_ge(sems[s], v)
                    ins = fn(e) if fn is not None else None
                    if inc is not None:
                        ins.then_inc(sems[inc], n)
            return run

        block.tensor(runner("tensor"))
        block.vector(runner("vector"))
        block.scalar(runner("scalar"))
        block.gpsimd(runner("gpsimd"))
        block.sync(runner("sync"))
    return nc


def _prep_shared(inputs):
    f = np.float32
    Wqkv = np.asarray(inputs["L0_attn_Wqkv"], f)
    Wo = np.asarray(inputs["L0_attn_Wo"], f)
    gate_w = np.asarray(inputs["L0_ffn_gate_w"], f)
    up_w = np.asarray(inputs["L0_ffn_up_w"], f)
    down_w = np.asarray(inputs["L0_ffn_down_w"], f)
    gate_a = np.asarray(inputs["L0_ffn_gate_a"], f)
    gate_b = np.asarray(inputs["L0_ffn_gate_b"], f)
    up_a = np.asarray(inputs["L0_ffn_up_a"], f)
    up_b = np.asarray(inputs["L0_ffn_up_b"], f)
    down_a = np.asarray(inputs["L0_ffn_down_a"], f)
    down_b = np.asarray(inputs["L0_ffn_down_b"], f)
    bqkv = np.asarray(inputs["L0_attn_bqkv"], f)
    bo = np.asarray(inputs["L0_attn_bo"], f)

    d = {}
    d["wqkv_t"] = np.ascontiguousarray(
        Wqkv.reshape(32, 128, 24, 128).transpose(2, 1, 0, 3).reshape(24, 128, 4096))
    d["wo_t"] = np.ascontiguousarray(
        Wo.reshape(8, 128, 32, 128).transpose(2, 1, 0, 3).reshape(32, 128, 1024))

    aug_g = np.zeros((4224, DH), f)
    aug_g[:4096] = gate_w.T
    aug_g[4096:4112] = gate_b.T
    d["gup_g"] = np.ascontiguousarray(
        aug_g.reshape(33, 128, 86, 128).transpose(2, 1, 0, 3).reshape(86, 128, 4224))
    aug_u = np.zeros((4224, DH), f)
    aug_u[:4096] = up_w.T
    aug_u[4128:4144] = up_b.T
    d["gup_u"] = np.ascontiguousarray(
        aug_u.reshape(33, 128, 86, 128).transpose(2, 1, 0, 3).reshape(86, 128, 4224))

    aug_d = np.zeros((11136, 4096), f)
    aug_d[:DH] = down_w.T
    aug_d[DH:DH + 16] = down_b.T
    d["dwn_t"] = np.ascontiguousarray(
        aug_d.reshape(87, 128, 32, 128).transpose(2, 1, 0, 3).reshape(32, 128, 11136))

    A = np.zeros((4096, 48), f)
    A[:, 0:16] = gate_a.T
    A[:, 32:48] = up_a.T
    d["gaup_h"] = np.ascontiguousarray(
        A.reshape(32, 128, 48).transpose(1, 0, 2).reshape(128, 1536))
    d["da_h"] = np.ascontiguousarray(
        down_a.T.reshape(86, 128, 16).transpose(1, 0, 2).reshape(128, 1376))

    d["bqkv_h"] = np.ascontiguousarray(bqkv.reshape(24, 128).T)
    d["bo_h"] = np.ascontiguousarray(bo.reshape(32, 128).T)
    d["ident_h"] = np.eye(128, dtype=f)
    return d


def _tile_x(xT_half):
    return np.ascontiguousarray(
        xT_half.reshape(4, 8, 128, 1024).transpose(0, 2, 1, 3).reshape(4, 128, 8192))


def kernel(**inputs):
    shared = _prep_shared(inputs)
    x = np.asarray(inputs["L0_attn_x"], np.float32)
    in_maps = []
    for c in range(8):
        b, half = c // 2, c % 2
        xT = x[b].T
        m = dict(shared)
        m["xo_t"] = _tile_x(np.ascontiguousarray(xT[:, half * T:(half + 1) * T]))
        m["xp_t"] = _tile_x(np.ascontiguousarray(xT[:, (1 - half) * T:(2 - half) * T]))
        in_maps.append(m)
    nc = build_program()
    res = run_bass_kernel_spmd(nc, in_maps, core_ids=list(range(8)))
    out = np.empty((4, 2048, 4096), np.float32)
    for c in range(8):
        b, half = c // 2, c % 2
        out[b, half * T:(half + 1) * T, :] = np.asarray(res.results[c]["o"]).T
    return out


# version 19
# speedup vs baseline: 1.0066x; 1.0066x over previous
import contextlib
import numpy as np
import concourse.bass as bass
import concourse.mybir as mybir
from concourse.bass_utils import run_bass_kernel_spmd

F32 = mybir.dt.float32
BF16 = mybir.dt.bfloat16
AF = mybir.ActivationFunctionType
ALU = mybir.AluOpType
AX = mybir.AxisListType

D = 4096
DC = 32
T = 1024
S = 2048
DH = 11008
HCN = 86
HH = 43
EPS = 1e-5
SM_SCALE = 0.03125
INV_D = 1.0 / 4096.0

NT = (slice(0, 512), slice(512, 1024))
ENGINES = ("tensor", "vector", "scalar", "gpsimd", "sync")

DEBUG_TAPS = False
TAP_KEYS = None
TRUNC = "full"
CSKIP = ()
TAIL_POOL = True
TAIL_OG = True


class _Stop(Exception):
    pass

PERS = 17024
BG = PERS + 1280
R56 = BG + 147456
R7 = R56 + 32768
ARENA_END = R7 + 30464
assert ARENA_END <= 229344 - 128, ARENA_END


class Sched:
    def __init__(self):
        self.ops = {e: [] for e in ENGINES}
        self.cnt = {}
        self.last_wait = {e: {} for e in ENGINES}

    def emit(self, eng, fn, waits=(), inc=None, n=1):
        w = {}
        def add(ev):
            if ev is None:
                return
            if isinstance(ev, (list, tuple)) and len(ev) == 2 \
                    and isinstance(ev[0], str) and isinstance(ev[1], int):
                s, v = ev
                if v > self.last_wait[eng].get(s, 0):
                    w[s] = max(w.get(s, 0), v)
                return
            for sub in ev:
                add(sub)
        add(list(waits))
        for s, v in w.items():
            self.last_wait[eng][s] = v
        ev_out = None
        if inc is not None:
            self.cnt[inc] = self.cnt.get(inc, 0) + n
            ev_out = (inc, self.cnt[inc])
        self.ops[eng].append((tuple(w.items()), fn, inc, n))
        return ev_out

    def at(self, sem):
        return (sem, self.cnt.get(sem, 0))


def build_program():
    nc = bass.Bass("TRN2", target_bir_lowering=False)

    def inp(name, shape):
        return nc.dram_tensor(name, list(shape), F32, kind="ExternalInput").ap()

    xo_t = inp("xo_t", (4, 128, 8192))
    xp_t = inp("xp_t", (4, 128, 8192))
    wqkv_t = inp("wqkv_t", (24, 128, 4096))
    wo_t = inp("wo_t", (32, 128, 1024))
    gup_g = inp("gup_g", (HCN, 128, 4224))
    gup_u = inp("gup_u", (HCN, 128, 4224))
    dwn_t = inp("dwn_t", (32, 128, 11136))
    gaup_h = inp("gaup_h", (128, 1536))
    da_h = inp("da_h", (128, 1376))
    bqkv_h = inp("bqkv_h", (128, 24))
    bo_h = inp("bo_h", (128, 32))
    ident_h = inp("ident_h", (128, 128))

    o = nc.dram_tensor("o", [D, T], F32, kind="ExternalOutput").ap()
    ypark = nc.dram_tensor("ypark", [4, 128, 8192], BF16, kind="Internal").ap()
    xbf = nc.dram_tensor("xbf", [4, 128, 8192], BF16, kind="Internal").ap()
    accd = nc.dram_tensor("accd", [32, 128, 1024], F32, kind="Internal").ap()
    doutd = nc.dram_tensor("doutd", [32, 128, 1024], F32, kind="Internal").ap()

    taps = {}
    def tapon(nm):
        return DEBUG_TAPS and (TAP_KEYS is None or nm in TAP_KEYS)
    if DEBUG_TAPS:
        for nm, shape in (
            ("t_ln1", (128, 32768)), ("t_q", (128, 8192)), ("t_k", (128, 16384)),
            ("t_vtm", (128, 16384)), ("t_probsT", (128, 16384)),
            ("t_negmaxT", (1, 1024)), ("t_ao", (128, 8192)),
            ("t_y", (128, 32768)), ("t_ln3", (128, 32768)), ("t_hid", (128, 44032)),
        ):
            taps[nm] = nc.dram_tensor(nm, list(shape), F32, kind="ExternalOutput").ap()

    def sb(name, off, shape, dtype):
        return nc.alloc_sbuf_tensor_at(name, list(shape), dtype, offset=off).ap()

    ident_bf = sb("ident_bf", PERS + 0, (128, 128), BF16)
    ones_row = sb("ones_row", PERS + 256, (1, 128), BF16)
    ones_col = sb("ones_col", PERS + 512, (128, 1), BF16)
    eps_sb = sb("eps_sb", PERS + 576, (1, 1), F32)
    bqkv_sb = sb("bqkv_sb", PERS + 640, (128, 24), F32)
    bo_sb = sb("bo_sb", PERS + 768, (128, 32), F32)
    nm1 = [sb("nm1a", PERS + 896, (128, 1), F32),
           sb("nm1b", PERS + 928, (128, 1), F32)]
    nm2 = [sb("nm2a", PERS + 960, (128, 1), F32),
           sb("nm2b", PERS + 992, (128, 1), F32)]
    nmbf = [sb("nmbfa", PERS + 1024, (128, 1), BF16),
            sb("nmbfb", PERS + 1056, (128, 1), BF16)]

    q_fm = sb("q_fm", BG + 0, (128, 8192), BF16)
    k_fm = sb("k_fm", BG + 16384, (128, 16384), BF16)
    v_fm = sb("v_fm", BG + 49152, (128, 16384), BF16)
    xh = sb("xh", BG + 81920, (128, 32768), BF16)
    v_tm = sb("v_tm", BG + 81920, (128, 16384), BF16)
    probsT = sb("probsT", BG + 114688, (128, 16384), BF16)
    z = sb("z", BG + 0, (128, 32768), BF16)
    ln3 = sb("ln3", BG + 65536, (128, 32768), BF16)
    hidden = sb("hidden", BG + 0, (128, 29696), BF16)
    uwb = [sb("uwb0", R56 + 16896, (128, 4224), BF16),
           sb("uwb1", BG + 131072, (128, 4224), BF16)]
    dwb = [sb("dwb0", R56 + 16896, (128, 3712), BF16),
           sb("dwb1", BG + 131072, (128, 3712), BF16)]
    ytile = [sb("ytile%d" % i, BG + 2048 * i, (128, 1024), BF16)
             for i in range(4)]
    ypre = [sb("ypre%d" % i, R56 + 256 + 2048 * i, (128, 1024), BF16)
            for i in range(8)] + \
           [sb("ypre%d" % (8 + i), BG + 59392 + 2048 * i, (128, 1024), BF16)
            for i in range(3)]
    otile = [sb("otile%d" % i, BG + 131072 + 4096 * i, (128, 1024), F32)
             for i in range(4)]

    wqb = [sb("wqb0", R56 + 0, (128, 4096), BF16),
           sb("wqb1", R56 + 8192, (128, 4096), BF16)]
    attnout = sb("attnout", R56 + 0, (128, 8192), BF16)
    wob = [sb("wob0", R56 + 16384, (128, 1024), BF16),
           sb("wob1", R56 + 18432, (128, 1024), BF16)]
    xtile = [sb("xtile%d" % i, R7 + 4096 + 2048 * i, (128, 1024), BF16)
             for i in range(4)]
    zbf_buf = sb("zbf_buf", R56 + 25344, (128, 1024), BF16)
    sqc_buf = sb("sqc_buf", R56 + 27392, (128, 1024), BF16)
    lo1_bf = sb("lo1_bf", R56 + 25344, (1, 1024), BF16)
    lo2_bf = sb("lo2_bf", R56 + 27392, (1, 1024), BF16)
    gwb = [sb("gwb0", R56 + 0, (128, 4224), BF16),
           sb("gwb1", R56 + 8448, (128, 4224), BF16)]
    dout_sb = sb("dout_sb", BG + 65536, (128, 32768), BF16)

    sqb = [sb("sqb0", R7 + 0, (128, 1024), BF16),
           sb("sqb1", R7 + 2048, (128, 1024), BF16)]
    st_a = sb("st_a", R7 + 4096, (1, 1024), F32)
    st_b = sb("st_b", R7 + 8192, (1, 1024), F32)
    bf_a = sb("bf_a", R7 + 12288, (1, 1024), BF16)
    bf_b = sb("bf_b", R7 + 14336, (1, 1024), BF16)
    rb_bf = sb("rb_bf", R7 + 16384, (128, 1024), BF16)
    nb_bf = sb("nb_bf", R7 + 18432, (128, 1024), BF16)
    negmaxT = sb("negmaxT", R7 + 0, (1, 1024), BF16)
    zbf2 = sb("zbf2", R7 + 0, (128, 1024), BF16)
    sqc2 = sb("sqc2", R7 + 2048, (128, 1024), BF16)
    st_c = sb("st_c", R7 + 0, (1, 1024), F32)
    rb_f32 = sb("rb_f32", R7 + 16384, (128, 1024), F32)
    nb_f32 = sb("nb_f32", R7 + 20480, (128, 1024), F32)
    tmpg_pad = sb("tmpg_pad", R7 + 0, (128, 1024), BF16)
    tmpu_pad = sb("tmpu_pad", R7 + 2048, (128, 1024), BF16)
    tmpd_pad = sb("tmpd_pad", R7 + 4096, (128, 1024), BF16)
    silu_buf = sb("silu_buf", R7 + 6144, (128, 1024), BF16)
    acct = [sb("acct0", R7 + 8192, (128, 1024), F32),
            sb("acct1", R7 + 12288, (128, 1024), F32)]
    tmpd_h1 = sb("tmpd_h1", R7 + 16384, (16, 1024), F32)
    est_a = sb("est_a", R7 + 0, (1, 1024), F32)
    est_b = sb("est_b", R7 + 4096, (1, 1024), F32)
    est_c = sb("est_c", R7 + 8192, (1, 1024), F32)
    erb_v = sb("erb_v", R7 + 16384, (128, 1024), BF16)
    enb_v = sb("enb_v", R7 + 18432, (128, 1024), BF16)
    tbv = sb("tbv", R7 + 20480, (128, 1024), BF16)
    tbp = sb("tbp", R7 + 22528, (128, 1024), BF16)
    erb_p = sb("erb_p", BG + 8192, (128, 1024), BF16)
    enb_p = sb("enb_p", BG + 10240, (128, 1024), BF16)
    gaup_sb = sb("gaup_sb", R7 + 24576, (128, 1536), BF16)
    da_sb = sb("da_sb", R7 + 27648, (128, 1376), BF16)

    psA = nc.alloc_psum_tensor("psA", [128, 1024], F32).ap()
    psB = nc.alloc_psum_tensor("psB", [128, 1024], F32).ap()
    psStat = nc.alloc_psum_tensor("psStat", [33, 1024], F32).ap()
    psT = [nc.alloc_psum_tensor("psT1", [128, 1024], BF16).ap(),
           nc.alloc_psum_tensor("psT2", [128, 1024], BF16).ap()]
    psAB = (psA, psB)

    sch = Sched()

    def E(eng, fn, waits=(), inc=None, n=1):
        if stopped[0]:
            return None
        return sch.emit(eng, fn, waits, inc, n)

    stopped = [False]

    def ckpt(name):
        if TRUNC == name:
            stopped[0] = True

    GD_SEMS = ("su", "su2", "xa0", "xa1", "xa2", "xa3", "xa4", "xa5", "xa6", "xa7",
               "wq0", "wq1", "wo0", "wo1",
               "gw0", "gw1", "uw0", "uw1", "dw0", "dw1", "og0", "og1", "tap")
    SC_SEMS = ("yt0", "yt1", "yt2", "yt3", "xk0", "xk1", "xk2", "xk3",
               "xk4", "xk5", "xk6", "xk7",
               "pp")
    SD_SEMS = ("sb", "xt0", "xt1", "xt2", "xt3", "yp0", "yp1", "yp2", "yp3", "ap",
               "ac0", "ac1", "ow0", "ow1")

    def gdma(dst, src, sem, waits=()):
        assert sem in GD_SEMS, sem
        return E("gpsimd", lambda e, d=dst, s=src: e.dma_start(d, s), waits, sem, 16)

    def sdma(dst, src, sem, waits=()):
        assert sem in SD_SEMS, sem
        return E("sync", lambda e, d=dst, s=src: e.dma_start(d, s), waits, sem, 16)

    def cdma(dst, src, sem, waits=()):
        assert sem in SC_SEMS, sem
        return E("scalar", lambda e, d=dst, s=src: e.dma_start(d, s), waits, sem, 16)

    def mm(out_ap, lhs_ap, rhs_ap, start, stop, waits=()):
        return E("tensor",
                 lambda e, o=out_ap, l=lhs_ap, r=rhs_ap, st=start, sp=stop:
                 e.matmul(o, l, r, start=st, stop=sp), waits, "ts")

    def vop(fn, waits=()):
        return E("vector", fn, waits, "vs")

    def sop(fn, waits=()):
        return E("scalar", fn, waits, "ss")

    E("gpsimd", lambda e: e.memset(ones_row[:], 1.0), inc="gs")
    E("gpsimd", lambda e: e.memset(ones_col[:], 1.0), inc="gs")
    ev_memset = E("gpsimd", lambda e: e.memset(eps_sb[:], EPS), inc="gs")
    ev_ident = gdma(ident_bf[:], ident_h[:], "su")
    sdma(bqkv_sb[:], bqkv_h[:], "sb")
    sdma(bo_sb[:], bo_h[:], "sb")
    ev_bq = ev_bo = sch.at("sb")

    psAB_free = [None, None]
    psStat_free = None

    def ln_finalize(wait_ev, a_st, b_st, a_bf, b_bf, scr, extra=()):
        E("vector", lambda e: e.tensor_scalar(a_st[:], psStat[0:1, :], INV_D, None, ALU.mult),
          (wait_ev, ev_memset) + tuple(extra), inc="vs")
        ev_sr = E("vector", lambda e: e.tensor_scalar(b_st[:], psStat[32:33, :], INV_D, None, ALU.mult),
                  inc="vs")
        E("vector", lambda e: e.scalar_tensor_tensor(scr[:], a_st[:], -1.0, a_st[:],
                                                     ALU.mult, ALU.mult), inc="vs")
        ev_var = E("vector", lambda e: e.tensor_add(b_st[:], b_st[:], scr[:]), inc="vs")
        E("scalar", lambda e: e.activation(b_st[:], b_st[:], AF.Ln, bias=eps_sb[0:1, 0:1]),
          (ev_var,), inc="ss")
        ev_rstd = E("scalar", lambda e: e.activation(b_st[:], b_st[:], AF.Exp, scale=-0.5),
                    inc="ss")
        E("vector", lambda e: e.scalar_tensor_tensor(a_st[:], a_st[:], -1.0, b_st[:],
                                                     ALU.mult, ALU.mult), (ev_rstd,), inc="vs")
        E("vector", lambda e: e.tensor_copy(a_bf[:], a_st[:]), inc="vs")
        ev_done = E("vector", lambda e: e.tensor_copy(b_bf[:], b_st[:]), inc="vs")
        return ev_sr, ev_done

    def bcast_single(bf_ap, ps, waits):
        mm(ps[:, 0:512], ones_row[0:1, :], bf_ap[0:1, 0:512], True, True, waits)
        return mm(ps[:, 512:1024], ones_row[0:1, :], bf_ap[0:1, 512:1024], True, True)

    def bcast_double(hi_ap, lo_ap, ps, waits):
        mm(ps[:, 0:512], ones_row[0:1, :], hi_ap[0:1, 0:512], True, False, waits)
        mm(ps[:, 0:512], ones_row[0:1, :], lo_ap[0:1, 0:512], False, True)
        mm(ps[:, 512:1024], ones_row[0:1, :], hi_ap[0:1, 512:1024], True, False)
        return mm(ps[:, 512:1024], ones_row[0:1, :], lo_ap[0:1, 512:1024], False, True)

    def hi_lo(src_f32, hi_bf, lo_f32_scr, lo_bf, waits):
        ev1 = sop(lambda e: e.activation(lo_f32_scr[:], hi_bf[:], AF.Identity), waits)
        ev2 = vop(lambda e: e.tensor_sub(lo_f32_scr[:], src_f32[:], lo_f32_scr[:]), (ev1,))
        return vop(lambda e: e.tensor_copy(lo_bf[:], lo_f32_scr[:]), (ev2,))

    wq_free = [None, None]
    ev_qkv_evac = {}
    last_qkv_mm = None

    for hi_half, (xsrc, fcs) in enumerate(((xo_t, list(range(24))),
                                           (xp_t, list(range(8, 24))))):
        ev_wq_pre = {}
        ld_s = []
        for sg in range(8):
            g, hlf = sg // 2, sg % 2
            waits = [last_qkv_mm, sch.at("xk%d" % sg)] if hi_half == 1 else []
            ld_s.append(gdma(xh[:, sg * 4096:(sg + 1) * 4096],
                             xsrc[g][:, hlf * 4096:(hlf + 1) * 4096],
                             "xa%d" % sg, waits))
            if sg == 1:
                ev_wq_pre[0] = gdma(wqb[0][:], wqkv_t[fcs[0]], "wq0", (wq_free[0],))
        ev_wq_pre[1] = gdma(wqb[1][:], wqkv_t[fcs[1]], "wq1", (wq_free[1],))
        sq_done = {}
        stat_mm = {}
        park_ev = {}
        for c in range(DC):
            if hi_half == 0 and c % 4 == 0:
                sg = c // 4
                park_ev[sg] = cdma(xbf[sg // 2][:, (sg % 2) * 4096:(sg % 2 + 1) * 4096],
                                   xh[:, sg * 4096:(sg + 1) * 4096], "xk%d" % sg,
                                   (ld_s[sg],))
            w = [ld_s[c // 4]]
            if c >= 2:
                w.append(stat_mm[c - 2])
            sq_done[c] = sop(lambda e, c=c: e.activation(
                sqb[c % 2][:], xh[:, c * 1024:(c + 1) * 1024], AF.Square), w)
            st, sp = (c == 0), (c == DC - 1)
            w2 = [sq_done[c]]
            if c == 0:
                w2.append(psStat_free)
            first = True
            for nts in NT:
                mm(psStat[0:1, nts], ones_col[:, 0:1],
                   xh[:, c * 1024 + nts.start:c * 1024 + nts.stop], st, sp,
                   w2 if first else ())
                first = False
                stat_mm[c] = mm(psStat[32:33, nts], ones_col[:, 0:1],
                                sqb[c % 2][:, nts], st, sp)
        ev_sr, ev_fin = ln_finalize(stat_mm[DC - 1], st_a, st_b, bf_a, bf_b, st_c)
        psStat_free = ev_sr
        ev_b1 = bcast_single(bf_b, psA, (ev_fin, psAB_free[0]))
        ev_b2 = bcast_single(bf_a, psB, (psAB_free[1],))
        ev_rbc = vop(lambda e: e.tensor_copy(rb_bf[:], psA[:]), (ev_b1,))
        ev_nbc = vop(lambda e: e.tensor_copy(nb_bf[:], psB[:]), (ev_b2,))
        psAB_free = [ev_rbc, ev_nbc]
        ev_norm_v = ev_norm_g = None
        norm_ev = {}
        for c in range(DC):
            cs = slice(c * 1024, (c + 1) * 1024)
            w1 = [ev_rbc, ev_nbc]
            if hi_half == 0:
                w1.append(park_ev[c // 4])
            if c % 5 == 4:
                E("gpsimd", lambda e, cs=cs: e.tensor_tensor(
                    xh[:, cs], xh[:, cs], rb_bf[:], ALU.mult), w1, inc="gs")
                ev_norm_g = E("gpsimd", lambda e, cs=cs: e.tensor_tensor(
                    xh[:, cs], xh[:, cs], nb_bf[:], ALU.add), inc="gs")
                norm_ev[c] = ev_norm_g
            else:
                vop(lambda e, cs=cs: e.tensor_tensor(xh[:, cs], xh[:, cs], rb_bf[:], ALU.mult),
                    w1)
                ev_norm_v = vop(lambda e, cs=cs: e.tensor_tensor(
                    xh[:, cs], xh[:, cs], nb_bf[:], ALU.add))
                norm_ev[c] = ev_norm_v
        ev_norm = [ev_norm_v, ev_norm_g]
        if tapon("t_ln1") and hi_half == 0:
            gdma(taps["t_ln1"][:], xh[:], "tap", (ev_norm,))
        for i, fc in enumerate(fcs):
            par = i % 2
            if i in ev_wq_pre:
                ev_w = ev_wq_pre[i]
            else:
                ev_w = gdma(wqb[par][:], wqkv_t[fc], "wq%d" % par, (wq_free[par],))
            w0 = [ev_w, psAB_free[par]]
            if i >= 2:
                w0.append(ev_norm)
            last = None
            for c in range(DC):
                w_c = list(w0) if last is None else []
                if i < 2:
                    w_c.append(norm_ev[c])
                for nts in NT:
                    last = mm(psAB[par][:, nts], wqb[par][:, c * 128:(c + 1) * 128],
                              xh[:, c * 1024 + nts.start:c * 1024 + nts.stop],
                              c == 0, c == DC - 1, w_c if w_c else ())
                    w_c = []
            wq_free[par] = last
            last_qkv_mm = last
            if fc < 8:
                dst = q_fm[:, fc * 1024:(fc + 1) * 1024]
            elif fc < 16:
                base = (fc - 8) * 2048 + hi_half * 1024
                dst = k_fm[:, base:base + 1024]
            else:
                base = (fc - 16) * 2048 + hi_half * 1024
                dst = v_fm[:, base:base + 1024]
            ev_ev = vop(lambda e, par=par, dst=dst, fc=fc: e.tensor_scalar(
                dst, psAB[par][:], bqkv_sb[:, fc:fc + 1], None, ALU.add), (last, ev_bq))
            psAB_free[par] = ev_ev
            ev_qkv_evac[(hi_half, fc)] = ev_ev

    ev_q_done = ev_qkv_evac[(0, 7)]
    ev_k_done = ev_qkv_evac[(1, 15)]
    ev_v_done = ev_qkv_evac[(1, 23)]
    if tapon("t_q"):
        gdma(taps["t_q"][:], q_fm[:], "tap", (ev_q_done,))
    if tapon("t_k"):
        gdma(taps["t_k"][:], k_fm[:], "tap", (ev_k_done,))

    ckpt("A")
    gdma(gaup_sb[:], gaup_h[:], "su2")
    gdma(da_sb[:], da_h[:], "su2")
    ev_gaup = ev_da = sch.at("su2")
    psT_free = [None, None]
    ev_vtm = None
    for kc in range(16):
        par = kc % 2
        tr = None
        for dc in range(8):
            w = (ev_v_done, psT_free[par], ev_ident) if dc == 0 else ()
            src = v_fm[:, dc * 2048 + kc * 128:dc * 2048 + kc * 128 + 128]
            tr = E("tensor", lambda e, par=par, dc=dc, src=src: e.transpose(
                psT[par][:, dc * 128:(dc + 1) * 128], src, ident_bf[:]), w, "ts")
        ev_vtm = vop(lambda e, par=par, kc=kc: e.tensor_copy(
            v_tm[:, kc * 1024:(kc + 1) * 1024], psT[par][:]), (tr,))
        psT_free[par] = ev_vtm
    if tapon("t_vtm"):
        gdma(taps["t_vtm"][:], v_tm[:], "tap", (ev_vtm,))

    last_q_mm = None
    for c in range(8):
        w = [ev_q_done] if c == 0 else [last_q_mm]
        ev_s = sop(lambda e, c=c: e.activation(
            sqb[1][:], q_fm[:, c * 1024:(c + 1) * 1024], AF.Square), w)
        first = True
        for nts in NT:
            last_q_mm = mm(psStat[0:1, nts], ones_col[:, 0:1], sqb[1][:, nts],
                           c == 0, c == 7,
                           (ev_s, psStat_free) if first else (ev_s,))
            first = False
    ev_mk = []
    last_k_mm = None
    for half in range(2):
        for c in range(8):
            w = [ev_k_done] if (half == 0 and c == 0) else [last_k_mm]
            if half == 1 and c == 0:
                w.append(ev_mk[0])
            ev_s = sop(lambda e, c=c, half=half: e.activation(
                sqb[0][:], k_fm[:, c * 2048 + half * 1024:c * 2048 + half * 1024 + 1024],
                AF.Square), w)
            first = True
            for nts in NT:
                last_k_mm = mm(psStat[32:33, nts], ones_col[:, 0:1], sqb[0][:, nts],
                               c == 0, c == 7, (ev_s,) if first else ())
                first = False
        ev_mk.append(vop(lambda e, half=half: e.reduce_max(
            nm1[half][0:1, 0:1], psStat[32:33, :], axis=AX.X), (last_k_mm,)))
    ev_mkx = vop(lambda e: e.tensor_tensor(nm1[0][0:1, 0:1], nm1[0][0:1, 0:1],
                                           nm1[1][0:1, 0:1], ALU.max), (ev_mk[0], ev_mk[1]))
    ev_p = vop(lambda e: e.tensor_scalar(st_a[:], psStat[0:1, :], nm1[0][0:1, 0:1],
                                         None, ALU.mult), (ev_mkx, ev_mkx))
    ev_l = sop(lambda e: e.activation(st_a[:], st_a[:], AF.Ln), (ev_p,))
    ev_e = sop(lambda e: e.activation(st_a[:], st_a[:], AF.Exp, scale=0.5))
    ev_nmT_evac = vop(lambda e: e.tensor_scalar(negmaxT[:], st_a[:], -1.0, None,
                                                ALU.mult), (ev_e,))
    psStat_free = ev_nmT_evac
    psAB_free = [psAB_free[0], psAB_free[1]]
    if tapon("t_negmaxT"):
        gdma(taps["t_negmaxT"][:], negmaxT[:], "tap", (ev_nmT_evac,))

    pend_b = []
    ev_exp = None
    last_rs = None
    for kc in range(16):
        par = kc % 2
        ps = psAB[par]
        w0 = [ev_nmT_evac, psAB_free[par]]
        last = None
        for dc in range(8):
            lhs = k_fm[:, dc * 2048 + kc * 128:dc * 2048 + kc * 128 + 128]
            for nts in NT:
                last = mm(ps[:, nts], lhs, q_fm[:, dc * 1024 + nts.start:dc * 1024 + nts.stop],
                          dc == 0, False, w0 if last is None else ())
        for nts in NT:
            last = mm(ps[:, nts], ones_row[0:1, :], negmaxT[0:1, nts], False, True)
        ev_exp = sop(lambda e, par=par, kc=kc: e.activation(
            probsT[:, kc * 1024:(kc + 1) * 1024], psAB[par][:], AF.Exp, scale=SM_SCALE),
            (last,))
        psAB_free[par] = ev_exp
        def emit_rs(kc=kc, ev_exp=ev_exp):
            nonlocal last_rs
            w_rs = [ev_exp]
            if kc == 0:
                w_rs.append(psStat_free)
            first = True
            for nts in NT:
                last_rs = mm(psStat[0:1, nts], ones_col[:, 0:1],
                             probsT[:, kc * 1024 + nts.start:kc * 1024 + nts.stop],
                             kc == 0, kc == 15, w_rs if first else ())
                first = False
        pend_b.append(emit_rs)
        if len(pend_b) > 1:
            pend_b.pop(0)()
    if tapon("t_probsT"):
        gdma(taps["t_probsT"][:], probsT[:], "tap", (ev_exp,))

    while pend_b:
        pend_b.pop(0)()
    ev_rs = vop(lambda e: e.reciprocal(st_a[:], psStat[0:1, :]), (last_rs,))
    psStat_free = ev_rs
    ev_rs_bf = vop(lambda e: e.tensor_copy(bf_a[:], st_a[:]), (ev_rs, ev_rs))
    ev_bc = bcast_single(bf_a, psA, (ev_rs_bf, psAB_free[0]))
    ev_rsbc = vop(lambda e: e.tensor_copy(rb_bf[:], psA[:]), (ev_bc,))
    psAB_free[0] = ev_rsbc

    wob_free = [None, None]
    ev_wo_pre = {}
    for i in (0, 1):
        ev_wo_pre[i] = gdma(wob[i][:], wo_t[i], "wo%d" % i, ())

    ev_ao = None
    for dc in range(8):
        par = dc % 2
        ps = psAB[par]
        w0 = [ev_vtm, ev_exp, psAB_free[par]]
        last = None
        for kc in range(16):
            lhs = v_tm[:, kc * 1024 + dc * 128:kc * 1024 + dc * 128 + 128]
            for nts in NT:
                last = mm(ps[:, nts], lhs, probsT[:, kc * 1024 + nts.start:kc * 1024 + nts.stop],
                          kc == 0, kc == 15, w0 if last is None else ())
        ev_ao = vop(lambda e, par=par, dc=dc: e.tensor_tensor(
            attnout[:, dc * 1024:(dc + 1) * 1024], psAB[par][:], rb_bf[:], ALU.mult),
            (last, ev_rsbc))
        psAB_free[par] = ev_ao
    if tapon("t_ao"):
        gdma(taps["t_ao"][:], attnout[:], "tap", (ev_ao,))

    ckpt("B")
    ev_zev = {}
    stat_mm_c = {}
    ev_sbf = {}
    pend_c = []
    for fc in range(32):
        par = fc % 2
        if fc < 2:
            ev_w = ev_wo_pre[fc]
        else:
            ev_w = gdma(wob[par][:], wo_t[fc], "wo%d" % par, (wob_free[par],))
        ps = psAB[par]
        w0 = [ev_w, ev_ao, psAB_free[par]]
        last = None
        for dc in range(8):
            lhs = wob[par][:, dc * 128:(dc + 1) * 128]
            for nts in NT:
                last = mm(ps[:, nts], lhs, attnout[:, dc * 1024 + nts.start:dc * 1024 + nts.stop],
                          dc == 0, dc == 7, w0 if last is None else ())
        wob_free[par] = last
        while len(pend_c) > 1:
            pend_c.pop(0)()
        if "evac" not in CSKIP:
            w_z = [last, ev_bo]
            if DEBUG_TAPS and fc == 0:
                w_z.append(sch.at("tap"))
            ev_zev[fc] = vop(lambda e, par=par, fc=fc: e.tensor_scalar(
                z[:, fc * 1024:(fc + 1) * 1024], psAB[par][:], bo_sb[:, fc:fc + 1], None, ALU.add),
                w_z)
        else:
            ev_zev[fc] = vop(lambda e, par=par, fc=fc: e.tensor_copy(
                z[:, fc * 1024:(fc + 1) * 1024], psAB[par][:]), (last,))
        sq_ = sqc_buf if par == 0 else sqc2
        zs_c = z[:, fc * 1024:(fc + 1) * 1024]
        if "act" not in CSKIP:
            w_s = [ev_zev[fc]]
            if fc >= 2:
                w_s.append(stat_mm_c[fc - 2])
            ev_sbf[fc] = sop(lambda e, sq_=sq_, zs_c=zs_c: e.activation(
                sq_[:], zs_c, AF.Square), w_s)
        else:
            ev_sbf[fc] = ev_zev[fc]
        psAB_free[par] = ev_zev[fc]
        if "stats" not in CSKIP and "act" not in CSKIP:
            def emit_stats_c(fc=fc, sq_=sq_, zs_c=zs_c):
                w_st = [ev_sbf[fc]]
                if fc == 0:
                    w_st.append(psStat_free)
                first = True
                for nts in NT:
                    mm(psStat[0:1, nts], ones_col[:, 0:1], zs_c[:, nts], fc == 0, fc == 31,
                       w_st if first else ())
                    first = False
                    stat_mm_c[fc] = mm(psStat[32:33, nts], ones_col[:, 0:1], sq_[:, nts],
                                       fc == 0, fc == 31)
            pend_c.append(emit_stats_c)
        else:
            stat_mm_c[fc] = ev_zev[fc]
    if DEBUG_TAPS and TRUNC == "C1":
        gdma(taps["t_y"][:], z[:], "tap", (ev_zev[31],))
    ckpt("C1")
    for p_ in pend_c:
        p_()
    pend_c = []
    ev_sr, ev_fin = ln_finalize(stat_mm_c[31], st_a, st_b, bf_a, bf_b, st_c,
                                extra=(sch.at("tap"),) if DEBUG_TAPS else ())
    psStat_free = ev_sr
    ev_b1 = bcast_single(bf_b, psA, (ev_fin, psAB_free[0]))
    ev_b2 = bcast_single(bf_a, psB, (psAB_free[1],))
    ev_rbf = vop(lambda e: e.tensor_copy(rb_bf[:], psA[:]), (ev_b1,))
    ev_nbf = vop(lambda e: e.tensor_copy(nb_bf[:], psB[:]), (ev_b2,))
    psAB_free = [ev_rbf, ev_nbf]
    ckpt("C2")
    def eop(fc3, fn, waits=(), mod=3):
        if fc3 % mod == mod - 1:
            return E("gpsimd", fn, waits, inc="gs")
        return E("vector", fn, waits, inc="vs")

    xt_free = [None, None, None, None]
    ev_y = {}
    ev_dcast = {}
    stat_mm_d = {}
    for fc in range(32):
        xb = fc % 4
        w_x = [xt_free[xb]] if fc >= 4 else [ev_fin, ev_rbf, ev_nbf]
        ev_x = sdma(xtile[xb][:], xbf[fc // 8][:, (fc % 8) * 1024:(fc % 8) * 1024 + 1024],
                    "xt%d" % xb, w_x)
        zs = z[:, fc * 1024:(fc + 1) * 1024]
        w1 = [ev_rbf, ev_nbf, ev_zev[fc], stat_mm_c[31]]
        if DEBUG_TAPS and fc == 0:
            w1.append(sch.at("tap"))
        eop(fc, lambda e, zs=zs: e.tensor_tensor(zs, zs, rb_bf[:], ALU.mult), w1)
        eop(fc, lambda e, zs=zs: e.tensor_tensor(zs, zs, nb_bf[:], ALU.add))
        ev_y[fc] = eop(fc, lambda e, zs=zs, xb=xb: e.tensor_tensor(
            zs, zs, xtile[xb][:], ALU.add), (ev_x,))
        xt_free[xb] = ev_y[fc]
        w_c = [ev_y[fc]]
        if fc >= 1:
            w_c.append(stat_mm_d[fc - 1])
        ev_dcast[fc] = sop(lambda e, zs=zs: e.activation(sqc_buf[:], zs, AF.Square), w_c)
        w_st = [ev_dcast[fc]]
        if fc == 0:
            w_st.append(psStat_free)
        first = True
        for nts in NT:
            mm(psStat[0:1, nts], ones_col[:, 0:1], zs[:, nts], fc == 0, fc == 31,
               w_st if first else ())
            first = False
            stat_mm_d[fc] = mm(psStat[32:33, nts], ones_col[:, 0:1], sqc_buf[:, nts],
                               fc == 0, fc == 31)
    if tapon("t_y"):
        gdma(taps["t_y"][:], z[:], "tap", (ev_y[31],))

    ckpt("C")
    ev_sr, ev_fin = ln_finalize(stat_mm_d[31], st_a, st_b, bf_a, bf_b, st_c)
    psStat_free = ev_sr
    ev_b1 = bcast_single(bf_b, psA, (ev_fin, psAB_free[0]))
    ev_b2 = bcast_single(bf_a, psB, (psAB_free[1],))
    ev_rbc = vop(lambda e: e.tensor_copy(rb_bf[:], psA[:]), (ev_b1,))
    ev_nbc = vop(lambda e: e.tensor_copy(nb_bf[:], psB[:]), (ev_b2,))
    psAB_free = [ev_rbc, ev_nbc]

    ev_gw_pre = {}
    ev_uw_pre = {}
    for hc in (0, 1):
        par = hc % 2
        w_g = [wob_free[0], wob_free[1]]
        if DEBUG_TAPS:
            w_g.append(sch.at("tap"))
        ev_gw_pre[hc] = gdma(gwb[par][:], gup_g[hc], "gw%d" % par, w_g)
    E("gpsimd", lambda e: e.memset(tmpg_pad[:], 0.0), (ev_fin, ev_rbc, ev_nbc), inc="gs")
    E("gpsimd", lambda e: e.memset(tmpu_pad[:], 0.0), inc="gs")
    ev_pads = E("gpsimd", lambda e: e.memset(tmpd_pad[:], 0.0), inc="gs")

    ev_n3_v = ev_n3_g = None
    ev_e0 = None
    ev_l1 = {}
    w_e0 = [ev_gaup, ev_pads, psAB_free[0]]
    for fc in range(32):
        cs = slice(fc * 1024, (fc + 1) * 1024)
        ev_l1[fc] = eop(fc, lambda e, cs=cs: e.tensor_tensor(
            ln3[:, cs], z[:, cs], rb_bf[:], ALU.mult), (ev_rbc, ev_nbc), mod=5)
        ev_nn = eop(fc, lambda e, cs=cs: e.tensor_tensor(
            ln3[:, cs], ln3[:, cs], nb_bf[:], ALU.add), mod=5)
        if fc % 5 == 4:
            ev_n3_g = ev_nn
        else:
            ev_n3_v = ev_nn
        lhs = gaup_sb[:, fc * 48:(fc + 1) * 48]
        for nts in NT:
            ev_e0 = mm(psA[0:48, nts], lhs, ln3[:, fc * 1024 + nts.start:fc * 1024 + nts.stop],
                       fc == 0, fc == DC - 1, [ev_nn] + (w_e0 if fc == 0 else []))
    ev_n3 = [ev_n3_v, ev_n3_g]
    yp_ev = []
    for g in range(4):
        w = [ev_l1[g * 8 + i] for i in range(8)] + [stat_mm_d[31]]
        if DEBUG_TAPS and g == 0:
            w.append(sch.at("tap"))
        yp_ev.append(sdma(ypark[g], z[:, g * 8192:(g + 1) * 8192], "yp%d" % g, w))
    yp_all = [sch.at("yp%d" % g) for g in range(4)]
    for hc in (0, 1):
        ev_uw_pre[hc] = gdma(uwb[hc % 2][:], gup_u[hc], "uw%d" % (hc % 2),
                             (wob_free[0], wob_free[1]))
    if tapon("t_ln3"):
        gdma(taps["t_ln3"][:], ln3[:], "tap", (ev_n3,))

    ckpt("D")
    ev_tg = vop(lambda e: e.tensor_copy(tmpg_pad[0:16, :], psA[0:16, :]), (ev_e0,))
    ev_tu = vop(lambda e: e.tensor_copy(tmpu_pad[32:48, :], psA[32:48, :]))
    psAB_free[0] = ev_tu

    gw_free = [None, None]
    uw_free = [None, None]
    dw_free = [None, None]
    acct_free = [None, None]
    psA_free = ev_tu
    psB_free = psAB_free[1]
    silu_free = None
    ev_tdp = None
    ev_park = {}
    ev_dpark = {}
    stat_mm_e = {}
    pend_e = []
    last_acc_ld = [None, None]
    ev_hid_tap = None
    last_down_mm = None

    HPH = ((0, 29), (29, 29), (58, 28))
    ev_prev_down = None
    for pi, (h0, cnt) in enumerate(HPH):
        last_ph = (pi == 2)
        ev_mul = None
        for j in range(cnt):
            hc = h0 + j
            par = hc % 2
            if hc in ev_gw_pre:
                ev_gw = ev_gw_pre[hc]
                ev_uw = ev_uw_pre[hc]
            else:
                ev_gw = gdma(gwb[par][:], gup_g[hc], "gw%d" % par, (gw_free[par],))
                ev_uw = gdma(uwb[par][:], gup_u[hc], "uw%d" % par,
                             (uw_free[par], dw_free[par]))
            w0 = [ev_gw, psA_free, ev_tg, ev_n3]
            last_g = None
            for c in range(33):
                lhs = gwb[par][:, c * 128:(c + 1) * 128]
                for nts in NT:
                    rhs = (ln3[:, c * 1024 + nts.start:c * 1024 + nts.stop]
                           if c < DC else tmpg_pad[:, nts])
                    last_g = mm(psA[:, nts], lhs, rhs, c == 0, c == 32,
                                w0 if last_g is None else ())
            gw_free[par] = last_g
            ev_sl = sop(lambda e: e.activation(silu_buf[:], psA[:], AF.Silu),
                        (last_g, silu_free))
            psA_free = ev_sl
            w0 = [ev_uw, psB_free, ev_tu]
            last_u = None
            for c in range(33):
                lhs = uwb[par][:, c * 128:(c + 1) * 128]
                for nts in NT:
                    rhs = (ln3[:, c * 1024 + nts.start:c * 1024 + nts.stop]
                           if c < DC else tmpu_pad[:, nts])
                    last_u = mm(psB[:, nts], lhs, rhs, c == 0, c == 32,
                                w0 if last_u is None else ())
            uw_free[par] = last_u
            w_m = [last_u, ev_sl]
            if j == 0:
                w_m += [ev_prev_down] + yp_all
                if DEBUG_TAPS:
                    w_m.append(sch.at("tap"))
            ev_mul = vop(lambda e, j=j: e.tensor_tensor(
                hidden[:, j * 1024:(j + 1) * 1024], silu_buf[:], psB[:], ALU.mult), w_m)
            psB_free = ev_mul
            silu_free = ev_mul
        if tapon("t_hid") and pi == 0:
            ev_hid_tap = gdma(taps["t_hid"][:, 0:cnt * 1024], hidden[:, 0:cnt * 1024],
                              "tap", (ev_mul,))
        if last_ph:
            for i in range(11):
                cdma(ypre[i][:], ypark[i // 8][:, (i % 8) * 1024:(i % 8) * 1024 + 1024],
                     "pp", [gw_free[0], gw_free[1], uw_free[0], uw_free[1]] + yp_all)
        w0 = [ev_mul, psA_free, ev_da]
        last_td = None
        for j in range(cnt):
            hc = h0 + j
            lhs = da_sb[:, hc * 16:(hc + 1) * 16]
            for nts in NT:
                last_td = mm(psA[0:16, nts], lhs, hidden[:, j * 1024 + nts.start:j * 1024 + nts.stop],
                             j == 0, j == cnt - 1, w0 if last_td is None else ())
        if pi == 0:
            ev_td = vop(lambda e: e.tensor_copy(tmpd_h1[:], psA[0:16, :]), (last_td,))
        else:
            ev_td = vop(lambda e: e.tensor_add(tmpd_h1[:], tmpd_h1[:], psA[0:16, :]),
                        (last_td,))
            if last_ph:
                ev_tdp = vop(lambda e: e.tensor_copy(tmpd_pad[0:16, :], tmpd_h1[:]))
                ev_td = ev_tdp
        psA_free = ev_td

        ev_gu_done = (gw_free[0], gw_free[1], uw_free[0], uw_free[1])
        ap_prev = sch.at("ap")
        nch = cnt + (1 if last_ph else 0)
        col0 = h0 * 128
        psd_free = [psA_free, psB_free]
        for fc in range(32):
            par = fc % 2
            ev_dw = gdma(dwb[par][:, 0:nch * 128], dwn_t[fc][:, col0:col0 + nch * 128],
                         "dw%d" % par, (dw_free[par], uw_free[par]))
            ps = psAB[par]
            w0 = [ev_dw, ev_mul, psd_free[par]]
            if last_ph:
                w0.append(ev_tdp)
            last = None
            for j in range(nch):
                lhs = dwb[par][:, j * 128:(j + 1) * 128]
                for nts in NT:
                    rhs = (hidden[:, j * 1024 + nts.start:j * 1024 + nts.stop]
                           if j < cnt else tmpd_pad[:, nts])
                    last = mm(ps[:, nts], lhs, rhs, j == 0, j == nch - 1,
                              w0 if last is None else ())
            dw_free[par] = last
            last_down_mm = last
            while pend_e:
                pend_e.pop(0)()
            if pi == 0:
                ev_cp = vop(lambda e, par=par: e.tensor_copy(acct[par][:], psAB[par][:]),
                            (last, acct_free[par]))
                ev_park[fc] = sdma(accd[fc], acct[par][:], "ap", (ev_cp,))
                acct_free[par] = ev_park[fc]
                psd_free[par] = ev_cp
            elif pi == 1:
                ev_ld = sdma(acct[par][:], accd[fc], "ac%d" % par,
                             (acct_free[par], ap_prev))
                ev_add = vop(lambda e, par=par: e.tensor_tensor(
                    acct[par][:], psAB[par][:], acct[par][:], ALU.add), (last, ev_ld))
                psd_free[par] = ev_add
                ev_park[fc] = sdma(accd[fc], acct[par][:], "ap", (ev_add,))
                acct_free[par] = ev_park[fc]
            else:
                ev_ld = sdma(acct[par][:], accd[fc], "ac%d" % par,
                             (acct_free[par], ap_prev))
                last_acc_ld[par] = ev_ld
                w_add = [last, ev_ld]
                if fc == 0:
                    w_add.append(ev_gu_done)
                ds = dout_sb[:, fc * 1024:(fc + 1) * 1024]
                ev_add = vop(lambda e, par=par, ds=ds: e.tensor_tensor(
                    ds, psAB[par][:], acct[par][:], ALU.add), w_add)
                psd_free[par] = ev_add
                acct_free[par] = ev_add
                w_s = [ev_add]
                if fc >= 1:
                    w_s.append(stat_mm_e[fc - 1])
                ev_sq = sop(lambda e, ds=ds: e.activation(sqc_buf[:], ds, AF.Square), w_s)
                def emit_stats_e(fc=fc, ev_sq=ev_sq, ds=ds):
                    w_st = [ev_sq]
                    if fc == 0:
                        w_st.append(psStat_free)
                    first = True
                    for nts in NT:
                        mm(psStat[0:1, nts], ones_col[:, 0:1],
                           ds[:, nts], fc == 0, fc == 31, w_st if first else ())
                        first = False
                        stat_mm_e[fc] = mm(psStat[32:33, nts], ones_col[:, 0:1], sqc_buf[:, nts],
                                           fc == 0, fc == 31)
                pend_e.append(emit_stats_e)
        while pend_e:
            pend_e.pop(0)()
        ev_prev_down = last_down_mm
        psA_free, psB_free = psd_free[0], psd_free[1]
        psAB_free = [psA_free, psB_free]

    ckpt("E")
    ev_sr, ev_fin = ln_finalize(stat_mm_e[31], est_a, est_b, bf_a, bf_b, est_c,
                                extra=(last_acc_ld[0], last_acc_ld[1]))
    psStat_free = ev_sr
    ev_b1 = bcast_single(bf_b, psA, (ev_fin, psAB_free[0]))
    ev_b2 = bcast_single(bf_a, psB, (psAB_free[1],))
    ev_rv = vop(lambda e: e.tensor_copy(erb_v[:], psA[:]), (ev_b1,))
    ev_rp = vop(lambda e: e.tensor_copy(erb_p[:], psA[:]), (ev_b1,))
    ev_nv = vop(lambda e: e.tensor_copy(enb_v[:], psB[:]), (ev_b2,))
    ev_np = vop(lambda e: e.tensor_copy(enb_p[:], psB[:]), (ev_b2,))
    ckpt("E2")

    pp_all = sch.at("pp")
    yt_free = [None] * 4
    ot_free = [None] * 4
    POOL_FC = frozenset((1, 3, 5, 8, 10, 12, 15, 17, 19, 22, 24, 26, 29, 31))
    ev_out = None
    for fc in range(32):
        b = fc % 4
        if fc < 11:
            ybuf, ev_yl = ypre[fc], pp_all
        else:
            ybuf = ytile[b]
            w_y = [yt_free[b]] if fc >= 15 else [last_down_mm]
            ev_yl = cdma(ybuf[:], ypark[fc // 8][:, (fc % 8) * 1024:(fc % 8) * 1024 + 1024],
                         "yt%d" % b, w_y)
        ds = dout_sb[:, fc * 1024:(fc + 1) * 1024]
        ot = otile[b]
        if fc in POOL_FC and TAIL_POOL:
            eng, esem, erbX, enbX, tb = "gpsimd", "gs", erb_p, enb_p, tbp
            w1 = [ev_rp, ev_np]
        else:
            eng, esem, erbX, enbX, tb = "vector", "vs", erb_v, enb_v, tbv
            w1 = [ev_rv, ev_nv]
        E(eng, lambda e, ds=ds, tb=tb, erbX=erbX: e.tensor_tensor(
            tb[:], ds, erbX[:], ALU.mult), w1, inc=esem)
        E(eng, lambda e, tb=tb, enbX=enbX: e.tensor_tensor(
            tb[:], tb[:], enbX[:], ALU.add), inc=esem)
        ev_o = E(eng, lambda e, ot=ot, tb=tb, ybuf=ybuf: e.tensor_tensor(
            ot[:], tb[:], ybuf[:], ALU.add), (ev_yl, ot_free[b]), inc=esem)
        if fc % 2 == 0:
            ev_out = sdma(o[fc * 128:(fc + 1) * 128, :], ot[:], "ow%d" % (b // 2), (ev_o,))
        elif TAIL_OG:
            ev_out = gdma(o[fc * 128:(fc + 1) * 128, :], ot[:], "og%d" % (b // 2), (ev_o,))
        else:
            ev_out = sdma(o[fc * 128:(fc + 1) * 128, :], ot[:], "ow%d" % (b // 2), (ev_o,))
        yt_free[b] = ev_o
        ot_free[b] = ev_out
    stopped[0] = False
    E("sync", None, tuple(sch.at(s) for s in SD_SEMS))
    E("gpsimd", None, tuple(sch.at(s) for s in GD_SEMS))
    E("scalar", None, tuple(sch.at(s) for s in SC_SEMS))

    es = contextlib.ExitStack()
    with es:
        sems = {}
        for s in ("ts", "vs", "ss", "gs") + GD_SEMS + SD_SEMS + SC_SEMS:
            sems[s] = es.enter_context(nc.semaphore(s))
        block = es.enter_context(nc.Block())

        def runner(name):
            def run(e):
                for waits, fn, inc, n in sch.ops[name]:
                    for s, v in waits:
                        e.wait_ge(sems[s], v)
                    ins = fn(e) if fn is not None else None
                    if inc is not None:
                        ins.then_inc(sems[inc], n)
            return run

        block.tensor(runner("tensor"))
        block.vector(runner("vector"))
        block.scalar(runner("scalar"))
        block.gpsimd(runner("gpsimd"))
        block.sync(runner("sync"))
    return nc


def _prep_shared(inputs):
    f = np.float32
    Wqkv = np.asarray(inputs["L0_attn_Wqkv"], f)
    Wo = np.asarray(inputs["L0_attn_Wo"], f)
    gate_w = np.asarray(inputs["L0_ffn_gate_w"], f)
    up_w = np.asarray(inputs["L0_ffn_up_w"], f)
    down_w = np.asarray(inputs["L0_ffn_down_w"], f)
    gate_a = np.asarray(inputs["L0_ffn_gate_a"], f)
    gate_b = np.asarray(inputs["L0_ffn_gate_b"], f)
    up_a = np.asarray(inputs["L0_ffn_up_a"], f)
    up_b = np.asarray(inputs["L0_ffn_up_b"], f)
    down_a = np.asarray(inputs["L0_ffn_down_a"], f)
    down_b = np.asarray(inputs["L0_ffn_down_b"], f)
    bqkv = np.asarray(inputs["L0_attn_bqkv"], f)
    bo = np.asarray(inputs["L0_attn_bo"], f)

    d = {}
    d["wqkv_t"] = np.ascontiguousarray(
        Wqkv.reshape(32, 128, 24, 128).transpose(2, 1, 0, 3).reshape(24, 128, 4096))
    d["wo_t"] = np.ascontiguousarray(
        Wo.reshape(8, 128, 32, 128).transpose(2, 1, 0, 3).reshape(32, 128, 1024))

    aug_g = np.zeros((4224, DH), f)
    aug_g[:4096] = gate_w.T
    aug_g[4096:4112] = gate_b.T
    d["gup_g"] = np.ascontiguousarray(
        aug_g.reshape(33, 128, 86, 128).transpose(2, 1, 0, 3).reshape(86, 128, 4224))
    aug_u = np.zeros((4224, DH), f)
    aug_u[:4096] = up_w.T
    aug_u[4128:4144] = up_b.T
    d["gup_u"] = np.ascontiguousarray(
        aug_u.reshape(33, 128, 86, 128).transpose(2, 1, 0, 3).reshape(86, 128, 4224))

    aug_d = np.zeros((11136, 4096), f)
    aug_d[:DH] = down_w.T
    aug_d[DH:DH + 16] = down_b.T
    d["dwn_t"] = np.ascontiguousarray(
        aug_d.reshape(87, 128, 32, 128).transpose(2, 1, 0, 3).reshape(32, 128, 11136))

    A = np.zeros((4096, 48), f)
    A[:, 0:16] = gate_a.T
    A[:, 32:48] = up_a.T
    d["gaup_h"] = np.ascontiguousarray(
        A.reshape(32, 128, 48).transpose(1, 0, 2).reshape(128, 1536))
    d["da_h"] = np.ascontiguousarray(
        down_a.T.reshape(86, 128, 16).transpose(1, 0, 2).reshape(128, 1376))

    d["bqkv_h"] = np.ascontiguousarray(bqkv.reshape(24, 128).T)
    d["bo_h"] = np.ascontiguousarray(bo.reshape(32, 128).T)
    d["ident_h"] = np.eye(128, dtype=f)
    return d


def _tile_x(xT_half):
    return np.ascontiguousarray(
        xT_half.reshape(4, 8, 128, 1024).transpose(0, 2, 1, 3).reshape(4, 128, 8192))


def kernel(**inputs):
    shared = _prep_shared(inputs)
    x = np.asarray(inputs["L0_attn_x"], np.float32)
    in_maps = []
    for c in range(8):
        b, half = c // 2, c % 2
        xT = x[b].T
        m = dict(shared)
        m["xo_t"] = _tile_x(np.ascontiguousarray(xT[:, half * T:(half + 1) * T]))
        m["xp_t"] = _tile_x(np.ascontiguousarray(xT[:, (1 - half) * T:(2 - half) * T]))
        in_maps.append(m)
    nc = build_program()
    res = run_bass_kernel_spmd(nc, in_maps, core_ids=list(range(8)))
    out = np.empty((4, 2048, 4096), np.float32)
    for c in range(8):
        b, half = c // 2, c % 2
        out[b, half * T:(half + 1) * T, :] = np.asarray(res.results[c]["o"]).T
    return out


# version 20
# speedup vs baseline: 1.0067x; 1.0001x over previous
import contextlib
import numpy as np
import concourse.bass as bass
import concourse.mybir as mybir
from concourse.bass_utils import run_bass_kernel_spmd

F32 = mybir.dt.float32
BF16 = mybir.dt.bfloat16
AF = mybir.ActivationFunctionType
ALU = mybir.AluOpType
AX = mybir.AxisListType

D = 4096
DC = 32
T = 1024
S = 2048
DH = 11008
HCN = 86
HH = 43
EPS = 1e-5
SM_SCALE = 0.03125
INV_D = 1.0 / 4096.0

NT = (slice(0, 512), slice(512, 1024))
ENGINES = ("tensor", "vector", "scalar", "gpsimd", "sync")

DEBUG_TAPS = False
TAP_KEYS = None
TRUNC = "full"
CSKIP = ()
TAIL_POOL = True
TAIL_OG = True


class _Stop(Exception):
    pass

PERS = 17024
BG = PERS + 1280
R56 = BG + 147456
R7 = R56 + 32768
ARENA_END = R7 + 30464
assert ARENA_END <= 229344 - 128, ARENA_END


class Sched:
    def __init__(self):
        self.ops = {e: [] for e in ENGINES}
        self.cnt = {}
        self.last_wait = {e: {} for e in ENGINES}

    def emit(self, eng, fn, waits=(), inc=None, n=1):
        w = {}
        def add(ev):
            if ev is None:
                return
            if isinstance(ev, (list, tuple)) and len(ev) == 2 \
                    and isinstance(ev[0], str) and isinstance(ev[1], int):
                s, v = ev
                if v > self.last_wait[eng].get(s, 0):
                    w[s] = max(w.get(s, 0), v)
                return
            for sub in ev:
                add(sub)
        add(list(waits))
        for s, v in w.items():
            self.last_wait[eng][s] = v
        ev_out = None
        if inc is not None:
            self.cnt[inc] = self.cnt.get(inc, 0) + n
            ev_out = (inc, self.cnt[inc])
        self.ops[eng].append((tuple(w.items()), fn, inc, n))
        return ev_out

    def at(self, sem):
        return (sem, self.cnt.get(sem, 0))


def build_program():
    nc = bass.Bass("TRN2", target_bir_lowering=False)

    def inp(name, shape):
        return nc.dram_tensor(name, list(shape), F32, kind="ExternalInput").ap()

    xo_t = inp("xo_t", (4, 128, 8192))
    xp_t = inp("xp_t", (4, 128, 8192))
    wqkv_t = inp("wqkv_t", (24, 128, 4096))
    wo_t = inp("wo_t", (32, 128, 1024))
    gup_g = inp("gup_g", (HCN, 128, 4224))
    gup_u = inp("gup_u", (HCN, 128, 4224))
    dwn_t = inp("dwn_t", (32, 128, 11136))
    gaup_h = inp("gaup_h", (128, 1536))
    da_h = inp("da_h", (128, 1376))
    bqkv_h = inp("bqkv_h", (128, 24))
    bo_h = inp("bo_h", (128, 32))
    ident_h = inp("ident_h", (128, 128))

    o = nc.dram_tensor("o", [D, T], F32, kind="ExternalOutput").ap()
    ypark = nc.dram_tensor("ypark", [4, 128, 8192], BF16, kind="Internal").ap()
    xbf = nc.dram_tensor("xbf", [4, 128, 8192], BF16, kind="Internal").ap()
    accd = nc.dram_tensor("accd", [32, 128, 1024], F32, kind="Internal").ap()
    doutd = nc.dram_tensor("doutd", [32, 128, 1024], F32, kind="Internal").ap()

    taps = {}
    def tapon(nm):
        return DEBUG_TAPS and (TAP_KEYS is None or nm in TAP_KEYS)
    if DEBUG_TAPS:
        for nm, shape in (
            ("t_ln1", (128, 32768)), ("t_q", (128, 8192)), ("t_k", (128, 16384)),
            ("t_vtm", (128, 16384)), ("t_probsT", (128, 16384)),
            ("t_negmaxT", (1, 1024)), ("t_ao", (128, 8192)),
            ("t_y", (128, 32768)), ("t_ln3", (128, 32768)), ("t_hid", (128, 44032)),
        ):
            taps[nm] = nc.dram_tensor(nm, list(shape), F32, kind="ExternalOutput").ap()

    def sb(name, off, shape, dtype):
        return nc.alloc_sbuf_tensor_at(name, list(shape), dtype, offset=off).ap()

    ident_bf = sb("ident_bf", PERS + 0, (128, 128), BF16)
    ones_row = sb("ones_row", PERS + 256, (1, 128), BF16)
    ones_col = sb("ones_col", PERS + 512, (128, 1), BF16)
    eps_sb = sb("eps_sb", PERS + 576, (1, 1), F32)
    bqkv_sb = sb("bqkv_sb", PERS + 640, (128, 24), F32)
    bo_sb = sb("bo_sb", PERS + 768, (128, 32), F32)
    nm1 = [sb("nm1a", PERS + 896, (128, 1), F32),
           sb("nm1b", PERS + 928, (128, 1), F32)]
    nm2 = [sb("nm2a", PERS + 960, (128, 1), F32),
           sb("nm2b", PERS + 992, (128, 1), F32)]
    nmbf = [sb("nmbfa", PERS + 1024, (128, 1), BF16),
            sb("nmbfb", PERS + 1056, (128, 1), BF16)]

    q_fm = sb("q_fm", BG + 0, (128, 8192), BF16)
    k_fm = sb("k_fm", BG + 16384, (128, 16384), BF16)
    v_fm = sb("v_fm", BG + 49152, (128, 16384), BF16)
    xh = sb("xh", BG + 81920, (128, 32768), BF16)
    v_tm = sb("v_tm", BG + 81920, (128, 16384), BF16)
    probsT = sb("probsT", BG + 114688, (128, 16384), BF16)
    z = sb("z", BG + 0, (128, 32768), BF16)
    ln3 = sb("ln3", BG + 65536, (128, 32768), BF16)
    hidden = sb("hidden", BG + 0, (128, 29696), BF16)
    uwb = [sb("uwb0", R56 + 16896, (128, 4224), BF16),
           sb("uwb1", BG + 131072, (128, 4224), BF16)]
    dwb = [sb("dwb0", R56 + 16896, (128, 3712), BF16),
           sb("dwb1", BG + 131072, (128, 3712), BF16)]
    ytile = [sb("ytile%d" % i, BG + 2048 * i, (128, 1024), BF16)
             for i in range(4)]
    ypre = [sb("ypre%d" % i, R56 + 256 + 2048 * i, (128, 1024), BF16)
            for i in range(8)] + \
           [sb("ypre%d" % (8 + i), BG + 59392 + 2048 * i, (128, 1024), BF16)
            for i in range(3)]
    otile = [sb("otile%d" % i, BG + 131072 + 4096 * i, (128, 1024), F32)
             for i in range(4)]

    wqb = [sb("wqb0", R56 + 0, (128, 4096), BF16),
           sb("wqb1", R56 + 8192, (128, 4096), BF16)]
    attnout = sb("attnout", R56 + 0, (128, 8192), BF16)
    wob = [sb("wob0", R56 + 16384, (128, 1024), BF16),
           sb("wob1", R56 + 18432, (128, 1024), BF16)]
    xtile = [sb("xtile%d" % i, R7 + 4096 + 2048 * i, (128, 1024), BF16)
             for i in range(4)]
    zbf_buf = sb("zbf_buf", R56 + 25344, (128, 1024), BF16)
    sqc_buf = sb("sqc_buf", R56 + 27392, (128, 1024), BF16)
    lo1_bf = sb("lo1_bf", R56 + 25344, (1, 1024), BF16)
    lo2_bf = sb("lo2_bf", R56 + 27392, (1, 1024), BF16)
    gwb = [sb("gwb0", R56 + 0, (128, 4224), BF16),
           sb("gwb1", R56 + 8448, (128, 4224), BF16)]
    dout_sb = sb("dout_sb", BG + 65536, (128, 32768), BF16)

    sqb = [sb("sqb0", R7 + 0, (128, 1024), BF16),
           sb("sqb1", R7 + 2048, (128, 1024), BF16)]
    st_a = sb("st_a", R7 + 4096, (1, 1024), F32)
    st_b = sb("st_b", R7 + 8192, (1, 1024), F32)
    bf_a = sb("bf_a", R7 + 12288, (1, 1024), BF16)
    bf_b = sb("bf_b", R7 + 14336, (1, 1024), BF16)
    rb_bf = sb("rb_bf", R7 + 16384, (128, 1024), BF16)
    nb_bf = sb("nb_bf", R7 + 18432, (128, 1024), BF16)
    negmaxT = sb("negmaxT", R7 + 0, (1, 1024), BF16)
    zbf2 = sb("zbf2", R7 + 0, (128, 1024), BF16)
    sqc2 = sb("sqc2", R7 + 2048, (128, 1024), BF16)
    st_c = sb("st_c", R7 + 0, (1, 1024), F32)
    rb_f32 = sb("rb_f32", R7 + 16384, (128, 1024), F32)
    nb_f32 = sb("nb_f32", R7 + 20480, (128, 1024), F32)
    tmpg_pad = sb("tmpg_pad", R7 + 0, (128, 1024), BF16)
    tmpu_pad = sb("tmpu_pad", R7 + 2048, (128, 1024), BF16)
    tmpd_pad = sb("tmpd_pad", R7 + 4096, (128, 1024), BF16)
    silu_buf = sb("silu_buf", R7 + 6144, (128, 1024), BF16)
    acct = [sb("acct0", R7 + 8192, (128, 1024), F32),
            sb("acct1", R7 + 12288, (128, 1024), F32)]
    tmpd_h1 = sb("tmpd_h1", R7 + 16384, (16, 1024), F32)
    est_a = sb("est_a", R7 + 0, (1, 1024), F32)
    est_b = sb("est_b", R7 + 4096, (1, 1024), F32)
    est_c = sb("est_c", R7 + 8192, (1, 1024), F32)
    erb_v = sb("erb_v", R7 + 16384, (128, 1024), BF16)
    enb_v = sb("enb_v", R7 + 18432, (128, 1024), BF16)
    tbv = sb("tbv", R7 + 20480, (128, 1024), BF16)
    tbp = sb("tbp", R7 + 22528, (128, 1024), BF16)
    erb_p = sb("erb_p", BG + 8192, (128, 1024), BF16)
    enb_p = sb("enb_p", BG + 10240, (128, 1024), BF16)
    gaup_sb = sb("gaup_sb", R7 + 24576, (128, 1536), BF16)
    da_sb = sb("da_sb", R7 + 27648, (128, 1376), BF16)

    psA = nc.alloc_psum_tensor("psA", [128, 1024], F32).ap()
    psB = nc.alloc_psum_tensor("psB", [128, 1024], F32).ap()
    psStat = nc.alloc_psum_tensor("psStat", [33, 1024], F32).ap()
    psT = [nc.alloc_psum_tensor("psT1", [128, 1024], BF16).ap(),
           nc.alloc_psum_tensor("psT2", [128, 1024], BF16).ap()]
    psAB = (psA, psB)

    sch = Sched()

    def E(eng, fn, waits=(), inc=None, n=1):
        if stopped[0]:
            return None
        return sch.emit(eng, fn, waits, inc, n)

    stopped = [False]

    def ckpt(name):
        if TRUNC == name:
            stopped[0] = True

    GD_SEMS = ("su", "su2", "xa0", "xa1", "xa2", "xa3", "xa4", "xa5", "xa6", "xa7",
               "wq0", "wq1", "wo0", "wo1",
               "gw0", "gw1", "uw0", "uw1", "dw0", "dw1", "tap")
    SC_SEMS = ("yt0", "yt1", "xk0", "xk1", "xk2", "xk3",
               "xk4", "xk5", "xk6", "xk7",
               "oc0", "oc1",
               "pp")
    SD_SEMS = ("sb", "xt0", "xt1", "xt2", "xt3", "yp0", "yp1", "yp2", "yp3", "ap",
               "ac0", "ac1", "ow0", "ow1", "ys2", "ys3")

    def gdma(dst, src, sem, waits=()):
        assert sem in GD_SEMS, sem
        return E("gpsimd", lambda e, d=dst, s=src: e.dma_start(d, s), waits, sem, 16)

    def sdma(dst, src, sem, waits=()):
        assert sem in SD_SEMS, sem
        return E("sync", lambda e, d=dst, s=src: e.dma_start(d, s), waits, sem, 16)

    def cdma(dst, src, sem, waits=()):
        assert sem in SC_SEMS, sem
        return E("scalar", lambda e, d=dst, s=src: e.dma_start(d, s), waits, sem, 16)

    def mm(out_ap, lhs_ap, rhs_ap, start, stop, waits=()):
        return E("tensor",
                 lambda e, o=out_ap, l=lhs_ap, r=rhs_ap, st=start, sp=stop:
                 e.matmul(o, l, r, start=st, stop=sp), waits, "ts")

    def vop(fn, waits=()):
        return E("vector", fn, waits, "vs")

    def sop(fn, waits=()):
        return E("scalar", fn, waits, "ss")

    E("gpsimd", lambda e: e.memset(ones_row[:], 1.0), inc="gs")
    E("gpsimd", lambda e: e.memset(ones_col[:], 1.0), inc="gs")
    ev_memset = E("gpsimd", lambda e: e.memset(eps_sb[:], EPS), inc="gs")
    ev_ident = gdma(ident_bf[:], ident_h[:], "su")
    sdma(bqkv_sb[:], bqkv_h[:], "sb")
    sdma(bo_sb[:], bo_h[:], "sb")
    ev_bq = ev_bo = sch.at("sb")

    psAB_free = [None, None]
    psStat_free = None

    def ln_finalize(wait_ev, a_st, b_st, a_bf, b_bf, scr, extra=()):
        E("vector", lambda e: e.tensor_scalar(a_st[:], psStat[0:1, :], INV_D, None, ALU.mult),
          (wait_ev, ev_memset) + tuple(extra), inc="vs")
        ev_sr = E("vector", lambda e: e.tensor_scalar(b_st[:], psStat[32:33, :], INV_D, None, ALU.mult),
                  inc="vs")
        E("vector", lambda e: e.scalar_tensor_tensor(scr[:], a_st[:], -1.0, a_st[:],
                                                     ALU.mult, ALU.mult), inc="vs")
        ev_var = E("vector", lambda e: e.tensor_add(b_st[:], b_st[:], scr[:]), inc="vs")
        E("scalar", lambda e: e.activation(b_st[:], b_st[:], AF.Ln, bias=eps_sb[0:1, 0:1]),
          (ev_var,), inc="ss")
        ev_rstd = E("scalar", lambda e: e.activation(b_st[:], b_st[:], AF.Exp, scale=-0.5),
                    inc="ss")
        E("vector", lambda e: e.scalar_tensor_tensor(a_st[:], a_st[:], -1.0, b_st[:],
                                                     ALU.mult, ALU.mult), (ev_rstd,), inc="vs")
        E("vector", lambda e: e.tensor_copy(a_bf[:], a_st[:]), inc="vs")
        ev_done = E("vector", lambda e: e.tensor_copy(b_bf[:], b_st[:]), inc="vs")
        return ev_sr, ev_done

    def bcast_single(bf_ap, ps, waits):
        mm(ps[:, 0:512], ones_row[0:1, :], bf_ap[0:1, 0:512], True, True, waits)
        return mm(ps[:, 512:1024], ones_row[0:1, :], bf_ap[0:1, 512:1024], True, True)

    def bcast_double(hi_ap, lo_ap, ps, waits):
        mm(ps[:, 0:512], ones_row[0:1, :], hi_ap[0:1, 0:512], True, False, waits)
        mm(ps[:, 0:512], ones_row[0:1, :], lo_ap[0:1, 0:512], False, True)
        mm(ps[:, 512:1024], ones_row[0:1, :], hi_ap[0:1, 512:1024], True, False)
        return mm(ps[:, 512:1024], ones_row[0:1, :], lo_ap[0:1, 512:1024], False, True)

    def hi_lo(src_f32, hi_bf, lo_f32_scr, lo_bf, waits):
        ev1 = sop(lambda e: e.activation(lo_f32_scr[:], hi_bf[:], AF.Identity), waits)
        ev2 = vop(lambda e: e.tensor_sub(lo_f32_scr[:], src_f32[:], lo_f32_scr[:]), (ev1,))
        return vop(lambda e: e.tensor_copy(lo_bf[:], lo_f32_scr[:]), (ev2,))

    wq_free = [None, None]
    ev_qkv_evac = {}
    last_qkv_mm = None

    for hi_half, (xsrc, fcs) in enumerate(((xo_t, list(range(24))),
                                           (xp_t, list(range(8, 24))))):
        ev_wq_pre = {}
        ld_s = []
        for sg in range(8):
            g, hlf = sg // 2, sg % 2
            waits = [last_qkv_mm, sch.at("xk%d" % sg)] if hi_half == 1 else []
            ld_s.append(gdma(xh[:, sg * 4096:(sg + 1) * 4096],
                             xsrc[g][:, hlf * 4096:(hlf + 1) * 4096],
                             "xa%d" % sg, waits))
            if sg == 1:
                ev_wq_pre[0] = gdma(wqb[0][:], wqkv_t[fcs[0]], "wq0", (wq_free[0],))
        ev_wq_pre[1] = gdma(wqb[1][:], wqkv_t[fcs[1]], "wq1", (wq_free[1],))
        sq_done = {}
        stat_mm = {}
        park_ev = {}
        for c in range(DC):
            if hi_half == 0 and c % 4 == 0:
                sg = c // 4
                park_ev[sg] = cdma(xbf[sg // 2][:, (sg % 2) * 4096:(sg % 2 + 1) * 4096],
                                   xh[:, sg * 4096:(sg + 1) * 4096], "xk%d" % sg,
                                   (ld_s[sg],))
            w = [ld_s[c // 4]]
            if c >= 2:
                w.append(stat_mm[c - 2])
            sq_done[c] = sop(lambda e, c=c: e.activation(
                sqb[c % 2][:], xh[:, c * 1024:(c + 1) * 1024], AF.Square), w)
            st, sp = (c == 0), (c == DC - 1)
            w2 = [sq_done[c]]
            if c == 0:
                w2.append(psStat_free)
            first = True
            for nts in NT:
                mm(psStat[0:1, nts], ones_col[:, 0:1],
                   xh[:, c * 1024 + nts.start:c * 1024 + nts.stop], st, sp,
                   w2 if first else ())
                first = False
                stat_mm[c] = mm(psStat[32:33, nts], ones_col[:, 0:1],
                                sqb[c % 2][:, nts], st, sp)
        ev_sr, ev_fin = ln_finalize(stat_mm[DC - 1], st_a, st_b, bf_a, bf_b, st_c)
        psStat_free = ev_sr
        ev_b1 = bcast_single(bf_b, psA, (ev_fin, psAB_free[0]))
        ev_b2 = bcast_single(bf_a, psB, (psAB_free[1],))
        ev_rbc = vop(lambda e: e.tensor_copy(rb_bf[:], psA[:]), (ev_b1,))
        ev_nbc = vop(lambda e: e.tensor_copy(nb_bf[:], psB[:]), (ev_b2,))
        psAB_free = [ev_rbc, ev_nbc]
        ev_norm_v = ev_norm_g = None
        norm_ev = {}
        for c in range(DC):
            cs = slice(c * 1024, (c + 1) * 1024)
            w1 = [ev_rbc, ev_nbc]
            if hi_half == 0:
                w1.append(park_ev[c // 4])
            if c % 5 == 4:
                E("gpsimd", lambda e, cs=cs: e.tensor_tensor(
                    xh[:, cs], xh[:, cs], rb_bf[:], ALU.mult), w1, inc="gs")
                ev_norm_g = E("gpsimd", lambda e, cs=cs: e.tensor_tensor(
                    xh[:, cs], xh[:, cs], nb_bf[:], ALU.add), inc="gs")
                norm_ev[c] = ev_norm_g
            else:
                vop(lambda e, cs=cs: e.tensor_tensor(xh[:, cs], xh[:, cs], rb_bf[:], ALU.mult),
                    w1)
                ev_norm_v = vop(lambda e, cs=cs: e.tensor_tensor(
                    xh[:, cs], xh[:, cs], nb_bf[:], ALU.add))
                norm_ev[c] = ev_norm_v
        ev_norm = [ev_norm_v, ev_norm_g]
        if tapon("t_ln1") and hi_half == 0:
            gdma(taps["t_ln1"][:], xh[:], "tap", (ev_norm,))
        for i, fc in enumerate(fcs):
            par = i % 2
            if i in ev_wq_pre:
                ev_w = ev_wq_pre[i]
            else:
                ev_w = gdma(wqb[par][:], wqkv_t[fc], "wq%d" % par, (wq_free[par],))
            w0 = [ev_w, psAB_free[par]]
            if i >= 2:
                w0.append(ev_norm)
            last = None
            for c in range(DC):
                w_c = list(w0) if last is None else []
                if i < 2:
                    w_c.append(norm_ev[c])
                for nts in NT:
                    last = mm(psAB[par][:, nts], wqb[par][:, c * 128:(c + 1) * 128],
                              xh[:, c * 1024 + nts.start:c * 1024 + nts.stop],
                              c == 0, c == DC - 1, w_c if w_c else ())
                    w_c = []
            wq_free[par] = last
            last_qkv_mm = last
            if fc < 8:
                dst = q_fm[:, fc * 1024:(fc + 1) * 1024]
            elif fc < 16:
                base = (fc - 8) * 2048 + hi_half * 1024
                dst = k_fm[:, base:base + 1024]
            else:
                base = (fc - 16) * 2048 + hi_half * 1024
                dst = v_fm[:, base:base + 1024]
            ev_ev = vop(lambda e, par=par, dst=dst, fc=fc: e.tensor_scalar(
                dst, psAB[par][:], bqkv_sb[:, fc:fc + 1], None, ALU.add), (last, ev_bq))
            psAB_free[par] = ev_ev
            ev_qkv_evac[(hi_half, fc)] = ev_ev

    ev_q_done = ev_qkv_evac[(0, 7)]
    ev_k_done = ev_qkv_evac[(1, 15)]
    ev_v_done = ev_qkv_evac[(1, 23)]
    if tapon("t_q"):
        gdma(taps["t_q"][:], q_fm[:], "tap", (ev_q_done,))
    if tapon("t_k"):
        gdma(taps["t_k"][:], k_fm[:], "tap", (ev_k_done,))

    ckpt("A")
    gdma(gaup_sb[:], gaup_h[:], "su2")
    gdma(da_sb[:], da_h[:], "su2")
    ev_gaup = ev_da = sch.at("su2")
    psT_free = [None, None]
    ev_vtm = None
    for kc in range(16):
        par = kc % 2
        tr = None
        for dc in range(8):
            w = (ev_v_done, psT_free[par], ev_ident) if dc == 0 else ()
            src = v_fm[:, dc * 2048 + kc * 128:dc * 2048 + kc * 128 + 128]
            tr = E("tensor", lambda e, par=par, dc=dc, src=src: e.transpose(
                psT[par][:, dc * 128:(dc + 1) * 128], src, ident_bf[:]), w, "ts")
        ev_vtm = vop(lambda e, par=par, kc=kc: e.tensor_copy(
            v_tm[:, kc * 1024:(kc + 1) * 1024], psT[par][:]), (tr,))
        psT_free[par] = ev_vtm
    if tapon("t_vtm"):
        gdma(taps["t_vtm"][:], v_tm[:], "tap", (ev_vtm,))

    last_q_mm = None
    for c in range(8):
        w = [ev_q_done] if c == 0 else [last_q_mm]
        ev_s = sop(lambda e, c=c: e.activation(
            sqb[1][:], q_fm[:, c * 1024:(c + 1) * 1024], AF.Square), w)
        first = True
        for nts in NT:
            last_q_mm = mm(psStat[0:1, nts], ones_col[:, 0:1], sqb[1][:, nts],
                           c == 0, c == 7,
                           (ev_s, psStat_free) if first else (ev_s,))
            first = False
    ev_mk = []
    last_k_mm = None
    for half in range(2):
        for c in range(8):
            w = [ev_k_done] if (half == 0 and c == 0) else [last_k_mm]
            if half == 1 and c == 0:
                w.append(ev_mk[0])
            ev_s = sop(lambda e, c=c, half=half: e.activation(
                sqb[0][:], k_fm[:, c * 2048 + half * 1024:c * 2048 + half * 1024 + 1024],
                AF.Square), w)
            first = True
            for nts in NT:
                last_k_mm = mm(psStat[32:33, nts], ones_col[:, 0:1], sqb[0][:, nts],
                               c == 0, c == 7, (ev_s,) if first else ())
                first = False
        ev_mk.append(vop(lambda e, half=half: e.reduce_max(
            nm1[half][0:1, 0:1], psStat[32:33, :], axis=AX.X), (last_k_mm,)))
    ev_mkx = vop(lambda e: e.tensor_tensor(nm1[0][0:1, 0:1], nm1[0][0:1, 0:1],
                                           nm1[1][0:1, 0:1], ALU.max), (ev_mk[0], ev_mk[1]))
    ev_p = vop(lambda e: e.tensor_scalar(st_a[:], psStat[0:1, :], nm1[0][0:1, 0:1],
                                         None, ALU.mult), (ev_mkx, ev_mkx))
    ev_l = sop(lambda e: e.activation(st_a[:], st_a[:], AF.Ln), (ev_p,))
    ev_e = sop(lambda e: e.activation(st_a[:], st_a[:], AF.Exp, scale=0.5))
    ev_nmT_evac = vop(lambda e: e.tensor_scalar(negmaxT[:], st_a[:], -1.0, None,
                                                ALU.mult), (ev_e,))
    psStat_free = ev_nmT_evac
    psAB_free = [psAB_free[0], psAB_free[1]]
    if tapon("t_negmaxT"):
        gdma(taps["t_negmaxT"][:], negmaxT[:], "tap", (ev_nmT_evac,))

    pend_b = []
    ev_exp = None
    last_rs = None
    for kc in range(16):
        par = kc % 2
        ps = psAB[par]
        w0 = [ev_nmT_evac, psAB_free[par]]
        last = None
        for dc in range(8):
            lhs = k_fm[:, dc * 2048 + kc * 128:dc * 2048 + kc * 128 + 128]
            for nts in NT:
                last = mm(ps[:, nts], lhs, q_fm[:, dc * 1024 + nts.start:dc * 1024 + nts.stop],
                          dc == 0, False, w0 if last is None else ())
        for nts in NT:
            last = mm(ps[:, nts], ones_row[0:1, :], negmaxT[0:1, nts], False, True)
        ev_exp = sop(lambda e, par=par, kc=kc: e.activation(
            probsT[:, kc * 1024:(kc + 1) * 1024], psAB[par][:], AF.Exp, scale=SM_SCALE),
            (last,))
        psAB_free[par] = ev_exp
        def emit_rs(kc=kc, ev_exp=ev_exp):
            nonlocal last_rs
            w_rs = [ev_exp]
            if kc == 0:
                w_rs.append(psStat_free)
            first = True
            for nts in NT:
                last_rs = mm(psStat[0:1, nts], ones_col[:, 0:1],
                             probsT[:, kc * 1024 + nts.start:kc * 1024 + nts.stop],
                             kc == 0, kc == 15, w_rs if first else ())
                first = False
        pend_b.append(emit_rs)
        if len(pend_b) > 1:
            pend_b.pop(0)()
    if tapon("t_probsT"):
        gdma(taps["t_probsT"][:], probsT[:], "tap", (ev_exp,))

    while pend_b:
        pend_b.pop(0)()
    ev_rs = vop(lambda e: e.reciprocal(st_a[:], psStat[0:1, :]), (last_rs,))
    psStat_free = ev_rs
    ev_rs_bf = vop(lambda e: e.tensor_copy(bf_a[:], st_a[:]), (ev_rs, ev_rs))
    ev_bc = bcast_single(bf_a, psA, (ev_rs_bf, psAB_free[0]))
    ev_rsbc = vop(lambda e: e.tensor_copy(rb_bf[:], psA[:]), (ev_bc,))
    psAB_free[0] = ev_rsbc

    wob_free = [None, None]
    ev_wo_pre = {}
    for i in (0, 1):
        ev_wo_pre[i] = gdma(wob[i][:], wo_t[i], "wo%d" % i, ())

    ev_ao = None
    for dc in range(8):
        par = dc % 2
        ps = psAB[par]
        w0 = [ev_vtm, ev_exp, psAB_free[par]]
        last = None
        for kc in range(16):
            lhs = v_tm[:, kc * 1024 + dc * 128:kc * 1024 + dc * 128 + 128]
            for nts in NT:
                last = mm(ps[:, nts], lhs, probsT[:, kc * 1024 + nts.start:kc * 1024 + nts.stop],
                          kc == 0, kc == 15, w0 if last is None else ())
        ev_ao = vop(lambda e, par=par, dc=dc: e.tensor_tensor(
            attnout[:, dc * 1024:(dc + 1) * 1024], psAB[par][:], rb_bf[:], ALU.mult),
            (last, ev_rsbc))
        psAB_free[par] = ev_ao
    if tapon("t_ao"):
        gdma(taps["t_ao"][:], attnout[:], "tap", (ev_ao,))

    ckpt("B")
    ev_zev = {}
    stat_mm_c = {}
    ev_sbf = {}
    pend_c = []
    for fc in range(32):
        par = fc % 2
        if fc < 2:
            ev_w = ev_wo_pre[fc]
        else:
            ev_w = gdma(wob[par][:], wo_t[fc], "wo%d" % par, (wob_free[par],))
        ps = psAB[par]
        w0 = [ev_w, ev_ao, psAB_free[par]]
        last = None
        for dc in range(8):
            lhs = wob[par][:, dc * 128:(dc + 1) * 128]
            for nts in NT:
                last = mm(ps[:, nts], lhs, attnout[:, dc * 1024 + nts.start:dc * 1024 + nts.stop],
                          dc == 0, dc == 7, w0 if last is None else ())
        wob_free[par] = last
        while len(pend_c) > 1:
            pend_c.pop(0)()
        if "evac" not in CSKIP:
            w_z = [last, ev_bo]
            if DEBUG_TAPS and fc == 0:
                w_z.append(sch.at("tap"))
            ev_zev[fc] = vop(lambda e, par=par, fc=fc: e.tensor_scalar(
                z[:, fc * 1024:(fc + 1) * 1024], psAB[par][:], bo_sb[:, fc:fc + 1], None, ALU.add),
                w_z)
        else:
            ev_zev[fc] = vop(lambda e, par=par, fc=fc: e.tensor_copy(
                z[:, fc * 1024:(fc + 1) * 1024], psAB[par][:]), (last,))
        sq_ = sqc_buf if par == 0 else sqc2
        zs_c = z[:, fc * 1024:(fc + 1) * 1024]
        if "act" not in CSKIP:
            w_s = [ev_zev[fc]]
            if fc >= 2:
                w_s.append(stat_mm_c[fc - 2])
            ev_sbf[fc] = sop(lambda e, sq_=sq_, zs_c=zs_c: e.activation(
                sq_[:], zs_c, AF.Square), w_s)
        else:
            ev_sbf[fc] = ev_zev[fc]
        psAB_free[par] = ev_zev[fc]
        if "stats" not in CSKIP and "act" not in CSKIP:
            def emit_stats_c(fc=fc, sq_=sq_, zs_c=zs_c):
                w_st = [ev_sbf[fc]]
                if fc == 0:
                    w_st.append(psStat_free)
                first = True
                for nts in NT:
                    mm(psStat[0:1, nts], ones_col[:, 0:1], zs_c[:, nts], fc == 0, fc == 31,
                       w_st if first else ())
                    first = False
                    stat_mm_c[fc] = mm(psStat[32:33, nts], ones_col[:, 0:1], sq_[:, nts],
                                       fc == 0, fc == 31)
            pend_c.append(emit_stats_c)
        else:
            stat_mm_c[fc] = ev_zev[fc]
    if DEBUG_TAPS and TRUNC == "C1":
        gdma(taps["t_y"][:], z[:], "tap", (ev_zev[31],))
    ckpt("C1")
    for p_ in pend_c:
        p_()
    pend_c = []
    ev_sr, ev_fin = ln_finalize(stat_mm_c[31], st_a, st_b, bf_a, bf_b, st_c,
                                extra=(sch.at("tap"),) if DEBUG_TAPS else ())
    psStat_free = ev_sr
    ev_b1 = bcast_single(bf_b, psA, (ev_fin, psAB_free[0]))
    ev_b2 = bcast_single(bf_a, psB, (psAB_free[1],))
    ev_rbf = vop(lambda e: e.tensor_copy(rb_bf[:], psA[:]), (ev_b1,))
    ev_nbf = vop(lambda e: e.tensor_copy(nb_bf[:], psB[:]), (ev_b2,))
    psAB_free = [ev_rbf, ev_nbf]
    ckpt("C2")
    def eop(fc3, fn, waits=(), mod=3):
        if fc3 % mod == mod - 1:
            return E("gpsimd", fn, waits, inc="gs")
        return E("vector", fn, waits, inc="vs")

    xt_free = [None, None, None, None]
    ev_y = {}
    ev_dcast = {}
    stat_mm_d = {}
    for fc in range(32):
        xb = fc % 4
        w_x = [xt_free[xb]] if fc >= 4 else [ev_fin, ev_rbf, ev_nbf]
        ev_x = sdma(xtile[xb][:], xbf[fc // 8][:, (fc % 8) * 1024:(fc % 8) * 1024 + 1024],
                    "xt%d" % xb, w_x)
        zs = z[:, fc * 1024:(fc + 1) * 1024]
        w1 = [ev_rbf, ev_nbf, ev_zev[fc], stat_mm_c[31]]
        if DEBUG_TAPS and fc == 0:
            w1.append(sch.at("tap"))
        eop(fc, lambda e, zs=zs: e.tensor_tensor(zs, zs, rb_bf[:], ALU.mult), w1)
        eop(fc, lambda e, zs=zs: e.tensor_tensor(zs, zs, nb_bf[:], ALU.add))
        ev_y[fc] = eop(fc, lambda e, zs=zs, xb=xb: e.tensor_tensor(
            zs, zs, xtile[xb][:], ALU.add), (ev_x,))
        xt_free[xb] = ev_y[fc]
        w_c = [ev_y[fc]]
        if fc >= 1:
            w_c.append(stat_mm_d[fc - 1])
        ev_dcast[fc] = sop(lambda e, zs=zs: e.activation(sqc_buf[:], zs, AF.Square), w_c)
        w_st = [ev_dcast[fc]]
        if fc == 0:
            w_st.append(psStat_free)
        first = True
        for nts in NT:
            mm(psStat[0:1, nts], ones_col[:, 0:1], zs[:, nts], fc == 0, fc == 31,
               w_st if first else ())
            first = False
            stat_mm_d[fc] = mm(psStat[32:33, nts], ones_col[:, 0:1], sqc_buf[:, nts],
                               fc == 0, fc == 31)
    if tapon("t_y"):
        gdma(taps["t_y"][:], z[:], "tap", (ev_y[31],))

    ckpt("C")
    ev_sr, ev_fin = ln_finalize(stat_mm_d[31], st_a, st_b, bf_a, bf_b, st_c)
    psStat_free = ev_sr
    ev_b1 = bcast_single(bf_b, psA, (ev_fin, psAB_free[0]))
    ev_b2 = bcast_single(bf_a, psB, (psAB_free[1],))
    ev_rbc = vop(lambda e: e.tensor_copy(rb_bf[:], psA[:]), (ev_b1,))
    ev_nbc = vop(lambda e: e.tensor_copy(nb_bf[:], psB[:]), (ev_b2,))
    psAB_free = [ev_rbc, ev_nbc]

    ev_gw_pre = {}
    ev_uw_pre = {}
    for hc in (0, 1):
        par = hc % 2
        w_g = [wob_free[0], wob_free[1]]
        if DEBUG_TAPS:
            w_g.append(sch.at("tap"))
        ev_gw_pre[hc] = gdma(gwb[par][:], gup_g[hc], "gw%d" % par, w_g)
    E("gpsimd", lambda e: e.memset(tmpg_pad[:], 0.0), (ev_fin, ev_rbc, ev_nbc), inc="gs")
    E("gpsimd", lambda e: e.memset(tmpu_pad[:], 0.0), inc="gs")
    ev_pads = E("gpsimd", lambda e: e.memset(tmpd_pad[:], 0.0), inc="gs")

    ev_n3_v = ev_n3_g = None
    ev_e0 = None
    ev_l1 = {}
    w_e0 = [ev_gaup, ev_pads, psAB_free[0]]
    for fc in range(32):
        cs = slice(fc * 1024, (fc + 1) * 1024)
        ev_l1[fc] = eop(fc, lambda e, cs=cs: e.tensor_tensor(
            ln3[:, cs], z[:, cs], rb_bf[:], ALU.mult), (ev_rbc, ev_nbc), mod=5)
        ev_nn = eop(fc, lambda e, cs=cs: e.tensor_tensor(
            ln3[:, cs], ln3[:, cs], nb_bf[:], ALU.add), mod=5)
        if fc % 5 == 4:
            ev_n3_g = ev_nn
        else:
            ev_n3_v = ev_nn
        lhs = gaup_sb[:, fc * 48:(fc + 1) * 48]
        for nts in NT:
            ev_e0 = mm(psA[0:48, nts], lhs, ln3[:, fc * 1024 + nts.start:fc * 1024 + nts.stop],
                       fc == 0, fc == DC - 1, [ev_nn] + (w_e0 if fc == 0 else []))
    ev_n3 = [ev_n3_v, ev_n3_g]
    yp_ev = []
    for g in range(4):
        w = [ev_l1[g * 8 + i] for i in range(8)] + [stat_mm_d[31]]
        if DEBUG_TAPS and g == 0:
            w.append(sch.at("tap"))
        yp_ev.append(sdma(ypark[g], z[:, g * 8192:(g + 1) * 8192], "yp%d" % g, w))
    yp_all = [sch.at("yp%d" % g) for g in range(4)]
    for hc in (0, 1):
        ev_uw_pre[hc] = gdma(uwb[hc % 2][:], gup_u[hc], "uw%d" % (hc % 2),
                             (wob_free[0], wob_free[1]))
    if tapon("t_ln3"):
        gdma(taps["t_ln3"][:], ln3[:], "tap", (ev_n3,))

    ckpt("D")
    ev_tg = vop(lambda e: e.tensor_copy(tmpg_pad[0:16, :], psA[0:16, :]), (ev_e0,))
    ev_tu = vop(lambda e: e.tensor_copy(tmpu_pad[32:48, :], psA[32:48, :]))
    psAB_free[0] = ev_tu

    gw_free = [None, None]
    uw_free = [None, None]
    dw_free = [None, None]
    acct_free = [None, None]
    psA_free = ev_tu
    psB_free = psAB_free[1]
    silu_free = None
    ev_tdp = None
    ev_park = {}
    ev_dpark = {}
    stat_mm_e = {}
    pend_e = []
    last_acc_ld = [None, None]
    ev_hid_tap = None
    last_down_mm = None

    HPH = ((0, 29), (29, 29), (58, 28))
    ev_prev_down = None
    for pi, (h0, cnt) in enumerate(HPH):
        last_ph = (pi == 2)
        ev_mul = None
        for j in range(cnt):
            hc = h0 + j
            par = hc % 2
            if hc in ev_gw_pre:
                ev_gw = ev_gw_pre[hc]
                ev_uw = ev_uw_pre[hc]
            else:
                ev_gw = gdma(gwb[par][:], gup_g[hc], "gw%d" % par, (gw_free[par],))
                ev_uw = gdma(uwb[par][:], gup_u[hc], "uw%d" % par,
                             (uw_free[par], dw_free[par]))
            w0 = [ev_gw, psA_free, ev_tg, ev_n3]
            last_g = None
            for c in range(33):
                lhs = gwb[par][:, c * 128:(c + 1) * 128]
                for nts in NT:
                    rhs = (ln3[:, c * 1024 + nts.start:c * 1024 + nts.stop]
                           if c < DC else tmpg_pad[:, nts])
                    last_g = mm(psA[:, nts], lhs, rhs, c == 0, c == 32,
                                w0 if last_g is None else ())
            gw_free[par] = last_g
            ev_sl = sop(lambda e: e.activation(silu_buf[:], psA[:], AF.Silu),
                        (last_g, silu_free))
            psA_free = ev_sl
            w0 = [ev_uw, psB_free, ev_tu]
            last_u = None
            for c in range(33):
                lhs = uwb[par][:, c * 128:(c + 1) * 128]
                for nts in NT:
                    rhs = (ln3[:, c * 1024 + nts.start:c * 1024 + nts.stop]
                           if c < DC else tmpu_pad[:, nts])
                    last_u = mm(psB[:, nts], lhs, rhs, c == 0, c == 32,
                                w0 if last_u is None else ())
            uw_free[par] = last_u
            w_m = [last_u, ev_sl]
            if j == 0:
                w_m += [ev_prev_down] + yp_all
                if DEBUG_TAPS:
                    w_m.append(sch.at("tap"))
            ev_mul = vop(lambda e, j=j: e.tensor_tensor(
                hidden[:, j * 1024:(j + 1) * 1024], silu_buf[:], psB[:], ALU.mult), w_m)
            psB_free = ev_mul
            silu_free = ev_mul
        if tapon("t_hid") and pi == 0:
            ev_hid_tap = gdma(taps["t_hid"][:, 0:cnt * 1024], hidden[:, 0:cnt * 1024],
                              "tap", (ev_mul,))
        if last_ph:
            for i in range(11):
                cdma(ypre[i][:], ypark[i // 8][:, (i % 8) * 1024:(i % 8) * 1024 + 1024],
                     "pp", [gw_free[0], gw_free[1], uw_free[0], uw_free[1]] + yp_all)
        w0 = [ev_mul, psA_free, ev_da]
        last_td = None
        for j in range(cnt):
            hc = h0 + j
            lhs = da_sb[:, hc * 16:(hc + 1) * 16]
            for nts in NT:
                last_td = mm(psA[0:16, nts], lhs, hidden[:, j * 1024 + nts.start:j * 1024 + nts.stop],
                             j == 0, j == cnt - 1, w0 if last_td is None else ())
        if pi == 0:
            ev_td = vop(lambda e: e.tensor_copy(tmpd_h1[:], psA[0:16, :]), (last_td,))
        else:
            ev_td = vop(lambda e: e.tensor_add(tmpd_h1[:], tmpd_h1[:], psA[0:16, :]),
                        (last_td,))
            if last_ph:
                ev_tdp = vop(lambda e: e.tensor_copy(tmpd_pad[0:16, :], tmpd_h1[:]))
                ev_td = ev_tdp
        psA_free = ev_td

        ev_gu_done = (gw_free[0], gw_free[1], uw_free[0], uw_free[1])
        ap_prev = sch.at("ap")
        nch = cnt + (1 if last_ph else 0)
        col0 = h0 * 128
        psd_free = [psA_free, psB_free]
        for fc in range(32):
            par = fc % 2
            ev_dw = gdma(dwb[par][:, 0:nch * 128], dwn_t[fc][:, col0:col0 + nch * 128],
                         "dw%d" % par, (dw_free[par], uw_free[par]))
            ps = psAB[par]
            w0 = [ev_dw, ev_mul, psd_free[par]]
            if last_ph:
                w0.append(ev_tdp)
            last = None
            for j in range(nch):
                lhs = dwb[par][:, j * 128:(j + 1) * 128]
                for nts in NT:
                    rhs = (hidden[:, j * 1024 + nts.start:j * 1024 + nts.stop]
                           if j < cnt else tmpd_pad[:, nts])
                    last = mm(ps[:, nts], lhs, rhs, j == 0, j == nch - 1,
                              w0 if last is None else ())
            dw_free[par] = last
            last_down_mm = last
            while pend_e:
                pend_e.pop(0)()
            if pi == 0:
                ev_cp = vop(lambda e, par=par: e.tensor_copy(acct[par][:], psAB[par][:]),
                            (last, acct_free[par]))
                ev_park[fc] = sdma(accd[fc], acct[par][:], "ap", (ev_cp,))
                acct_free[par] = ev_park[fc]
                psd_free[par] = ev_cp
            elif pi == 1:
                ev_ld = sdma(acct[par][:], accd[fc], "ac%d" % par,
                             (acct_free[par], ap_prev))
                ev_add = vop(lambda e, par=par: e.tensor_tensor(
                    acct[par][:], psAB[par][:], acct[par][:], ALU.add), (last, ev_ld))
                psd_free[par] = ev_add
                ev_park[fc] = sdma(accd[fc], acct[par][:], "ap", (ev_add,))
                acct_free[par] = ev_park[fc]
            else:
                ev_ld = sdma(acct[par][:], accd[fc], "ac%d" % par,
                             (acct_free[par], ap_prev))
                last_acc_ld[par] = ev_ld
                w_add = [last, ev_ld]
                if fc == 0:
                    w_add.append(ev_gu_done)
                ds = dout_sb[:, fc * 1024:(fc + 1) * 1024]
                ev_add = vop(lambda e, par=par, ds=ds: e.tensor_tensor(
                    ds, psAB[par][:], acct[par][:], ALU.add), w_add)
                psd_free[par] = ev_add
                acct_free[par] = ev_add
                w_s = [ev_add]
                if fc >= 1:
                    w_s.append(stat_mm_e[fc - 1])
                ev_sq = sop(lambda e, ds=ds: e.activation(sqc_buf[:], ds, AF.Square), w_s)
                def emit_stats_e(fc=fc, ev_sq=ev_sq, ds=ds):
                    w_st = [ev_sq]
                    if fc == 0:
                        w_st.append(psStat_free)
                    first = True
                    for nts in NT:
                        mm(psStat[0:1, nts], ones_col[:, 0:1],
                           ds[:, nts], fc == 0, fc == 31, w_st if first else ())
                        first = False
                        stat_mm_e[fc] = mm(psStat[32:33, nts], ones_col[:, 0:1], sqc_buf[:, nts],
                                           fc == 0, fc == 31)
                pend_e.append(emit_stats_e)
        while pend_e:
            pend_e.pop(0)()
        ev_prev_down = last_down_mm
        psA_free, psB_free = psd_free[0], psd_free[1]
        psAB_free = [psA_free, psB_free]

    ckpt("E")
    ev_sr, ev_fin = ln_finalize(stat_mm_e[31], est_a, est_b, bf_a, bf_b, est_c,
                                extra=(last_acc_ld[0], last_acc_ld[1]))
    psStat_free = ev_sr
    ev_b1 = bcast_single(bf_b, psA, (ev_fin, psAB_free[0]))
    ev_b2 = bcast_single(bf_a, psB, (psAB_free[1],))
    ev_rv = vop(lambda e: e.tensor_copy(erb_v[:], psA[:]), (ev_b1,))
    ev_nv = vop(lambda e: e.tensor_copy(enb_v[:], psB[:]), (ev_b2,))
    ckpt("E2")

    pp_all = sch.at("pp")
    yt_free = [None] * 4
    ot_free = [None] * 4
    tb4 = [tbv, tbp, erb_p, enb_p]
    tb_free = [None] * 4
    DVE_FIN = frozenset((0, 2, 4, 6))
    ev_out = None
    for fc in range(32):
        b = fc % 4
        if fc < 11:
            ybuf, ev_yl = ypre[fc], pp_all
        else:
            ybuf = ytile[b]
            w_y = [yt_free[b]] if fc >= 15 else [last_down_mm]
            ysrc = ypark[fc // 8][:, (fc % 8) * 1024:(fc % 8) * 1024 + 1024]
            if b < 2:
                ev_yl = cdma(ybuf[:], ysrc, "yt%d" % b, w_y)
            else:
                ev_yl = sdma(ybuf[:], ysrc, "ys%d" % b, w_y)
        ds = dout_sb[:, fc * 1024:(fc + 1) * 1024]
        ot = otile[b]
        tb = tb4[b]
        vop(lambda e, ds=ds, tb=tb: e.tensor_tensor(
            tb[:], ds, erb_v[:], ALU.mult), (ev_rv, ev_nv, tb_free[b]))
        ev_add = vop(lambda e, tb=tb: e.tensor_tensor(
            tb[:], tb[:], enb_v[:], ALU.add))
        if fc in DVE_FIN:
            ev_o = vop(lambda e, ot=ot, tb=tb, ybuf=ybuf: e.tensor_tensor(
                ot[:], tb[:], ybuf[:], ALU.add), (ev_yl, ot_free[b]))
        else:
            ev_o = E("gpsimd", lambda e, ot=ot, tb=tb, ybuf=ybuf: e.tensor_tensor(
                ot[:], tb[:], ybuf[:], ALU.add), (ev_add, ev_yl, ot_free[b]), inc="gs")
        tb_free[b] = ev_o
        if fc % 2 == 0:
            ev_out = sdma(o[fc * 128:(fc + 1) * 128, :], ot[:], "ow%d" % (b // 2), (ev_o,))
        else:
            ev_out = cdma(o[fc * 128:(fc + 1) * 128, :], ot[:], "oc%d" % (b // 2), (ev_o,))
        yt_free[b] = ev_o
        ot_free[b] = ev_out
    stopped[0] = False
    E("sync", None, tuple(sch.at(s) for s in SD_SEMS))
    E("gpsimd", None, tuple(sch.at(s) for s in GD_SEMS))
    E("scalar", None, tuple(sch.at(s) for s in SC_SEMS))

    es = contextlib.ExitStack()
    with es:
        sems = {}
        for s in ("ts", "vs", "ss", "gs") + GD_SEMS + SD_SEMS + SC_SEMS:
            sems[s] = es.enter_context(nc.semaphore(s))
        block = es.enter_context(nc.Block())

        def runner(name):
            def run(e):
                for waits, fn, inc, n in sch.ops[name]:
                    for s, v in waits:
                        e.wait_ge(sems[s], v)
                    ins = fn(e) if fn is not None else None
                    if inc is not None:
                        ins.then_inc(sems[inc], n)
            return run

        block.tensor(runner("tensor"))
        block.vector(runner("vector"))
        block.scalar(runner("scalar"))
        block.gpsimd(runner("gpsimd"))
        block.sync(runner("sync"))
    return nc


def _prep_shared(inputs):
    f = np.float32
    Wqkv = np.asarray(inputs["L0_attn_Wqkv"], f)
    Wo = np.asarray(inputs["L0_attn_Wo"], f)
    gate_w = np.asarray(inputs["L0_ffn_gate_w"], f)
    up_w = np.asarray(inputs["L0_ffn_up_w"], f)
    down_w = np.asarray(inputs["L0_ffn_down_w"], f)
    gate_a = np.asarray(inputs["L0_ffn_gate_a"], f)
    gate_b = np.asarray(inputs["L0_ffn_gate_b"], f)
    up_a = np.asarray(inputs["L0_ffn_up_a"], f)
    up_b = np.asarray(inputs["L0_ffn_up_b"], f)
    down_a = np.asarray(inputs["L0_ffn_down_a"], f)
    down_b = np.asarray(inputs["L0_ffn_down_b"], f)
    bqkv = np.asarray(inputs["L0_attn_bqkv"], f)
    bo = np.asarray(inputs["L0_attn_bo"], f)

    d = {}
    d["wqkv_t"] = np.ascontiguousarray(
        Wqkv.reshape(32, 128, 24, 128).transpose(2, 1, 0, 3).reshape(24, 128, 4096))
    d["wo_t"] = np.ascontiguousarray(
        Wo.reshape(8, 128, 32, 128).transpose(2, 1, 0, 3).reshape(32, 128, 1024))

    aug_g = np.zeros((4224, DH), f)
    aug_g[:4096] = gate_w.T
    aug_g[4096:4112] = gate_b.T
    d["gup_g"] = np.ascontiguousarray(
        aug_g.reshape(33, 128, 86, 128).transpose(2, 1, 0, 3).reshape(86, 128, 4224))
    aug_u = np.zeros((4224, DH), f)
    aug_u[:4096] = up_w.T
    aug_u[4128:4144] = up_b.T
    d["gup_u"] = np.ascontiguousarray(
        aug_u.reshape(33, 128, 86, 128).transpose(2, 1, 0, 3).reshape(86, 128, 4224))

    aug_d = np.zeros((11136, 4096), f)
    aug_d[:DH] = down_w.T
    aug_d[DH:DH + 16] = down_b.T
    d["dwn_t"] = np.ascontiguousarray(
        aug_d.reshape(87, 128, 32, 128).transpose(2, 1, 0, 3).reshape(32, 128, 11136))

    A = np.zeros((4096, 48), f)
    A[:, 0:16] = gate_a.T
    A[:, 32:48] = up_a.T
    d["gaup_h"] = np.ascontiguousarray(
        A.reshape(32, 128, 48).transpose(1, 0, 2).reshape(128, 1536))
    d["da_h"] = np.ascontiguousarray(
        down_a.T.reshape(86, 128, 16).transpose(1, 0, 2).reshape(128, 1376))

    d["bqkv_h"] = np.ascontiguousarray(bqkv.reshape(24, 128).T)
    d["bo_h"] = np.ascontiguousarray(bo.reshape(32, 128).T)
    d["ident_h"] = np.eye(128, dtype=f)
    return d


def _tile_x(xT_half):
    return np.ascontiguousarray(
        xT_half.reshape(4, 8, 128, 1024).transpose(0, 2, 1, 3).reshape(4, 128, 8192))


def kernel(**inputs):
    shared = _prep_shared(inputs)
    x = np.asarray(inputs["L0_attn_x"], np.float32)
    in_maps = []
    for c in range(8):
        b, half = c // 2, c % 2
        xT = x[b].T
        m = dict(shared)
        m["xo_t"] = _tile_x(np.ascontiguousarray(xT[:, half * T:(half + 1) * T]))
        m["xp_t"] = _tile_x(np.ascontiguousarray(xT[:, (1 - half) * T:(2 - half) * T]))
        in_maps.append(m)
    nc = build_program()
    res = run_bass_kernel_spmd(nc, in_maps, core_ids=list(range(8)))
    out = np.empty((4, 2048, 4096), np.float32)
    for c in range(8):
        b, half = c // 2, c % 2
        out[b, half * T:(half + 1) * T, :] = np.asarray(res.results[c]["o"]).T
    return out


# version 21 (final)
# speedup vs baseline: 1.0097x; 1.0030x over previous
import contextlib
import numpy as np
import concourse.bass as bass
import concourse.mybir as mybir
from concourse.bass_utils import run_bass_kernel_spmd

F32 = mybir.dt.float32
BF16 = mybir.dt.bfloat16
AF = mybir.ActivationFunctionType
ALU = mybir.AluOpType
AX = mybir.AxisListType

D = 4096
DC = 32
T = 1024
S = 2048
DH = 11008
HCN = 86
HH = 43
EPS = 1e-5
SM_SCALE = 0.03125
INV_D = 1.0 / 4096.0

NT = (slice(0, 512), slice(512, 1024))
ENGINES = ("tensor", "vector", "scalar", "gpsimd", "sync")

DEBUG_TAPS = False
TAP_KEYS = None
TRUNC = "full"
CSKIP = ()
TAIL_POOL = True
TAIL_OG = True


class _Stop(Exception):
    pass

PERS = 17024
BG = PERS + 1280
R56 = BG + 147456
R7 = R56 + 32768
ARENA_END = R7 + 30464
assert ARENA_END <= 229344 - 128, ARENA_END


class Sched:
    def __init__(self):
        self.ops = {e: [] for e in ENGINES}
        self.cnt = {}
        self.last_wait = {e: {} for e in ENGINES}

    def emit(self, eng, fn, waits=(), inc=None, n=1):
        w = {}
        def add(ev):
            if ev is None:
                return
            if isinstance(ev, (list, tuple)) and len(ev) == 2 \
                    and isinstance(ev[0], str) and isinstance(ev[1], int):
                s, v = ev
                if v > self.last_wait[eng].get(s, 0):
                    w[s] = max(w.get(s, 0), v)
                return
            for sub in ev:
                add(sub)
        add(list(waits))
        for s, v in w.items():
            self.last_wait[eng][s] = v
        ev_out = None
        if inc is not None:
            self.cnt[inc] = self.cnt.get(inc, 0) + n
            ev_out = (inc, self.cnt[inc])
        self.ops[eng].append((tuple(w.items()), fn, inc, n))
        return ev_out

    def at(self, sem):
        return (sem, self.cnt.get(sem, 0))


def build_program():
    nc = bass.Bass("TRN2", target_bir_lowering=False)

    def inp(name, shape):
        return nc.dram_tensor(name, list(shape), F32, kind="ExternalInput").ap()

    xo_t = inp("xo_t", (4, 128, 8192))
    xp_t = inp("xp_t", (4, 128, 8192))
    wqkv_t = inp("wqkv_t", (24, 128, 4096))
    wo_t = inp("wo_t", (32, 128, 1024))
    gup_g = inp("gup_g", (HCN, 128, 4224))
    gup_u = inp("gup_u", (HCN, 128, 4224))
    dwn_t = inp("dwn_t", (32, 128, 11136))
    gaup_h = inp("gaup_h", (128, 1536))
    da_h = inp("da_h", (128, 1376))
    bqkv_h = inp("bqkv_h", (128, 24))
    bo_h = inp("bo_h", (128, 32))
    ident_h = inp("ident_h", (128, 128))

    o = nc.dram_tensor("o", [D, T], F32, kind="ExternalOutput").ap()
    ypark = nc.dram_tensor("ypark", [4, 128, 8192], BF16, kind="Internal").ap()
    xbf = nc.dram_tensor("xbf", [4, 128, 8192], BF16, kind="Internal").ap()
    accd = nc.dram_tensor("accd", [32, 128, 1024], F32, kind="Internal").ap()
    doutd = nc.dram_tensor("doutd", [32, 128, 1024], F32, kind="Internal").ap()

    taps = {}
    def tapon(nm):
        return DEBUG_TAPS and (TAP_KEYS is None or nm in TAP_KEYS)
    if DEBUG_TAPS:
        for nm, shape in (
            ("t_ln1", (128, 32768)), ("t_q", (128, 8192)), ("t_k", (128, 16384)),
            ("t_vtm", (128, 16384)), ("t_probsT", (128, 16384)),
            ("t_negmaxT", (1, 1024)), ("t_ao", (128, 8192)),
            ("t_y", (128, 32768)), ("t_ln3", (128, 32768)), ("t_hid", (128, 44032)),
        ):
            taps[nm] = nc.dram_tensor(nm, list(shape), F32, kind="ExternalOutput").ap()

    def sb(name, off, shape, dtype):
        return nc.alloc_sbuf_tensor_at(name, list(shape), dtype, offset=off).ap()

    ident_bf = sb("ident_bf", PERS + 0, (128, 128), BF16)
    ones_row = sb("ones_row", PERS + 256, (1, 128), BF16)
    ones_col = sb("ones_col", PERS + 512, (128, 1), BF16)
    eps_sb = sb("eps_sb", PERS + 576, (1, 1), F32)
    bqkv_sb = sb("bqkv_sb", PERS + 640, (128, 24), F32)
    bo_sb = sb("bo_sb", PERS + 768, (128, 32), F32)
    nm1 = [sb("nm1a", PERS + 896, (128, 1), F32),
           sb("nm1b", PERS + 928, (128, 1), F32)]
    nm2 = [sb("nm2a", PERS + 960, (128, 1), F32),
           sb("nm2b", PERS + 992, (128, 1), F32)]
    nmbf = [sb("nmbfa", PERS + 1024, (128, 1), BF16),
            sb("nmbfb", PERS + 1056, (128, 1), BF16)]

    q_fm = sb("q_fm", BG + 0, (128, 8192), BF16)
    k_fm = sb("k_fm", BG + 16384, (128, 16384), BF16)
    v_fm = sb("v_fm", BG + 49152, (128, 16384), BF16)
    xh = sb("xh", BG + 81920, (128, 32768), BF16)
    v_tm = sb("v_tm", BG + 81920, (128, 16384), BF16)
    probsT = sb("probsT", BG + 114688, (128, 16384), BF16)
    z = sb("z", BG + 0, (128, 32768), BF16)
    ln3 = sb("ln3", BG + 65536, (128, 32768), BF16)
    hidden = sb("hidden", BG + 0, (128, 29696), BF16)
    uwb = [sb("uwb0", R56 + 16896, (128, 4224), BF16),
           sb("uwb1", BG + 131072, (128, 4224), BF16)]
    dwb = [sb("dwb0", R56 + 16896, (128, 3712), BF16),
           sb("dwb1", BG + 131072, (128, 3712), BF16)]
    ytile = [sb("ytile%d" % i, BG + 2048 * i, (128, 1024), BF16)
             for i in range(4)]
    ypre = [sb("ypre%d" % i, R56 + 256 + 2048 * i, (128, 1024), BF16)
            for i in range(8)] + \
           [sb("ypre%d" % (8 + i), BG + 59392 + 2048 * i, (128, 1024), BF16)
            for i in range(3)]
    otile = [sb("otile%d" % i, BG + 131072 + 4096 * i, (128, 1024), F32)
             for i in range(4)]

    wqb = [sb("wqb0", R56 + 0, (128, 4096), BF16),
           sb("wqb1", R56 + 8192, (128, 4096), BF16)]
    attnout = sb("attnout", R56 + 0, (128, 8192), BF16)
    wob = [sb("wob0", R56 + 16384, (128, 1024), BF16),
           sb("wob1", R56 + 18432, (128, 1024), BF16)]
    xtile = [sb("xtile%d" % i, R7 + 4096 + 2048 * i, (128, 1024), BF16)
             for i in range(4)]
    zbf_buf = sb("zbf_buf", R56 + 25344, (128, 1024), BF16)
    sqc_buf = sb("sqc_buf", R56 + 27392, (128, 1024), BF16)
    lo1_bf = sb("lo1_bf", R56 + 25344, (1, 1024), BF16)
    lo2_bf = sb("lo2_bf", R56 + 27392, (1, 1024), BF16)
    gwb = [sb("gwb0", R56 + 0, (128, 4224), BF16),
           sb("gwb1", R56 + 8448, (128, 4224), BF16)]
    dout_sb = sb("dout_sb", BG + 65536, (128, 32768), BF16)

    sqb = [sb("sqb0", R7 + 0, (128, 1024), BF16),
           sb("sqb1", R7 + 2048, (128, 1024), BF16)]
    st_a = sb("st_a", R7 + 4096, (1, 1024), F32)
    st_b = sb("st_b", R7 + 8192, (1, 1024), F32)
    bf_a = sb("bf_a", R7 + 12288, (1, 1024), BF16)
    bf_b = sb("bf_b", R7 + 14336, (1, 1024), BF16)
    rb_bf = sb("rb_bf", R7 + 16384, (128, 1024), BF16)
    nb_bf = sb("nb_bf", R7 + 18432, (128, 1024), BF16)
    negmaxT = sb("negmaxT", R7 + 0, (1, 1024), BF16)
    zbf2 = sb("zbf2", R7 + 0, (128, 1024), BF16)
    sqc2 = sb("sqc2", R7 + 2048, (128, 1024), BF16)
    st_c = sb("st_c", R7 + 0, (1, 1024), F32)
    rb_f32 = sb("rb_f32", R7 + 16384, (128, 1024), F32)
    nb_f32 = sb("nb_f32", R7 + 20480, (128, 1024), F32)
    tmpg_pad = sb("tmpg_pad", R7 + 0, (128, 1024), BF16)
    tmpu_pad = sb("tmpu_pad", R7 + 2048, (128, 1024), BF16)
    tmpd_pad = sb("tmpd_pad", R7 + 4096, (128, 1024), BF16)
    silu_buf = sb("silu_buf", R7 + 6144, (128, 1024), BF16)
    acct = [sb("acct0", R7 + 8192, (128, 1024), F32),
            sb("acct1", R7 + 12288, (128, 1024), F32)]
    tmpd_h1 = sb("tmpd_h1", R7 + 16384, (16, 1024), F32)
    est_a = sb("est_a", R7 + 0, (1, 1024), F32)
    est_b = sb("est_b", R7 + 4096, (1, 1024), F32)
    est_c = sb("est_c", R7 + 8192, (1, 1024), F32)
    erb_v = sb("erb_v", R7 + 16384, (128, 1024), BF16)
    enb_v = sb("enb_v", R7 + 18432, (128, 1024), BF16)
    tbv = sb("tbv", R7 + 20480, (128, 1024), BF16)
    tbp = sb("tbp", R7 + 22528, (128, 1024), BF16)
    erb_p = sb("erb_p", BG + 8192, (128, 1024), BF16)
    enb_p = sb("enb_p", BG + 10240, (128, 1024), BF16)
    gaup_sb = sb("gaup_sb", R7 + 24576, (128, 1536), BF16)
    da_sb = sb("da_sb", R7 + 27648, (128, 1376), BF16)

    psA = nc.alloc_psum_tensor("psA", [128, 1024], F32).ap()
    psB = nc.alloc_psum_tensor("psB", [128, 1024], F32).ap()
    psStat = nc.alloc_psum_tensor("psStat", [33, 1024], F32).ap()
    psT = [nc.alloc_psum_tensor("psT1", [128, 1024], BF16).ap(),
           nc.alloc_psum_tensor("psT2", [128, 1024], BF16).ap()]
    psAB = (psA, psB)

    sch = Sched()

    def E(eng, fn, waits=(), inc=None, n=1):
        if stopped[0]:
            return None
        return sch.emit(eng, fn, waits, inc, n)

    stopped = [False]

    def ckpt(name):
        if TRUNC == name:
            stopped[0] = True

    GD_SEMS = ("su", "su2", "xa0", "xa1", "xa2", "xa3", "xa4", "xa5", "xa6", "xa7",
               "wq0", "wq1", "wo0", "wo1",
               "gw0", "gw1", "uw0", "uw1", "dw0", "dw1", "og0", "og1", "tap")
    SC_SEMS = ("yt0", "yt1", "yt2", "yt3", "xk0", "xk1", "xk2", "xk3",
               "xk4", "xk5", "xk6", "xk7",
               "pp")
    SD_SEMS = ("sb", "xt0", "xt1", "xt2", "xt3", "yp0", "yp1", "yp2", "yp3", "ap",
               "ac0", "ac1", "ow0", "ow1")

    def gdma(dst, src, sem, waits=()):
        assert sem in GD_SEMS, sem
        return E("gpsimd", lambda e, d=dst, s=src: e.dma_start(d, s), waits, sem, 16)

    def sdma(dst, src, sem, waits=()):
        assert sem in SD_SEMS, sem
        return E("sync", lambda e, d=dst, s=src: e.dma_start(d, s), waits, sem, 16)

    def cdma(dst, src, sem, waits=()):
        assert sem in SC_SEMS, sem
        return E("scalar", lambda e, d=dst, s=src: e.dma_start(d, s), waits, sem, 16)

    def mm(out_ap, lhs_ap, rhs_ap, start, stop, waits=()):
        return E("tensor",
                 lambda e, o=out_ap, l=lhs_ap, r=rhs_ap, st=start, sp=stop:
                 e.matmul(o, l, r, start=st, stop=sp), waits, "ts")

    def vop(fn, waits=()):
        return E("vector", fn, waits, "vs")

    def sop(fn, waits=()):
        return E("scalar", fn, waits, "ss")

    E("gpsimd", lambda e: e.memset(ones_row[:], 1.0), inc="gs")
    E("gpsimd", lambda e: e.memset(ones_col[:], 1.0), inc="gs")
    ev_memset = E("gpsimd", lambda e: e.memset(eps_sb[:], EPS), inc="gs")
    ev_ident = gdma(ident_bf[:], ident_h[:], "su")
    sdma(bqkv_sb[:], bqkv_h[:], "sb")
    sdma(bo_sb[:], bo_h[:], "sb")
    ev_bq = ev_bo = sch.at("sb")

    psAB_free = [None, None]
    psStat_free = None

    def ln_finalize(wait_ev, a_st, b_st, a_bf, b_bf, scr, extra=()):
        E("vector", lambda e: e.tensor_scalar(a_st[:], psStat[0:1, :], INV_D, None, ALU.mult),
          (wait_ev, ev_memset) + tuple(extra), inc="vs")
        ev_sr = E("vector", lambda e: e.tensor_scalar(b_st[:], psStat[32:33, :], INV_D, None, ALU.mult),
                  inc="vs")
        E("vector", lambda e: e.scalar_tensor_tensor(scr[:], a_st[:], -1.0, a_st[:],
                                                     ALU.mult, ALU.mult), inc="vs")
        ev_var = E("vector", lambda e: e.tensor_add(b_st[:], b_st[:], scr[:]), inc="vs")
        E("scalar", lambda e: e.activation(b_st[:], b_st[:], AF.Ln, bias=eps_sb[0:1, 0:1]),
          (ev_var,), inc="ss")
        ev_rstd = E("scalar", lambda e: e.activation(b_st[:], b_st[:], AF.Exp, scale=-0.5),
                    inc="ss")
        E("vector", lambda e: e.scalar_tensor_tensor(a_st[:], a_st[:], -1.0, b_st[:],
                                                     ALU.mult, ALU.mult), (ev_rstd,), inc="vs")
        E("vector", lambda e: e.tensor_copy(a_bf[:], a_st[:]), inc="vs")
        ev_done = E("vector", lambda e: e.tensor_copy(b_bf[:], b_st[:]), inc="vs")
        return ev_sr, ev_done

    def bcast_single(bf_ap, ps, waits):
        mm(ps[:, 0:512], ones_row[0:1, :], bf_ap[0:1, 0:512], True, True, waits)
        return mm(ps[:, 512:1024], ones_row[0:1, :], bf_ap[0:1, 512:1024], True, True)

    def bcast_double(hi_ap, lo_ap, ps, waits):
        mm(ps[:, 0:512], ones_row[0:1, :], hi_ap[0:1, 0:512], True, False, waits)
        mm(ps[:, 0:512], ones_row[0:1, :], lo_ap[0:1, 0:512], False, True)
        mm(ps[:, 512:1024], ones_row[0:1, :], hi_ap[0:1, 512:1024], True, False)
        return mm(ps[:, 512:1024], ones_row[0:1, :], lo_ap[0:1, 512:1024], False, True)

    def hi_lo(src_f32, hi_bf, lo_f32_scr, lo_bf, waits):
        ev1 = sop(lambda e: e.activation(lo_f32_scr[:], hi_bf[:], AF.Identity), waits)
        ev2 = vop(lambda e: e.tensor_sub(lo_f32_scr[:], src_f32[:], lo_f32_scr[:]), (ev1,))
        return vop(lambda e: e.tensor_copy(lo_bf[:], lo_f32_scr[:]), (ev2,))

    wq_free = [None, None]
    ev_qkv_evac = {}
    last_qkv_mm = None

    for hi_half, (xsrc, fcs) in enumerate(((xo_t, list(range(24))),
                                           (xp_t, list(range(8, 24))))):
        ev_wq_pre = {}
        ld_s = []
        for sg in range(8):
            g, hlf = sg // 2, sg % 2
            waits = [last_qkv_mm, sch.at("xk%d" % sg)] if hi_half == 1 else []
            ld_s.append(gdma(xh[:, sg * 4096:(sg + 1) * 4096],
                             xsrc[g][:, hlf * 4096:(hlf + 1) * 4096],
                             "xa%d" % sg, waits))
            if sg == 1:
                ev_wq_pre[0] = gdma(wqb[0][:], wqkv_t[fcs[0]], "wq0", (wq_free[0],))
        ev_wq_pre[1] = gdma(wqb[1][:], wqkv_t[fcs[1]], "wq1", (wq_free[1],))
        sq_done = {}
        stat_mm = {}
        park_ev = {}
        for c in range(DC):
            if hi_half == 0 and c % 4 == 0:
                sg = c // 4
                park_ev[sg] = cdma(xbf[sg // 2][:, (sg % 2) * 4096:(sg % 2 + 1) * 4096],
                                   xh[:, sg * 4096:(sg + 1) * 4096], "xk%d" % sg,
                                   (ld_s[sg],))
            w = [ld_s[c // 4]]
            if c >= 2:
                w.append(stat_mm[c - 2])
            sq_done[c] = sop(lambda e, c=c: e.activation(
                sqb[c % 2][:], xh[:, c * 1024:(c + 1) * 1024], AF.Square), w)
            st, sp = (c == 0), (c == DC - 1)
            w2 = [sq_done[c]]
            if c == 0:
                w2.append(psStat_free)
            first = True
            for nts in NT:
                mm(psStat[0:1, nts], ones_col[:, 0:1],
                   xh[:, c * 1024 + nts.start:c * 1024 + nts.stop], st, sp,
                   w2 if first else ())
                first = False
                stat_mm[c] = mm(psStat[32:33, nts], ones_col[:, 0:1],
                                sqb[c % 2][:, nts], st, sp)
        ev_sr, ev_fin = ln_finalize(stat_mm[DC - 1], st_a, st_b, bf_a, bf_b, st_c)
        psStat_free = ev_sr
        ev_b1 = bcast_single(bf_b, psA, (ev_fin, psAB_free[0]))
        ev_b2 = bcast_single(bf_a, psB, (psAB_free[1],))
        ev_rbc = vop(lambda e: e.tensor_copy(rb_bf[:], psA[:]), (ev_b1,))
        ev_nbc = vop(lambda e: e.tensor_copy(nb_bf[:], psB[:]), (ev_b2,))
        psAB_free = [ev_rbc, ev_nbc]
        ev_norm_v = ev_norm_g = None
        norm_ev = {}
        for c in range(DC):
            cs = slice(c * 1024, (c + 1) * 1024)
            w1 = [ev_rbc, ev_nbc]
            if hi_half == 0:
                w1.append(park_ev[c // 4])
            if c % 5 == 4:
                E("gpsimd", lambda e, cs=cs: e.tensor_tensor(
                    xh[:, cs], xh[:, cs], rb_bf[:], ALU.mult), w1, inc="gs")
                ev_norm_g = E("gpsimd", lambda e, cs=cs: e.tensor_tensor(
                    xh[:, cs], xh[:, cs], nb_bf[:], ALU.add), inc="gs")
                norm_ev[c] = ev_norm_g
            else:
                vop(lambda e, cs=cs: e.tensor_tensor(xh[:, cs], xh[:, cs], rb_bf[:], ALU.mult),
                    w1)
                ev_norm_v = vop(lambda e, cs=cs: e.tensor_tensor(
                    xh[:, cs], xh[:, cs], nb_bf[:], ALU.add))
                norm_ev[c] = ev_norm_v
        ev_norm = [ev_norm_v, ev_norm_g]
        if tapon("t_ln1") and hi_half == 0:
            gdma(taps["t_ln1"][:], xh[:], "tap", (ev_norm,))
        for i, fc in enumerate(fcs):
            par = i % 2
            if i in ev_wq_pre:
                ev_w = ev_wq_pre[i]
            else:
                ev_w = gdma(wqb[par][:], wqkv_t[fc], "wq%d" % par, (wq_free[par],))
            w0 = [ev_w, psAB_free[par]]
            if i >= 2:
                w0.append(ev_norm)
            last = None
            for c in range(DC):
                w_c = list(w0) if last is None else []
                if i < 2:
                    w_c.append(norm_ev[c])
                for nts in NT:
                    last = mm(psAB[par][:, nts], wqb[par][:, c * 128:(c + 1) * 128],
                              xh[:, c * 1024 + nts.start:c * 1024 + nts.stop],
                              c == 0, c == DC - 1, w_c if w_c else ())
                    w_c = []
            wq_free[par] = last
            last_qkv_mm = last
            if fc < 8:
                dst = q_fm[:, fc * 1024:(fc + 1) * 1024]
            elif fc < 16:
                base = (fc - 8) * 2048 + hi_half * 1024
                dst = k_fm[:, base:base + 1024]
            else:
                base = (fc - 16) * 2048 + hi_half * 1024
                dst = v_fm[:, base:base + 1024]
            ev_ev = vop(lambda e, par=par, dst=dst, fc=fc: e.tensor_scalar(
                dst, psAB[par][:], bqkv_sb[:, fc:fc + 1], None, ALU.add), (last, ev_bq))
            psAB_free[par] = ev_ev
            ev_qkv_evac[(hi_half, fc)] = ev_ev

    ev_q_done = ev_qkv_evac[(0, 7)]
    ev_k_done = ev_qkv_evac[(1, 15)]
    ev_v_done = ev_qkv_evac[(1, 23)]
    if tapon("t_q"):
        gdma(taps["t_q"][:], q_fm[:], "tap", (ev_q_done,))
    if tapon("t_k"):
        gdma(taps["t_k"][:], k_fm[:], "tap", (ev_k_done,))

    ckpt("A")
    gdma(gaup_sb[:], gaup_h[:], "su2")
    gdma(da_sb[:], da_h[:], "su2")
    ev_gaup = ev_da = sch.at("su2")
    psT_free = [None, None]
    ev_vtm = None
    for kc in range(16):
        par = kc % 2
        tr = None
        for dc in range(8):
            w = (ev_v_done, psT_free[par], ev_ident) if dc == 0 else ()
            src = v_fm[:, dc * 2048 + kc * 128:dc * 2048 + kc * 128 + 128]
            tr = E("tensor", lambda e, par=par, dc=dc, src=src: e.transpose(
                psT[par][:, dc * 128:(dc + 1) * 128], src, ident_bf[:]), w, "ts")
        ev_vtm = vop(lambda e, par=par, kc=kc: e.tensor_copy(
            v_tm[:, kc * 1024:(kc + 1) * 1024], psT[par][:]), (tr,))
        psT_free[par] = ev_vtm
    if tapon("t_vtm"):
        gdma(taps["t_vtm"][:], v_tm[:], "tap", (ev_vtm,))

    last_q_mm = None
    for c in range(8):
        w = [ev_q_done] if c == 0 else [last_q_mm]
        ev_s = sop(lambda e, c=c: e.activation(
            sqb[1][:], q_fm[:, c * 1024:(c + 1) * 1024], AF.Square), w)
        first = True
        for nts in NT:
            last_q_mm = mm(psStat[0:1, nts], ones_col[:, 0:1], sqb[1][:, nts],
                           c == 0, c == 7,
                           (ev_s, psStat_free) if first else (ev_s,))
            first = False
    ev_mk = []
    last_k_mm = None
    for half in range(2):
        for c in range(8):
            w = [ev_k_done] if (half == 0 and c == 0) else [last_k_mm]
            if half == 1 and c == 0:
                w.append(ev_mk[0])
            ev_s = sop(lambda e, c=c, half=half: e.activation(
                sqb[0][:], k_fm[:, c * 2048 + half * 1024:c * 2048 + half * 1024 + 1024],
                AF.Square), w)
            first = True
            for nts in NT:
                last_k_mm = mm(psStat[32:33, nts], ones_col[:, 0:1], sqb[0][:, nts],
                               c == 0, c == 7, (ev_s,) if first else ())
                first = False
        ev_mk.append(vop(lambda e, half=half: e.reduce_max(
            nm1[half][0:1, 0:1], psStat[32:33, :], axis=AX.X), (last_k_mm,)))
    ev_mkx = vop(lambda e: e.tensor_tensor(nm1[0][0:1, 0:1], nm1[0][0:1, 0:1],
                                           nm1[1][0:1, 0:1], ALU.max), (ev_mk[0], ev_mk[1]))
    ev_p = vop(lambda e: e.tensor_scalar(st_a[:], psStat[0:1, :], nm1[0][0:1, 0:1],
                                         None, ALU.mult), (ev_mkx, ev_mkx))
    ev_l = sop(lambda e: e.activation(st_a[:], st_a[:], AF.Ln), (ev_p,))
    ev_e = sop(lambda e: e.activation(st_a[:], st_a[:], AF.Exp, scale=0.5))
    ev_nmT_evac = vop(lambda e: e.tensor_scalar(negmaxT[:], st_a[:], -1.0, None,
                                                ALU.mult), (ev_e,))
    psStat_free = ev_nmT_evac
    psAB_free = [psAB_free[0], psAB_free[1]]
    if tapon("t_negmaxT"):
        gdma(taps["t_negmaxT"][:], negmaxT[:], "tap", (ev_nmT_evac,))

    pend_b = []
    ev_exp = None
    last_rs = None
    for kc in range(16):
        par = kc % 2
        ps = psAB[par]
        w0 = [ev_nmT_evac, psAB_free[par]]
        last = None
        for dc in range(8):
            lhs = k_fm[:, dc * 2048 + kc * 128:dc * 2048 + kc * 128 + 128]
            for nts in NT:
                last = mm(ps[:, nts], lhs, q_fm[:, dc * 1024 + nts.start:dc * 1024 + nts.stop],
                          dc == 0, False, w0 if last is None else ())
        for nts in NT:
            last = mm(ps[:, nts], ones_row[0:1, :], negmaxT[0:1, nts], False, True)
        ev_exp = sop(lambda e, par=par, kc=kc: e.activation(
            probsT[:, kc * 1024:(kc + 1) * 1024], psAB[par][:], AF.Exp, scale=SM_SCALE),
            (last,))
        psAB_free[par] = ev_exp
        def emit_rs(kc=kc, ev_exp=ev_exp):
            nonlocal last_rs
            w_rs = [ev_exp]
            if kc == 0:
                w_rs.append(psStat_free)
            first = True
            for nts in NT:
                last_rs = mm(psStat[0:1, nts], ones_col[:, 0:1],
                             probsT[:, kc * 1024 + nts.start:kc * 1024 + nts.stop],
                             kc == 0, kc == 15, w_rs if first else ())
                first = False
        pend_b.append(emit_rs)
        if len(pend_b) > 1:
            pend_b.pop(0)()
    if tapon("t_probsT"):
        gdma(taps["t_probsT"][:], probsT[:], "tap", (ev_exp,))

    while pend_b:
        pend_b.pop(0)()
    ev_rs = vop(lambda e: e.reciprocal(st_a[:], psStat[0:1, :]), (last_rs,))
    psStat_free = ev_rs
    ev_rs_bf = vop(lambda e: e.tensor_copy(bf_a[:], st_a[:]), (ev_rs, ev_rs))
    ev_bc = bcast_single(bf_a, psA, (ev_rs_bf, psAB_free[0]))
    ev_rsbc = vop(lambda e: e.tensor_copy(rb_bf[:], psA[:]), (ev_bc,))
    psAB_free[0] = ev_rsbc

    wob_free = [None, None]
    ev_wo_pre = {}
    for i in (0, 1):
        ev_wo_pre[i] = gdma(wob[i][:], wo_t[i], "wo%d" % i, ())

    ev_ao = None
    for dc in range(8):
        par = dc % 2
        ps = psAB[par]
        w0 = [ev_vtm, ev_exp, psAB_free[par]]
        last = None
        for kc in range(16):
            lhs = v_tm[:, kc * 1024 + dc * 128:kc * 1024 + dc * 128 + 128]
            for nts in NT:
                last = mm(ps[:, nts], lhs, probsT[:, kc * 1024 + nts.start:kc * 1024 + nts.stop],
                          kc == 0, kc == 15, w0 if last is None else ())
        ev_ao = vop(lambda e, par=par, dc=dc: e.tensor_tensor(
            attnout[:, dc * 1024:(dc + 1) * 1024], psAB[par][:], rb_bf[:], ALU.mult),
            (last, ev_rsbc))
        psAB_free[par] = ev_ao
    if tapon("t_ao"):
        gdma(taps["t_ao"][:], attnout[:], "tap", (ev_ao,))

    ckpt("B")
    ev_zev = {}
    stat_mm_c = {}
    ev_sbf = {}
    pend_c = []
    for fc in range(32):
        par = fc % 2
        if fc < 2:
            ev_w = ev_wo_pre[fc]
        else:
            ev_w = gdma(wob[par][:], wo_t[fc], "wo%d" % par, (wob_free[par],))
        ps = psAB[par]
        w0 = [ev_w, ev_ao, psAB_free[par]]
        last = None
        for dc in range(8):
            lhs = wob[par][:, dc * 128:(dc + 1) * 128]
            for nts in NT:
                last = mm(ps[:, nts], lhs, attnout[:, dc * 1024 + nts.start:dc * 1024 + nts.stop],
                          dc == 0, dc == 7, w0 if last is None else ())
        wob_free[par] = last
        while len(pend_c) > 1:
            pend_c.pop(0)()
        if "evac" not in CSKIP:
            w_z = [last, ev_bo]
            if DEBUG_TAPS and fc == 0:
                w_z.append(sch.at("tap"))
            ev_zev[fc] = vop(lambda e, par=par, fc=fc: e.tensor_scalar(
                z[:, fc * 1024:(fc + 1) * 1024], psAB[par][:], bo_sb[:, fc:fc + 1], None, ALU.add),
                w_z)
        else:
            ev_zev[fc] = vop(lambda e, par=par, fc=fc: e.tensor_copy(
                z[:, fc * 1024:(fc + 1) * 1024], psAB[par][:]), (last,))
        sq_ = sqc_buf if par == 0 else sqc2
        zs_c = z[:, fc * 1024:(fc + 1) * 1024]
        if "act" not in CSKIP:
            w_s = [ev_zev[fc]]
            if fc >= 2:
                w_s.append(stat_mm_c[fc - 2])
            ev_sbf[fc] = sop(lambda e, sq_=sq_, zs_c=zs_c: e.activation(
                sq_[:], zs_c, AF.Square), w_s)
        else:
            ev_sbf[fc] = ev_zev[fc]
        psAB_free[par] = ev_zev[fc]
        if "stats" not in CSKIP and "act" not in CSKIP:
            def emit_stats_c(fc=fc, sq_=sq_, zs_c=zs_c):
                w_st = [ev_sbf[fc]]
                if fc == 0:
                    w_st.append(psStat_free)
                first = True
                for nts in NT:
                    mm(psStat[0:1, nts], ones_col[:, 0:1], zs_c[:, nts], fc == 0, fc == 31,
                       w_st if first else ())
                    first = False
                    stat_mm_c[fc] = mm(psStat[32:33, nts], ones_col[:, 0:1], sq_[:, nts],
                                       fc == 0, fc == 31)
            pend_c.append(emit_stats_c)
        else:
            stat_mm_c[fc] = ev_zev[fc]
    if DEBUG_TAPS and TRUNC == "C1":
        gdma(taps["t_y"][:], z[:], "tap", (ev_zev[31],))
    ckpt("C1")
    for p_ in pend_c:
        p_()
    pend_c = []
    ev_sr, ev_fin = ln_finalize(stat_mm_c[31], st_a, st_b, bf_a, bf_b, st_c,
                                extra=(sch.at("tap"),) if DEBUG_TAPS else ())
    psStat_free = ev_sr
    ev_b1 = bcast_single(bf_b, psA, (ev_fin, psAB_free[0]))
    ev_b2 = bcast_single(bf_a, psB, (psAB_free[1],))
    ev_rbf = vop(lambda e: e.tensor_copy(rb_bf[:], psA[:]), (ev_b1,))
    ev_nbf = vop(lambda e: e.tensor_copy(nb_bf[:], psB[:]), (ev_b2,))
    psAB_free = [ev_rbf, ev_nbf]
    ckpt("C2")
    def eop(fc3, fn, waits=(), mod=3):
        if fc3 % mod == mod - 1:
            return E("gpsimd", fn, waits, inc="gs")
        return E("vector", fn, waits, inc="vs")

    xt_free = [None, None, None, None]
    ev_y = {}
    ev_dcast = {}
    stat_mm_d = {}
    for fc in range(32):
        xb = fc % 4
        w_x = [xt_free[xb]] if fc >= 4 else [ev_fin, ev_rbf, ev_nbf]
        ev_x = sdma(xtile[xb][:], xbf[fc // 8][:, (fc % 8) * 1024:(fc % 8) * 1024 + 1024],
                    "xt%d" % xb, w_x)
        zs = z[:, fc * 1024:(fc + 1) * 1024]
        w1 = [ev_rbf, ev_nbf, ev_zev[fc], stat_mm_c[31]]
        if DEBUG_TAPS and fc == 0:
            w1.append(sch.at("tap"))
        eop(fc, lambda e, zs=zs: e.tensor_tensor(zs, zs, rb_bf[:], ALU.mult), w1)
        eop(fc, lambda e, zs=zs: e.tensor_tensor(zs, zs, nb_bf[:], ALU.add))
        ev_y[fc] = eop(fc, lambda e, zs=zs, xb=xb: e.tensor_tensor(
            zs, zs, xtile[xb][:], ALU.add), (ev_x,))
        xt_free[xb] = ev_y[fc]
        w_c = [ev_y[fc]]
        if fc >= 1:
            w_c.append(stat_mm_d[fc - 1])
        ev_dcast[fc] = sop(lambda e, zs=zs: e.activation(sqc_buf[:], zs, AF.Square), w_c)
        w_st = [ev_dcast[fc]]
        if fc == 0:
            w_st.append(psStat_free)
        first = True
        for nts in NT:
            mm(psStat[0:1, nts], ones_col[:, 0:1], zs[:, nts], fc == 0, fc == 31,
               w_st if first else ())
            first = False
            stat_mm_d[fc] = mm(psStat[32:33, nts], ones_col[:, 0:1], sqc_buf[:, nts],
                               fc == 0, fc == 31)
    if tapon("t_y"):
        gdma(taps["t_y"][:], z[:], "tap", (ev_y[31],))

    ckpt("C")
    ev_sr, ev_fin = ln_finalize(stat_mm_d[31], st_a, st_b, bf_a, bf_b, st_c)
    psStat_free = ev_sr
    ev_b1 = bcast_single(bf_b, psA, (ev_fin, psAB_free[0]))
    ev_b2 = bcast_single(bf_a, psB, (psAB_free[1],))
    ev_rbc = vop(lambda e: e.tensor_copy(rb_bf[:], psA[:]), (ev_b1,))
    ev_nbc = vop(lambda e: e.tensor_copy(nb_bf[:], psB[:]), (ev_b2,))
    psAB_free = [ev_rbc, ev_nbc]

    ev_gw_pre = {}
    ev_uw_pre = {}
    for hc in (0, 1):
        par = hc % 2
        w_g = [wob_free[0], wob_free[1]]
        if DEBUG_TAPS:
            w_g.append(sch.at("tap"))
        ev_gw_pre[hc] = gdma(gwb[par][:], gup_g[hc], "gw%d" % par, w_g)
    E("gpsimd", lambda e: e.memset(tmpg_pad[:], 0.0), (ev_fin, ev_rbc, ev_nbc), inc="gs")
    E("gpsimd", lambda e: e.memset(tmpu_pad[:], 0.0), inc="gs")
    ev_pads = E("gpsimd", lambda e: e.memset(tmpd_pad[:], 0.0), inc="gs")

    ev_n3_v = ev_n3_g = None
    ev_e0 = None
    ev_l1 = {}
    w_e0 = [ev_gaup, ev_pads, psAB_free[0]]
    for fc in range(32):
        cs = slice(fc * 1024, (fc + 1) * 1024)
        ev_l1[fc] = eop(fc, lambda e, cs=cs: e.tensor_tensor(
            ln3[:, cs], z[:, cs], rb_bf[:], ALU.mult), (ev_rbc, ev_nbc), mod=5)
        ev_nn = eop(fc, lambda e, cs=cs: e.tensor_tensor(
            ln3[:, cs], ln3[:, cs], nb_bf[:], ALU.add), mod=5)
        if fc % 5 == 4:
            ev_n3_g = ev_nn
        else:
            ev_n3_v = ev_nn
        lhs = gaup_sb[:, fc * 48:(fc + 1) * 48]
        for nts in NT:
            ev_e0 = mm(psA[0:48, nts], lhs, ln3[:, fc * 1024 + nts.start:fc * 1024 + nts.stop],
                       fc == 0, fc == DC - 1, [ev_nn] + (w_e0 if fc == 0 else []))
    ev_n3 = [ev_n3_v, ev_n3_g]
    yp_ev = []
    for g in range(4):
        w = [ev_l1[g * 8 + i] for i in range(8)] + [stat_mm_d[31]]
        if DEBUG_TAPS and g == 0:
            w.append(sch.at("tap"))
        yp_ev.append(sdma(ypark[g], z[:, g * 8192:(g + 1) * 8192], "yp%d" % g, w))
    yp_all = [sch.at("yp%d" % g) for g in range(4)]
    for hc in (0, 1):
        ev_uw_pre[hc] = gdma(uwb[hc % 2][:], gup_u[hc], "uw%d" % (hc % 2),
                             (wob_free[0], wob_free[1]))
    if tapon("t_ln3"):
        gdma(taps["t_ln3"][:], ln3[:], "tap", (ev_n3,))

    ckpt("D")
    ev_tg = vop(lambda e: e.tensor_copy(tmpg_pad[0:16, :], psA[0:16, :]), (ev_e0,))
    ev_tu = vop(lambda e: e.tensor_copy(tmpu_pad[32:48, :], psA[32:48, :]))
    psAB_free[0] = ev_tu

    gw_free = [None, None]
    uw_free = [None, None]
    dw_free = [None, None]
    acct_free = [None, None]
    psA_free = ev_tu
    psB_free = psAB_free[1]
    silu_free = None
    ev_tdp = None
    ev_park = {}
    ev_dpark = {}
    stat_mm_e = {}
    pend_e = []
    last_acc_ld = [None, None]
    ev_hid_tap = None
    last_down_mm = None

    HPH = ((0, 29), (29, 29), (58, 28))
    ev_prev_down = None
    for pi, (h0, cnt) in enumerate(HPH):
        last_ph = (pi == 2)
        ev_mul = None
        for j in range(cnt):
            hc = h0 + j
            par = hc % 2
            if hc in ev_gw_pre:
                ev_gw = ev_gw_pre[hc]
                ev_uw = ev_uw_pre[hc]
            else:
                ev_gw = gdma(gwb[par][:], gup_g[hc], "gw%d" % par, (gw_free[par],))
                ev_uw = gdma(uwb[par][:], gup_u[hc], "uw%d" % par,
                             (uw_free[par], dw_free[par]))
            w0 = [ev_gw, psA_free, ev_tg, ev_n3]
            last_g = None
            for c in range(33):
                lhs = gwb[par][:, c * 128:(c + 1) * 128]
                for nts in NT:
                    rhs = (ln3[:, c * 1024 + nts.start:c * 1024 + nts.stop]
                           if c < DC else tmpg_pad[:, nts])
                    last_g = mm(psA[:, nts], lhs, rhs, c == 0, c == 32,
                                w0 if last_g is None else ())
            gw_free[par] = last_g
            ev_sl = sop(lambda e: e.activation(silu_buf[:], psA[:], AF.Silu),
                        (last_g, silu_free))
            psA_free = ev_sl
            w0 = [ev_uw, psB_free, ev_tu]
            last_u = None
            for c in range(33):
                lhs = uwb[par][:, c * 128:(c + 1) * 128]
                for nts in NT:
                    rhs = (ln3[:, c * 1024 + nts.start:c * 1024 + nts.stop]
                           if c < DC else tmpu_pad[:, nts])
                    last_u = mm(psB[:, nts], lhs, rhs, c == 0, c == 32,
                                w0 if last_u is None else ())
            uw_free[par] = last_u
            w_m = [last_u, ev_sl]
            if j == 0:
                w_m += [ev_prev_down] + yp_all
                if DEBUG_TAPS:
                    w_m.append(sch.at("tap"))
            ev_mul = vop(lambda e, j=j: e.tensor_tensor(
                hidden[:, j * 1024:(j + 1) * 1024], silu_buf[:], psB[:], ALU.mult), w_m)
            psB_free = ev_mul
            silu_free = ev_mul
        if tapon("t_hid") and pi == 0:
            ev_hid_tap = gdma(taps["t_hid"][:, 0:cnt * 1024], hidden[:, 0:cnt * 1024],
                              "tap", (ev_mul,))
        if last_ph:
            for i in range(11):
                cdma(ypre[i][:], ypark[i // 8][:, (i % 8) * 1024:(i % 8) * 1024 + 1024],
                     "pp", [gw_free[0], gw_free[1], uw_free[0], uw_free[1]] + yp_all)
        w0 = [ev_mul, psA_free, ev_da]
        last_td = None
        for j in range(cnt):
            hc = h0 + j
            lhs = da_sb[:, hc * 16:(hc + 1) * 16]
            for nts in NT:
                last_td = mm(psA[0:16, nts], lhs, hidden[:, j * 1024 + nts.start:j * 1024 + nts.stop],
                             j == 0, j == cnt - 1, w0 if last_td is None else ())
        if pi == 0:
            ev_td = vop(lambda e: e.tensor_copy(tmpd_h1[:], psA[0:16, :]), (last_td,))
        else:
            ev_td = vop(lambda e: e.tensor_add(tmpd_h1[:], tmpd_h1[:], psA[0:16, :]),
                        (last_td,))
            if last_ph:
                ev_tdp = vop(lambda e: e.tensor_copy(tmpd_pad[0:16, :], tmpd_h1[:]))
                ev_td = ev_tdp
        psA_free = ev_td

        ev_gu_done = (gw_free[0], gw_free[1], uw_free[0], uw_free[1])
        ap_prev = sch.at("ap")
        nch = cnt + (1 if last_ph else 0)
        col0 = h0 * 128
        psd_free = [psA_free, psB_free]
        for fc in range(32):
            par = fc % 2
            ev_dw = gdma(dwb[par][:, 0:nch * 128], dwn_t[fc][:, col0:col0 + nch * 128],
                         "dw%d" % par, (dw_free[par], uw_free[par]))
            ps = psAB[par]
            w0 = [ev_dw, ev_mul, psd_free[par]]
            if last_ph:
                w0.append(ev_tdp)
            last = None
            for j in range(nch):
                lhs = dwb[par][:, j * 128:(j + 1) * 128]
                for nts in NT:
                    rhs = (hidden[:, j * 1024 + nts.start:j * 1024 + nts.stop]
                           if j < cnt else tmpd_pad[:, nts])
                    last = mm(ps[:, nts], lhs, rhs, j == 0, j == nch - 1,
                              w0 if last is None else ())
            dw_free[par] = last
            last_down_mm = last
            while pend_e:
                pend_e.pop(0)()
            if pi == 0:
                ev_cp = vop(lambda e, par=par: e.tensor_copy(acct[par][:], psAB[par][:]),
                            (last, acct_free[par]))
                ev_park[fc] = sdma(accd[fc], acct[par][:], "ap", (ev_cp,))
                acct_free[par] = ev_park[fc]
                psd_free[par] = ev_cp
            elif pi == 1:
                ev_ld = sdma(acct[par][:], accd[fc], "ac%d" % par,
                             (acct_free[par], ap_prev))
                ev_add = vop(lambda e, par=par: e.tensor_tensor(
                    acct[par][:], psAB[par][:], acct[par][:], ALU.add), (last, ev_ld))
                psd_free[par] = ev_add
                ev_park[fc] = sdma(accd[fc], acct[par][:], "ap", (ev_add,))
                acct_free[par] = ev_park[fc]
            else:
                ev_ld = sdma(acct[par][:], accd[fc], "ac%d" % par,
                             (acct_free[par], ap_prev))
                last_acc_ld[par] = ev_ld
                w_add = [last, ev_ld]
                if fc == 0:
                    w_add.append(ev_gu_done)
                ds = dout_sb[:, fc * 1024:(fc + 1) * 1024]
                ev_add = vop(lambda e, par=par, ds=ds: e.tensor_tensor(
                    ds, psAB[par][:], acct[par][:], ALU.add), w_add)
                psd_free[par] = ev_add
                acct_free[par] = ev_add
                w_s = [ev_add]
                if fc >= 1:
                    w_s.append(stat_mm_e[fc - 1])
                ev_sq = sop(lambda e, ds=ds: e.activation(sqc_buf[:], ds, AF.Square), w_s)
                def emit_stats_e(fc=fc, ev_sq=ev_sq, ds=ds):
                    w_st = [ev_sq]
                    if fc == 0:
                        w_st.append(psStat_free)
                    first = True
                    for nts in NT:
                        mm(psStat[0:1, nts], ones_col[:, 0:1],
                           ds[:, nts], fc == 0, fc == 31, w_st if first else ())
                        first = False
                        stat_mm_e[fc] = mm(psStat[32:33, nts], ones_col[:, 0:1], sqc_buf[:, nts],
                                           fc == 0, fc == 31)
                pend_e.append(emit_stats_e)
        while pend_e:
            pend_e.pop(0)()
        ev_prev_down = last_down_mm
        psA_free, psB_free = psd_free[0], psd_free[1]
        psAB_free = [psA_free, psB_free]

    ckpt("E")
    ev_sr, ev_fin = ln_finalize(stat_mm_e[31], est_a, est_b, bf_a, bf_b, est_c,
                                extra=(last_acc_ld[0], last_acc_ld[1]))
    psStat_free = ev_sr
    ev_b1 = bcast_single(bf_b, psA, (ev_fin, psAB_free[0]))
    ev_b2 = bcast_single(bf_a, psB, (psAB_free[1],))
    ev_rv = vop(lambda e: e.tensor_copy(erb_v[:], psA[:]), (ev_b1,))
    ev_nv = vop(lambda e: e.tensor_copy(enb_v[:], psB[:]), (ev_b2,))
    ckpt("E2")

    pp_all = sch.at("pp")
    yt_free = [None] * 4
    ot_free = [None] * 4
    tb4 = [tbv, tbp, erb_p, enb_p]
    tb_free = [None] * 4
    DVE_FIN = frozenset((0, 2, 4, 6))
    ev_out = None
    for fc in range(32):
        b = fc % 4
        if fc < 11:
            ybuf, ev_yl = ypre[fc], pp_all
        else:
            ybuf = ytile[b]
            w_y = [yt_free[b]] if fc >= 15 else [last_down_mm]
            ev_yl = cdma(ybuf[:], ypark[fc // 8][:, (fc % 8) * 1024:(fc % 8) * 1024 + 1024],
                         "yt%d" % b, w_y)
        ds = dout_sb[:, fc * 1024:(fc + 1) * 1024]
        ot = otile[b]
        tb = tb4[b]
        vop(lambda e, ds=ds, tb=tb: e.tensor_tensor(
            tb[:], ds, erb_v[:], ALU.mult), (ev_rv, ev_nv, tb_free[b]))
        ev_add = vop(lambda e, tb=tb: e.tensor_tensor(
            tb[:], tb[:], enb_v[:], ALU.add))
        if fc in DVE_FIN:
            ev_o = vop(lambda e, ot=ot, tb=tb, ybuf=ybuf: e.tensor_tensor(
                ot[:], tb[:], ybuf[:], ALU.add), (ev_yl, ot_free[b]))
        else:
            ev_o = E("gpsimd", lambda e, ot=ot, tb=tb, ybuf=ybuf: e.tensor_tensor(
                ot[:], tb[:], ybuf[:], ALU.add), (ev_add, ev_yl, ot_free[b]), inc="gs")
        tb_free[b] = ev_o
        if fc % 2 == 0:
            ev_out = sdma(o[fc * 128:(fc + 1) * 128, :], ot[:], "ow%d" % (b // 2), (ev_o,))
        else:
            ev_out = gdma(o[fc * 128:(fc + 1) * 128, :], ot[:], "og%d" % (b // 2), (ev_o,))
        yt_free[b] = ev_o
        ot_free[b] = ev_out
    stopped[0] = False
    E("sync", None, tuple(sch.at(s) for s in SD_SEMS))
    E("gpsimd", None, tuple(sch.at(s) for s in GD_SEMS))
    E("scalar", None, tuple(sch.at(s) for s in SC_SEMS))

    es = contextlib.ExitStack()
    with es:
        sems = {}
        for s in ("ts", "vs", "ss", "gs") + GD_SEMS + SD_SEMS + SC_SEMS:
            sems[s] = es.enter_context(nc.semaphore(s))
        block = es.enter_context(nc.Block())

        def runner(name):
            def run(e):
                for waits, fn, inc, n in sch.ops[name]:
                    for s, v in waits:
                        e.wait_ge(sems[s], v)
                    ins = fn(e) if fn is not None else None
                    if inc is not None:
                        ins.then_inc(sems[inc], n)
            return run

        block.tensor(runner("tensor"))
        block.vector(runner("vector"))
        block.scalar(runner("scalar"))
        block.gpsimd(runner("gpsimd"))
        block.sync(runner("sync"))
    return nc


def _prep_shared(inputs):
    f = np.float32
    Wqkv = np.asarray(inputs["L0_attn_Wqkv"], f)
    Wo = np.asarray(inputs["L0_attn_Wo"], f)
    gate_w = np.asarray(inputs["L0_ffn_gate_w"], f)
    up_w = np.asarray(inputs["L0_ffn_up_w"], f)
    down_w = np.asarray(inputs["L0_ffn_down_w"], f)
    gate_a = np.asarray(inputs["L0_ffn_gate_a"], f)
    gate_b = np.asarray(inputs["L0_ffn_gate_b"], f)
    up_a = np.asarray(inputs["L0_ffn_up_a"], f)
    up_b = np.asarray(inputs["L0_ffn_up_b"], f)
    down_a = np.asarray(inputs["L0_ffn_down_a"], f)
    down_b = np.asarray(inputs["L0_ffn_down_b"], f)
    bqkv = np.asarray(inputs["L0_attn_bqkv"], f)
    bo = np.asarray(inputs["L0_attn_bo"], f)

    d = {}
    d["wqkv_t"] = np.ascontiguousarray(
        Wqkv.reshape(32, 128, 24, 128).transpose(2, 1, 0, 3).reshape(24, 128, 4096))
    d["wo_t"] = np.ascontiguousarray(
        Wo.reshape(8, 128, 32, 128).transpose(2, 1, 0, 3).reshape(32, 128, 1024))

    aug_g = np.zeros((4224, DH), f)
    aug_g[:4096] = gate_w.T
    aug_g[4096:4112] = gate_b.T
    d["gup_g"] = np.ascontiguousarray(
        aug_g.reshape(33, 128, 86, 128).transpose(2, 1, 0, 3).reshape(86, 128, 4224))
    aug_u = np.zeros((4224, DH), f)
    aug_u[:4096] = up_w.T
    aug_u[4128:4144] = up_b.T
    d["gup_u"] = np.ascontiguousarray(
        aug_u.reshape(33, 128, 86, 128).transpose(2, 1, 0, 3).reshape(86, 128, 4224))

    aug_d = np.zeros((11136, 4096), f)
    aug_d[:DH] = down_w.T
    aug_d[DH:DH + 16] = down_b.T
    d["dwn_t"] = np.ascontiguousarray(
        aug_d.reshape(87, 128, 32, 128).transpose(2, 1, 0, 3).reshape(32, 128, 11136))

    A = np.zeros((4096, 48), f)
    A[:, 0:16] = gate_a.T
    A[:, 32:48] = up_a.T
    d["gaup_h"] = np.ascontiguousarray(
        A.reshape(32, 128, 48).transpose(1, 0, 2).reshape(128, 1536))
    d["da_h"] = np.ascontiguousarray(
        down_a.T.reshape(86, 128, 16).transpose(1, 0, 2).reshape(128, 1376))

    d["bqkv_h"] = np.ascontiguousarray(bqkv.reshape(24, 128).T)
    d["bo_h"] = np.ascontiguousarray(bo.reshape(32, 128).T)
    d["ident_h"] = np.eye(128, dtype=f)
    return d


def _tile_x(xT_half):
    return np.ascontiguousarray(
        xT_half.reshape(4, 8, 128, 1024).transpose(0, 2, 1, 3).reshape(4, 128, 8192))


def kernel(**inputs):
    shared = _prep_shared(inputs)
    x = np.asarray(inputs["L0_attn_x"], np.float32)
    in_maps = []
    for c in range(8):
        b, half = c // 2, c % 2
        xT = x[b].T
        m = dict(shared)
        m["xo_t"] = _tile_x(np.ascontiguousarray(xT[:, half * T:(half + 1) * T]))
        m["xp_t"] = _tile_x(np.ascontiguousarray(xT[:, (1 - half) * T:(2 - half) * T]))
        in_maps.append(m)
    nc = build_program()
    res = run_bass_kernel_spmd(nc, in_maps, core_ids=list(range(8)))
    out = np.empty((4, 2048, 4096), np.float32)
    for c in range(8):
        b, half = c // 2, c % 2
        out[b, half * T:(half + 1) * T, :] = np.asarray(res.results[c]["o"]).T
    return out
